# version 02
# baseline (speedup 1.0000x reference)
import numpy as np
import concourse.bass as bass
import concourse.mybir as mybir
from concourse.bass_utils import run_bass_kernel_spmd

F32 = mybir.dt.float32
I32 = mybir.dt.int32
AF = mybir.ActivationFunctionType
ALU = mybir.AluOpType
AXX = mybir.AxisListType.X
DS = bass.ds

B, S, H, DH, HID = 4, 2048, 32, 128, 4096
NCORES = 8
SCALE = 1.0 / float(np.sqrt(128.0))
SHIFT = -30.0
EPS = 1e-5
VIS_LO, VIS_HI = 1, 513
NCH = 4
CH = 512
NFT = 16
NCT = 32

DMA_KEYS = ("c", "x", "w0", "w1", "kv0", "kv1", "h0", "h1", "f0", "f1",
            "p0", "p1", "o0", "o1", "e0", "e1",
            "xn", "qk0", "qk1", "v0", "v1", "ob0", "ob1",
            "fs0", "fs1", "fs2", "fs3", "ap0", "ap1", "ou0", "ou1", "gp")
ENG_KEYS = ("pe", "ac", "ve", "gc", "cc")


class Emitter:
    def __init__(self, sems):
        self.ops = {k: [] for k in ("sync", "scalar", "vector", "gpsimd", "tensor")}
        self.cnt = {}
        self.last_wait = {k: {} for k in self.ops}
        self.sems = sems

    def inc(self, key, n):
        self.cnt[key] = self.cnt.get(key, 0) + n
        return self.cnt[key]

    def val(self, key):
        return self.cnt.get(key, 0)

    def wait(self, eng, key, v=None):
        v = self.val(key) if v is None else v
        if not v or v <= 0:
            return
        lw = self.last_wait[eng]
        if lw.get(key, -1) >= v:
            return
        lw[key] = v
        s = self.sems[key]
        self.ops[eng].append(lambda e, s=s, vv=v: e.wait_ge(s, vv))

    def dma(self, q, out_ap, in_ap, key, slow=False):
        prev = self.val(key)
        if prev:
            self.wait(q, key, prev)
        v = self.inc(key, 16)
        s = self.sems[key]

        def run(e, o=out_ap, i=in_ap, s=s, slow=slow):
            if callable(o):
                o = o()
            if callable(i):
                i = i()
            if slow:
                e.dma_start(out=o, in_=i, allow_slow_non_contiguous=True).then_inc(s, 16)
            else:
                e.dma_start(out=o, in_=i).then_inc(s, 16)
        self.ops[q].append(run)
        return v

    def op(self, eng, fn, inc_key=None, inc_n=1):
        if inc_key is None:
            self.ops[eng].append(fn)
            return None
        v = self.inc(inc_key, inc_n)
        s = self.sems[inc_key]
        self.ops[eng].append(lambda e, f=fn, s=s, n=inc_n: f(e).then_inc(s, n))
        return v


def build_program():
    nc = bass.Bass("TRN2", target_bir_lowering=False, detect_race_conditions=False)

    xT = nc.dram_tensor("xT", [HID, S], F32, kind="ExternalInput").ap()
    qwT = nc.dram_tensor("qwT", [HID, 2048], F32, kind="ExternalInput").ap()
    kwT = nc.dram_tensor("kwT", [HID, 2048], F32, kind="ExternalInput").ap()
    vwT = nc.dram_tensor("vwT", [HID, 2048], F32, kind="ExternalInput").ap()
    owT = nc.dram_tensor("owT", [HID, HID], F32, kind="ExternalInput").ap()
    fwT = nc.dram_tensor("fwT", [HID, HID], I32, kind="ExternalInput").ap()
    sexpT = nc.dram_tensor("sexpT", [HID, HID], F32, kind="ExternalInput").ap()
    fbias = nc.dram_tensor("fbias", [HID, 1], F32, kind="ExternalInput").ap()
    cosT_d = nc.dram_tensor("cosT", [128, S], F32, kind="ExternalInput").ap()
    sinT_d = nc.dram_tensor("sinT", [128, S], F32, kind="ExternalInput").ap()
    ptT_d = nc.dram_tensor("ptT", [128, 128], F32, kind="ExternalInput").ap()
    ident_d = nc.dram_tensor("ident", [128, 128], F32, kind="ExternalInput").ap()
    ones_d = nc.dram_tensor("onesm", [128, 128], F32, kind="ExternalInput").ap()
    impsc_d = nc.dram_tensor("impsc", [1, 1], F32, kind="ExternalInput").ap()
    xTo = nc.dram_tensor("xTo", [HID, 1024], F32, kind="ExternalInput").ap()
    outT = nc.dram_tensor("outT", [HID, 1024], F32, kind="ExternalOutput").ap()

    qT_s = nc.dram_tensor("qT_s", [2048, S], F32).ap()
    kT_s = nc.dram_tensor("kT_s", [2048, S], F32).ap()
    vT_s = nc.dram_tensor("vT_s", [2048, S], F32).ap()
    xnT_s = nc.dram_tensor("xnT_s", [HID, S], F32)
    ohT_s = nc.dram_tensor("ohT_s", [2048, S], F32)
    ag_out = nc.dram_tensor("ag_out", [8 * 2048, S], F32)
    ffnT_s = nc.dram_tensor("ffnT_s", [HID, 1024], F32).ap()
    apT_s = nc.dram_tensor("apT_s", [HID, 1024], F32).ap()
    ar_in = nc.dram_tensor("ar_in", [1, 512], F32)
    ar_out = nc.dram_tensor("ar_out", [1, 512], F32)
    kbias_s = nc.dram_tensor("kbias_s", [640, 1], F32).ap()

    A1 = nc.alloc_sbuf_tensor("A1", [128, 16384], F32).ap()
    A2 = nc.alloc_sbuf_tensor("A2", [128, 8192], F32).ap()
    A3 = nc.alloc_sbuf_tensor("A3", [128, 8192], F32).ap()
    A4 = nc.alloc_sbuf_tensor("A4", [128, 6144], F32).ap()
    AI = nc.alloc_sbuf_tensor("AI", [128, 4096], I32).ap()
    ROWS = nc.alloc_sbuf_tensor("ROWS", [1, 3072], F32).ap()
    cosT = nc.alloc_sbuf_tensor("cosT_sb", [128, S], F32).ap()
    sinT = nc.alloc_sbuf_tensor("sinT_sb", [128, S], F32).ap()
    ptT = nc.alloc_sbuf_tensor("ptT_sb", [128, 128], F32).ap()
    ident = nc.alloc_sbuf_tensor("ident_sb", [128, 128], F32).ap()
    ones = nc.alloc_sbuf_tensor("ones_sb", [128, 128], F32).ap()
    sumq = nc.alloc_sbuf_tensor("sumq_sb", [128, 16], F32).ap()
    sumq_t = nc.alloc_sbuf_tensor("sumq_t", [128, 1], F32).ap()
    kbias = nc.alloc_sbuf_tensor("kbias_sb", [128, 8], F32).ap()
    m30 = nc.alloc_sbuf_tensor("m30_sb", [128, 1], F32).ap()
    impsc = nc.alloc_sbuf_tensor("impsc_sb", [1, 1], F32).ap()
    impcols = nc.alloc_sbuf_tensor("impcols_sb", [128, 4], F32).ap()
    fbias_c = nc.alloc_sbuf_tensor("fbias_sb", [128, 32], F32).ap()
    rankw = nc.alloc_sbuf_tensor("rankw_sb", [128, 8], F32).ap()

    P = [nc.alloc_psum_tensor(f"P{i}", [128, 512], F32).ap() for i in range(6)]
    P6 = nc.alloc_psum_tensor("P6", [1, 512], F32).ap()
    P7 = nc.alloc_psum_tensor("P7", [1, 512], F32).ap()

    sems, sem_ctx = {}, {}
    for k in DMA_KEYS + ENG_KEYS:
        g = nc.semaphore(k)
        sems[k] = g.__enter__()
        sem_ctx[k] = g
    E = Emitter(sems)

    R = {name: ROWS[0:1, 512 * i:512 * i + 512]
         for i, name in enumerate(("mean", "vpe", "b", "ss", "rstd", "negmr"))}

    ones_col = ones[:, 0:1]
    ones_row = ones[0:1, :]

    E.dma("sync", cosT[:], cosT_d, "c")
    E.dma("sync", sinT[:], sinT_d, "c")
    E.dma("sync", ptT[:], ptT_d, "c")
    E.dma("sync", ident[:], ident_d, "c")
    E.dma("sync", ones[:], ones_d, "c")
    E.dma("sync", impsc[:], impsc_d, "c")
    consts_ld = E.dma("sync", fbias_c[:],
                      fbias[:, 0:1].rearrange("(o p) one -> p (o one)", p=128), "c",
                      slow=True)
    E.wait("vector", "c", consts_ld)
    E.op("vector", lambda e: e.memset(sumq[:], 0.0))
    E.op("vector", lambda e: e.memset(m30[:], SHIFT), "ve")

    pidbox = {}

    def gp_init(e):
        pid = nc.partition_id([mybir.EngineType.Pool])
        g = pid % 2
        pidbox["goff"] = g * 1024
        pidbox["rb0"] = (pid - g) * 2048
        pidbox["rb1"] = (pid - g + 1) * 2048
        return e.memset(rankw[:, 7:8], 0.0)
    E.op("gpsimd", gp_init, "gc")

    xbuf = A1
    wslab = [A2[:, 0:4096], A2[:, 4096:8192]]
    xsq = [A4[:, 0:512], A4[:, 512:1024]]
    traw = A4[:, 1024:1536]
    t1b = A4[:, 1536:2048]
    t2b = A4[:, 2048:2560]
    qrout = [A4[:, 2560:3072], A4[:, 3072:3584]]
    vout = [A4[:, 3584:4096], A4[:, 4096:4608]]
    bc_rstd = A4[:, 4608:5120]
    bc_negmr = A4[:, 5120:5632]

    xbuf3 = xbuf.rearrange("p (c t) -> p c t", t=CH)

    slab_done = {}
    qr_store = {}
    v_store = {}
    xchunk_users = None
    last_evac_ac = 0
    last_rope_ve = 0
    prev_nm_v = 0
    prev_bc_cp = 0
    gi = 0

    for j in range(NCH):
        if xchunk_users:
            E.wait("sync", "pe", xchunk_users["pe"])
            E.wait("sync", "ve", xchunk_users["ve"])
            E.wait("sync", "xn", xchunk_users["xn"])
        xld = E.dma("sync", xbuf3,
                    xT.rearrange("(c p) t -> p c t", p=128)[:, :, CH * j:CH * j + CH],
                    "x")

        E.wait("scalar", "x", xld)
        E.wait("tensor", "x", xld)
        E.wait("tensor", "ve", prev_nm_v)
        sq_vals, ssq_vals = [], []
        for c in range(NCT):
            if c >= 2:
                E.wait("scalar", "pe", ssq_vals[c - 2])
            sq_vals.append(E.op("scalar", lambda e, c=c: e.activation(
                xsq[c % 2][:], xbuf[:, CH * c:CH * c + CH], AF.Square), "ac"))
            E.op("tensor", lambda e, c=c: e.matmul(
                P6[:], ones_col, xbuf[:, CH * c:CH * c + CH],
                start=(c == 0), stop=(c == NCT - 1)))
            E.wait("tensor", "ac", sq_vals[c])
            ssq_vals.append(E.op("tensor", lambda e, c=c: e.matmul(
                P7[:], ones_col, xsq[c % 2][:],
                start=(c == 0), stop=(c == NCT - 1)), "pe"))
        E.wait("vector", "pe", ssq_vals[-1])
        E.op("vector", lambda e: e.tensor_scalar(R["mean"], P6[:], 1.0 / HID, None, ALU.mult))
        E.op("vector", lambda e: e.tensor_scalar(R["vpe"], P7[:], 1.0 / HID, None, ALU.mult))
        E.op("vector", lambda e: e.tensor_mul(R["b"], R["mean"], R["mean"]))
        E.op("vector", lambda e: e.scalar_tensor_tensor(
            R["vpe"], R["vpe"], EPS, R["b"], ALU.add, ALU.subtract))
        rr_v = E.op("vector", lambda e: e.reciprocal(R["b"], R["vpe"]), "ve")
        E.wait("scalar", "ve", rr_v)
        ss_v = E.op("scalar", lambda e: e.activation(R["ss"], R["b"], AF.Sqrt), "ac")
        E.wait("vector", "ac", ss_v)
        E.op("vector", lambda e: e.tensor_mul(R["b"], R["ss"], R["ss"]))
        E.op("vector", lambda e: e.tensor_mul(R["b"], R["b"], R["vpe"]))
        E.op("vector", lambda e: e.tensor_scalar(R["b"], R["b"], -0.5, 1.5, ALU.mult, ALU.add))
        E.op("vector", lambda e: e.tensor_mul(R["rstd"], R["ss"], R["b"]))
        E.op("vector", lambda e: e.tensor_mul(R["negmr"], R["mean"], R["rstd"]))
        nm_v = E.op("vector", lambda e: e.tensor_scalar(
            R["negmr"], R["negmr"], -1.0, None, ALU.mult), "ve")
        prev_nm_v = nm_v
        E.wait("tensor", "ve", nm_v)
        E.wait("tensor", "ac", prev_bc_cp)
        bc_mm = E.op("tensor", lambda e: (
            e.matmul(P[2][:], ones_row, R["rstd"], start=True, stop=True),
            e.matmul(P[3][:], ones_row, R["negmr"], start=True, stop=True))[-1], "pe")
        E.wait("scalar", "pe", bc_mm)
        E.op("scalar", lambda e: e.copy(bc_rstd[:], P[2][:]))
        bc_cp = E.op("scalar", lambda e: e.copy(bc_negmr[:], P[3][:]), "ac")
        prev_bc_cp = bc_cp
        E.wait("vector", "ac", bc_cp)
        xn_v = None
        for c in range(NCT):
            E.op("vector", lambda e, c=c: e.tensor_mul(
                xbuf[:, CH * c:CH * c + CH], xbuf[:, CH * c:CH * c + CH], bc_rstd[:]))
            xn_v = E.op("vector", lambda e, c=c: e.tensor_add(
                xbuf[:, CH * c:CH * c + CH], xbuf[:, CH * c:CH * c + CH], bc_negmr[:]),
                "ve" if c == NCT - 1 else None)

        E.wait("scalar", "ve", xn_v)
        xn_st = E.dma("scalar",
                      xnT_s.ap().rearrange("(c p) t -> p c t", p=128)[:, :, CH * j:CH * j + CH],
                      xbuf3, "xn")

        for proj, wsrc, dst in (("q", qwT, qT_s), ("k", kwT, kT_s), ("v", vwT, vT_s)):
            for ft in range(NFT):
                par = gi % 2
                wkey = f"w{par}"
                if gi >= 2:
                    E.wait("sync", "pe", slab_done[gi - 2])
                wld = E.dma(
                    "sync", wslab[par].rearrange("p (c f) -> p c f", f=128),
                    wsrc.rearrange("(c p) f -> p c f", p=128)[:, :, 128 * ft:128 * ft + 128],
                    wkey)
                E.wait("tensor", wkey, wld)
                E.wait("tensor", "ve", xn_v)
                E.wait("tensor", "ac", last_evac_ac)
                pv = None
                for c in range(NCT):
                    ik = "pe" if c == NCT - 1 else None
                    pv = E.op("tensor", lambda e, c=c, par=par: e.matmul(
                        P[0][:], wslab[par][:, 128 * c:128 * c + 128],
                        xbuf[:, CH * c:CH * c + CH],
                        start=(c == 0), stop=(c == NCT - 1)), ik)
                slab_done[gi] = pv
                if proj == "v":
                    vp = gi % 2
                    vkey = f"v{vp}"
                    E.wait("scalar", "pe", pv)
                    if vp in v_store:
                        E.wait("scalar", vkey, v_store[vp])
                    ev = E.op("scalar", lambda e, vp=vp: e.copy(vout[vp][:], P[0][:]), "ac")
                    last_evac_ac = ev
                    v_store[vp] = E.dma(
                        "scalar", dst[128 * ft:128 * ft + 128, CH * j:CH * j + CH],
                        vout[vp][:], vkey)
                else:
                    qp = gi % 2
                    qkey = f"qk{qp}"
                    E.wait("scalar", "pe", pv)
                    E.wait("scalar", "ve", last_rope_ve)
                    ev = E.op("scalar", lambda e: e.copy(traw[:], P[0][:]), "ac")
                    last_evac_ac = ev
                    E.wait("tensor", "ac", ev)
                    E.wait("tensor", "ve", last_rope_ve)
                    rot = E.op("tensor", lambda e: e.matmul(
                        P[1][:], ptT[:], traw[:], start=True, stop=True), "pe")
                    E.wait("vector", "pe", rot)
                    if qp in qr_store:
                        E.wait("vector", qkey, qr_store[qp])
                    E.op("vector", lambda e, j=j: e.tensor_mul(
                        t1b[:], traw[:], cosT[:, CH * j:CH * j + CH]))
                    E.op("vector", lambda e, j=j: e.tensor_mul(
                        t2b[:], P[1][:], sinT[:, CH * j:CH * j + CH]))
                    rope_v = E.op("vector", lambda e, qp=qp: e.tensor_add(
                        qrout[qp][:], t1b[:], t2b[:]), "ve")
                    if proj == "q" and j >= 1:
                        lo = {1: 1, 2: 0, 3: 0}[j]
                        hi = {1: 512, 2: 512, 3: 456}[j]
                        rs_v = E.op("vector", lambda e, qp=qp, lo=lo, hi=hi: e.reduce_sum(
                            sumq_t[:], qrout[qp][:, lo:hi], axis=AXX), "ve")
                        E.wait("vector", "ve", rs_v)
                        rope_v = E.op("vector", lambda e, ft=ft: e.tensor_add(
                            sumq[:, ft:ft + 1], sumq[:, ft:ft + 1], sumq_t[:]), "ve")
                        E.wait("vector", "ve", rope_v)
                    last_rope_ve = rope_v
                    E.wait("scalar", "ve", rope_v)
                    qr_store[qp] = E.dma(
                        "scalar", dst[128 * ft:128 * ft + 128, CH * j:CH * j + CH],
                        qrout[qp][:], qkey)
                gi += 1
        xchunk_users = {"pe": E.val("pe"), "ve": E.val("ve"), "xn": xn_st}

    kvis = [A4[:, 0:512], A4[:, 512:1024]]
    for kk in ("qk0", "qk1", "v0", "v1"):
        E.wait("sync", kk, E.val(kk))
    E.wait("tensor", "ve", E.val("ve"))
    imp_vals = []
    for h in range(NFT):
        kvk = f"kv{h % 2}"
        if h >= 2:
            E.wait("sync", "pe", imp_vals[h - 2])
        kvld = E.dma("sync", kvis[h % 2][:],
                     kT_s[128 * h:128 * h + 128, VIS_LO:VIS_HI], kvk)
        E.wait("tensor", kvk, kvld)
        imp_vals.append(E.op("tensor", lambda e, h=h: e.matmul(
            P6[:], sumq[:, h:h + 1], kvis[h % 2][:],
            start=(h == 0), stop=(h == NFT - 1)), "pe"))
    E.wait("vector", "pe", imp_vals[-1])
    imp_v = E.op("vector", lambda e: e.tensor_scalar(
        R["mean"], P6[:], impsc[0:1, 0:1], None, ALU.mult), "ve")
    E.wait("gpsimd", "ve", imp_v)
    E.dma("gpsimd", ar_in.ap(), R["mean"], "gp")
    E.wait("gpsimd", "gp")
    ar_done = E.op("gpsimd", lambda e: e.collective_compute(
        "AllReduce", ALU.add, replica_groups=[list(range(8))],
        ins=[ar_in.ap().opt()], outs=[ar_out.ap().opt()]), "cc")
    E.wait("gpsimd", "cc", ar_done)
    impr_ld = E.dma("gpsimd", R["mean"], ar_out.ap(), "gp")
    impc_ld = E.dma("gpsimd", impcols[:],
                    ar_out.ap().rearrange("one (v p) -> (one p) v", p=128), "gp",
                    slow=True)
    E.wait("tensor", "gp", impc_ld)
    bcimp = E.op("tensor", lambda e: e.matmul(
        P[2][:], ones_row, R["mean"], start=True, stop=True), "pe")
    E.wait("scalar", "pe", bcimp)
    impbc = A4[:, 1024:1536]
    onestile = A4[:, 1536:2048]
    cgt = A4[:, 2048:2560]
    ceq = A4[:, 2560:3072]
    ultv = A4[:, 3072:3584]
    impbc_cp = E.op("scalar", lambda e: e.copy(impbc[:], P[2][:]), "ac")
    E.wait("vector", "ac", impbc_cp)
    ones512_v = E.op("vector", lambda e: e.memset(onestile[:], 1.0), "ve")
    E.wait("gpsimd", "ve", ones512_v)
    E.wait("vector", "gp", impc_ld)
    for vt in range(4):
        asel_v = E.op("gpsimd", lambda e, vt=vt: e.affine_select(
            ultv[:], onestile[:], pattern=[[-1, 512]], compare_op=ALU.is_gt,
            fill=0.0, base=128 * vt, channel_multiplier=1), "gc")
        E.op("vector", lambda e, vt=vt: e.tensor_scalar(
            cgt[:], impbc[:], impcols[:, vt:vt + 1], None, ALU.is_gt))
        E.op("vector", lambda e, vt=vt: e.tensor_scalar(
            ceq[:], impbc[:], impcols[:, vt:vt + 1], None, ALU.is_equal))
        E.wait("vector", "gc", asel_v)
        E.op("vector", lambda e: e.tensor_mul(ceq[:], ceq[:], ultv[:]))
        E.op("vector", lambda e: e.tensor_add(cgt[:], cgt[:], ceq[:]))
        v1 = E.op("vector", lambda e: e.reduce_sum(rankw[:, 0:1], cgt[:], axis=AXX), "ve")
        E.wait("vector", "ve", v1)
        v2 = E.op("vector", lambda e: e.tensor_scalar(
            rankw[:, 1:2], rankw[:, 0:1], 255.5, None, ALU.is_lt), "ve")
        E.wait("vector", "ve", v2)
        v3 = E.op("vector", lambda e: e.tensor_scalar(
            rankw[:, 2:3], rankw[:, 1:2], 1e30, -1e30, ALU.mult, ALU.add), "ve")
        E.wait("vector", "ve", v3)
        kb_v = E.op("vector", lambda e, vt=vt: e.tensor_scalar(
            kbias[:, vt:vt + 1], rankw[:, 2:3], SHIFT, None, ALU.add), "ve")
        E.wait("vector", "ve", kb_v)
        E.wait("gpsimd", "ve", kb_v)
    E.dma("gpsimd", kbias_s[0:1, 0:1], m30[0:1, 0:1], "gp")
    E.dma("gpsimd",
          kbias_s[1:513, 0:1].rearrange("(v p) one -> p (v one)", p=128),
          kbias[:, 0:4], "gp", slow=True)
    E.dma("gpsimd", kbias_s[513:640, 0:1], m30[0:127, 0:1], "gp")
    kb_ld = None
    for kt in range(5):
        kb_ld = E.dma("gpsimd", kbias[:, kt:kt + 1],
                      kbias_s[128 * kt:128 * kt + 128, 0:1], "gp")

    kqv = {"k": [A1[:, 0:2048], A1[:, 8192:10240]],
           "q": [A1[:, 2048:4096], A1[:, 10240:12288]],
           "v": [A1[:, 4096:6144], A1[:, 12288:14336]],
           "vt": [A1[:, 6144:8192], A1[:, 14336:16384]]}
    pbuf = A2
    outsb = [A4[:, 0:512], A4[:, 512:1024]]
    rbc = A4[:, 1024:1536]
    rrow = R["b"]

    head_pe_done = {}
    out_store = {}
    prev_onorm = 0
    prev_rbc_cp = 0
    E.wait("scalar", "gp", kb_ld)
    for h in range(NFT):
        par = h % 2
        hkey = f"h{par}"
        if h >= 2:
            E.wait("sync", "pe", head_pe_done[h - 2])
        E.dma("sync", kqv["k"][par][:], kT_s[128 * h:128 * h + 128, :], hkey)
        E.dma("sync", kqv["q"][par][:], qT_s[128 * h:128 * h + 128, :], hkey)
        ldall = E.dma("sync", kqv["v"][par][:], vT_s[128 * h:128 * h + 128, :], hkey)
        E.wait("tensor", hkey, ldall)
        cp_vals = []
        for kt in range(16):
            pb = P[4 + (kt % 2)]
            if kt >= 2:
                E.wait("tensor", "ac", cp_vals[kt - 2])
            tr = E.op("tensor", lambda e, kt=kt, par=par, pb=pb: e.transpose(
                pb[:, 0:128], kqv["v"][par][:, 128 * kt:128 * kt + 128], ident[:]), "pe")
            E.wait("scalar", "pe", tr)
            cp_vals.append(E.op("scalar", lambda e, kt=kt, par=par, pb=pb: e.copy(
                kqv["vt"][par][:, 128 * kt:128 * kt + 128], pb[:, 0:128]), "ac"))
        vt_ready = cp_vals[-1]
        for j in range(NCH):
            nkt = 4 * (j + 1)
            E.wait("scalar", "gc", E.val("gc"))
            exp_done = {}
            for kt in range(nkt):
                sb = P[kt % 2]
                if kt >= 2:
                    E.wait("tensor", "ac", exp_done[kt - 2])
                smm = E.op("tensor", lambda e, kt=kt, j=j, par=par, sb=sb: e.matmul(
                    sb[:], kqv["k"][par][:, 128 * kt:128 * kt + 128],
                    kqv["q"][par][:, CH * j:CH * j + CH],
                    start=True, stop=True), "pe")
                E.wait("scalar", "pe", smm)
                bias = kbias[:, kt:kt + 1] if kt <= 4 else m30[:, 0:1]
                exp_done[kt] = E.op("scalar", lambda e, kt=kt, sb=sb, bias=bias:
                                    e.activation(pbuf[:, CH * kt:CH * kt + CH], sb[:],
                                                 AF.Exp, bias=bias, scale=SCALE), "ac")
            asel = {}
            for kt in range(4 * j, nkt):
                E.wait("gpsimd", "ac", exp_done[kt])
                asel[kt] = E.op("gpsimd", lambda e, kt=kt, j=j: e.affine_select(
                    pbuf[:, CH * kt:CH * kt + CH], pbuf[:, CH * kt:CH * kt + CH],
                    pattern=[[1, 512]], compare_op=ALU.is_ge, fill=0.0,
                    base=CH * j - 128 * kt, channel_multiplier=-1), "gc")
            E.wait("tensor", "ac", vt_ready)
            E.wait("tensor", "ve", prev_onorm)
            pvmm = None
            for kt in range(nkt):
                E.wait("tensor", "ac", exp_done[kt])
                if kt in asel:
                    E.wait("tensor", "gc", asel[kt])
                E.op("tensor", lambda e, kt=kt, par=par, nkt=nkt: e.matmul(
                    P[2][:], kqv["vt"][par][:, 128 * kt:128 * kt + 128],
                    pbuf[:, CH * kt:CH * kt + CH],
                    start=(kt == 0), stop=(kt == nkt - 1)))
                pvmm = E.op("tensor", lambda e, kt=kt, nkt=nkt: e.matmul(
                    P6[:], ones_col, pbuf[:, CH * kt:CH * kt + CH],
                    start=(kt == 0), stop=(kt == nkt - 1)), "pe")
            E.wait("vector", "pe", pvmm)
            rec_v = E.op("vector", lambda e: e.reciprocal(rrow, P6[:]), "ve")
            E.wait("tensor", "ve", rec_v)
            E.wait("tensor", "ac", prev_rbc_cp)
            bc2 = E.op("tensor", lambda e: e.matmul(
                P[3][:], ones_row, rrow, start=True, stop=True), "pe")
            E.wait("scalar", "pe", bc2)
            rbc_cp = E.op("scalar", lambda e: e.copy(rbc[:], P[3][:]), "ac")
            prev_rbc_cp = rbc_cp
            opar = (h * NCH + j) % 2
            obkey = f"ob{opar}"
            E.wait("vector", "ac", rbc_cp)
            if opar in out_store:
                E.wait("vector", obkey, out_store[opar])
            onorm = E.op("vector", lambda e, opar=opar: e.tensor_mul(
                outsb[opar][:], P[2][:], rbc[:]), "ve")
            prev_onorm = onorm
            E.wait("scalar", "ve", onorm)
            out_store[opar] = E.dma(
                "scalar", ohT_s.ap()[128 * h:128 * h + 128, CH * j:CH * j + CH],
                outsb[opar][:], obkey)
        head_pe_done[h] = E.val("pe")

    b_pe = E.val("pe")

    xnh = A1
    fw = [AI[:, 0:2048], AI[:, 2048:4096]]
    sxp = [A3[:, 0:2048], A3[:, 2048:4096]]
    wd = [A3[:, 4096:6144], A3[:, 6144:8192]]
    fout = [A4[:, 0:512], A4[:, 512:1024]]
    fpart = [A4[:, 1024:1536], A4[:, 1536:2048]]

    E.wait("vector", "ob0", E.val("ob0"))
    E.wait("vector", "ob1", E.val("ob1"))
    E.wait("gpsimd", "ob0", E.val("ob0"))
    E.wait("gpsimd", "ob1", E.val("ob1"))
    E.wait("gpsimd", "pe", b_pe)
    E.wait("gpsimd", "xn", E.val("xn"))
    xn_ld0 = E.dma(
        "gpsimd", xnh.rearrange("p (i t) -> p i t", t=1024),
        lambda: xnT_s.ap().rearrange("(i p) t -> p i t", p=128)[
            :, 0:16, DS(pidbox["goff"], 1024)], "gp")
    ag_done = E.op("gpsimd", lambda e: e.collective_compute(
        "AllGather", ALU.bypass, replica_groups=[list(range(8))],
        ins=[ohT_s.ap().opt()], outs=[ag_out.ap().opt()]), "cc")

    wd_mm = {}
    slab2_done = {}
    fout_store = {}
    fp_ev = {}
    di = 0
    for half in range(2):
        if half == 1:
            E.wait("gpsimd", "pe", wd_mm[di - 1])
            xn_ld1 = E.dma(
                "gpsimd", xnh.rearrange("p (i t) -> p i t", t=1024),
                lambda: xnT_s.ap().rearrange("(i p) t -> p i t", p=128)[
                    :, 16:32, DS(pidbox["goff"], 1024)], "gp")
            E.wait("sync", "fs0", E.val("fs0"))
            E.wait("sync", "fs1", E.val("fs1"))
        for ot in range(32):
            par = di % 2
            fkey = f"f{par}"
            if di >= 2:
                E.wait("sync", "ve", slab2_done[di - 2])
            E.dma("sync", fw[par].rearrange("p (i o) -> p i o", o=128),
                  fwT.rearrange("(i p) o -> p i o", p=128)[
                      :, 16 * half:16 * half + 16, 128 * ot:128 * ot + 128], fkey)
            sxld = E.dma("sync", sxp[par].rearrange("p (i o) -> p i o", o=128),
                         sexpT.rearrange("(i p) o -> p i o", p=128)[
                             :, 16 * half:16 * half + 16, 128 * ot:128 * ot + 128], fkey)
            E.wait("vector", fkey, sxld)
            if di >= 2:
                E.wait("vector", "pe", wd_mm[di - 2])
            E.op("vector", lambda e, par=par: e.tensor_copy(wd[par][:], fw[par][:]))
            dq_v = E.op("vector", lambda e, par=par: e.tensor_mul(
                wd[par][:], wd[par][:], sxp[par][:]), "ve")
            slab2_done[di] = dq_v
            E.wait("tensor", "ve", dq_v)
            if di == 0:
                E.wait("tensor", "gp", xn_ld0)
            if half == 1 and ot == 0:
                E.wait("tensor", "gp", xn_ld1)
            for tc in range(2):
                E.wait("tensor", "ve", fp_ev.get(tc, 0))
            mmv = None
            for i in range(16):
                for tc in range(2):
                    ik = "pe" if (i == 15 and tc == 1) else None
                    mmv = E.op("tensor", lambda e, i=i, tc=tc, par=par: e.matmul(
                        P[tc][:], wd[par][:, 128 * i:128 * i + 128],
                        xnh[:, 1024 * i + CH * tc: 1024 * i + CH * tc + CH],
                        start=(i == 0), stop=(i == 15)), ik)
            wd_mm[di] = mmv
            for tc in range(2):
                skey = f"fs{tc}" if half == 0 else f"fs{2 + tc}"
                E.wait("vector", "pe", mmv)
                if half == 0:
                    prev = fout_store.get(tc)
                    if prev is not None:
                        E.wait("vector", prev[0], prev[1])
                    ev = E.op("vector", lambda e, tc=tc: e.tensor_copy(
                        fout[tc][:], P[tc][:]), "ve")
                else:
                    E.wait("sync", "ve", fp_ev.get(tc, 0))
                    pld = E.dma("sync", fpart[tc][:],
                                ffnT_s[128 * ot:128 * ot + 128, CH * tc:CH * tc + CH],
                                f"p{tc}")
                    E.wait("vector", f"p{tc}", pld)
                    prev = fout_store.get(tc)
                    if prev is not None:
                        E.wait("vector", prev[0], prev[1])
                    E.op("vector", lambda e, tc=tc, ot=ot: e.tensor_scalar(
                        fout[tc][:], P[tc][:], fbias_c[:, ot:ot + 1], None, ALU.add))
                    ev = E.op("vector", lambda e, tc=tc: e.tensor_add(
                        fout[tc][:], fout[tc][:], fpart[tc][:]), "ve")
                fp_ev[tc] = ev
                E.wait("scalar", "ve", ev)
                fout_store[tc] = (skey, E.dma(
                    "scalar", ffnT_s[128 * ot:128 * ot + 128, CH * tc:CH * tc + CH],
                    fout[tc][:], skey))
            di += 1
    d1_pe = E.val("pe")

    ohh = A1
    ows = [A3[:, 0:2048], A3[:, 2048:4096]]
    extr = A3[:, 4096:8192]
    oout = [A4[:, 0:512], A4[:, 512:1024]]

    def extile(kind, par):
        base = {"x": 0, "ffn": 1024, "ap": 2048}[kind] + 512 * par
        return extr[:, base:base + 512]

    oslab_done = {}
    oo_store = {}
    oev = {}
    oi = 0
    E.wait("gpsimd", "cc", ag_done)
    for half in range(2):
        rbk = "rb0" if half == 0 else "rb1"
        E.wait("gpsimd", "pe", d1_pe if half == 0 else oslab_done[oi - 1])
        ohld = E.dma(
            "gpsimd", ohh.rearrange("p (f t) -> p f t", t=1024),
            lambda rbk=rbk: ag_out.ap()[
                DS(pidbox[rbk], 2048), DS(pidbox["goff"], 1024)].rearrange(
                "(f p) t -> p f t", p=128), "gp")
        if half == 1:
            E.wait("sync", "ap0", E.val("ap0"))
            E.wait("sync", "ap1", E.val("ap1"))
            E.wait("sync", "fs2", E.val("fs2"))
            E.wait("sync", "fs3", E.val("fs3"))
        for ot in range(32):
            par = oi % 2
            okey = f"o{par}"
            if oi >= 2:
                E.wait("sync", "pe", oslab_done[oi - 2])
            owld = E.dma("sync", ows[par].rearrange("p (f o) -> p f o", o=128),
                         owT.rearrange("(f p) o -> p f o", p=128)[
                             :, 16 * half:16 * half + 16, 128 * ot:128 * ot + 128], okey)
            E.wait("tensor", okey, owld)
            if ot == 0:
                E.wait("tensor", "gp", ohld)
            for tc in range(2):
                E.wait("tensor", "ve", oev.get(tc, 0))
            ommv = None
            for f in range(16):
                for tc in range(2):
                    ik = "pe" if (f == 15 and tc == 1) else None
                    ommv = E.op("tensor", lambda e, f=f, tc=tc, par=par: e.matmul(
                        P[tc][:], ows[par][:, 128 * f:128 * f + 128],
                        ohh[:, 1024 * f + CH * tc: 1024 * f + CH * tc + CH],
                        start=(f == 0), stop=(f == 15)), ik)
            oslab_done[oi] = ommv
            for tc in range(2):
                E.wait("vector", "pe", ommv)
                prev = oo_store.get(tc)
                if half == 0:
                    if prev is not None:
                        E.wait("vector", prev[0], prev[1])
                    ev = E.op("vector", lambda e, tc=tc: e.tensor_copy(
                        oout[tc][:], P[tc][:]), "ve")
                    oev[tc] = ev
                    E.wait("scalar", "ve", ev)
                    oo_store[tc] = (f"ap{tc}", E.dma(
                        "scalar", apT_s[128 * ot:128 * ot + 128, CH * tc:CH * tc + CH],
                        oout[tc][:], f"ap{tc}"))
                else:
                    ekey = f"e{tc}"
                    E.wait("sync", "ve", oev.get(tc, 0))
                    E.dma("sync", extile("ap", tc)[:],
                          apT_s[128 * ot:128 * ot + 128, CH * tc:CH * tc + CH], ekey)
                    E.dma("sync", extile("ffn", tc)[:],
                          ffnT_s[128 * ot:128 * ot + 128, CH * tc:CH * tc + CH], ekey)
                    xld2 = E.dma("sync", extile("x", tc)[:],
                                 xTo[128 * ot:128 * ot + 128, CH * tc:CH * tc + CH],
                                 ekey)
                    E.wait("vector", ekey, xld2)
                    if prev is not None:
                        E.wait("vector", prev[0], prev[1])
                    E.op("vector", lambda e, tc=tc: e.tensor_add(
                        oout[tc][:], P[tc][:], extile("ap", tc)[:]))
                    E.op("vector", lambda e, tc=tc: e.tensor_add(
                        oout[tc][:], oout[tc][:], extile("x", tc)[:]))
                    ev = E.op("vector", lambda e, tc=tc: e.tensor_add(
                        oout[tc][:], oout[tc][:], extile("ffn", tc)[:]), "ve")
                    oev[tc] = ev
                    E.wait("scalar", "ve", ev)
                    oo_store[tc] = (f"ou{tc}", E.dma(
                        "scalar", outT[128 * ot:128 * ot + 128, CH * tc:CH * tc + CH],
                        oout[tc][:], f"ou{tc}"))
            oi += 1

    E.wait("scalar", "ou0", E.val("ou0"))
    E.wait("scalar", "ou1", E.val("ou1"))
    E.wait("gpsimd", "gp")

    with nc.Block() as block:
        block.sync(lambda e: [f(e) for f in E.ops["sync"]])
        block.scalar(lambda e: [f(e) for f in E.ops["scalar"]])
        block.vector(lambda e: [f(e) for f in E.ops["vector"]])
        block.gpsimd(lambda e: [f(e) for f in E.ops["gpsimd"]])
        block.tensor(lambda e: [f(e) for f in E.ops["tensor"]])

    for k in sem_ctx:
        sem_ctx[k].__exit__(None, None, None)
    return nc


_NC_CACHE = {}


def kernel(L0_attn_hidden_states, L0_attn_q_w, L0_attn_k_w, L0_attn_v_w,
           L0_attn_o_w, L0_ffn_qweight, L0_ffn_scales, L0_ffn_bias):
    x = np.asarray(L0_attn_hidden_states, dtype=np.float32)
    q_w = np.asarray(L0_attn_q_w, dtype=np.float32)
    k_w = np.asarray(L0_attn_k_w, dtype=np.float32)
    v_w = np.asarray(L0_attn_v_w, dtype=np.float32)
    o_w = np.asarray(L0_attn_o_w, dtype=np.float32)
    qw_i = np.asarray(L0_ffn_qweight, dtype=np.int32)
    scales = np.asarray(L0_ffn_scales, dtype=np.float32)
    fb = np.asarray(L0_ffn_bias, dtype=np.float32)

    owT = np.ascontiguousarray(o_w.T)
    fwT = np.ascontiguousarray(qw_i.T)
    s2 = scales.reshape(HID, 64)
    sexpT = np.ascontiguousarray(np.repeat(s2.T, 64, axis=0))
    fbias = np.ascontiguousarray(fb.reshape(HID, 1))

    inv = 1.0 / (10000.0 ** (np.arange(0, DH, 2, dtype=np.float64) / DH))
    freqs = np.arange(S, dtype=np.float64)[:, None] * inv[None, :]
    emb = np.concatenate([freqs, freqs], axis=-1)
    cosT = np.ascontiguousarray(np.cos(emb).T.astype(np.float32))
    sinT = np.ascontiguousarray(np.sin(emb).T.astype(np.float32))

    ptT = np.zeros((128, 128), dtype=np.float32)
    ptT[np.arange(64), np.arange(64) + 64] = 1.0
    ptT[np.arange(64) + 64, np.arange(64)] = -1.0

    ident = np.eye(128, dtype=np.float32)
    onesm = np.ones((128, 128), dtype=np.float32)

    in_maps = []
    for c in range(NCORES):
        b, g = c // 2, c % 2
        in_maps.append({
            "xT": np.ascontiguousarray(x[b].T),
            "xTo": np.ascontiguousarray(x[b].T[:, 1024 * g:1024 * g + 1024]),
            "qwT": np.ascontiguousarray(q_w.T[:, 2048 * g:2048 * g + 2048]),
            "kwT": np.ascontiguousarray(k_w.T[:, 2048 * g:2048 * g + 2048]),
            "vwT": np.ascontiguousarray(v_w.T[:, 2048 * g:2048 * g + 2048]),
            "owT": owT, "fwT": fwT, "sexpT": sexpT, "fbias": fbias,
            "cosT": cosT, "sinT": sinT, "ptT": ptT, "ident": ident,
            "onesm": onesm,
            "impsc": np.full((1, 1), 1.0 if b == 0 else 0.0, dtype=np.float32),
        })

    if "prog" not in _NC_CACHE:
        _NC_CACHE["prog"] = build_program()
    nc = _NC_CACHE["prog"]
    res = run_bass_kernel_spmd(nc, in_maps, core_ids=list(range(NCORES)))

    out = np.empty((B, S, HID), dtype=np.float32)
    for c in range(NCORES):
        b, g = c // 2, c % 2
        out[b, 1024 * g:1024 * g + 1024, :] = res.results[c]["outT"].T
    return out


# version 03
# speedup vs baseline: 1.0542x; 1.0542x over previous
import numpy as np
import concourse.bass as bass
import concourse.mybir as mybir
from concourse.bass_utils import run_bass_kernel_spmd

F32 = mybir.dt.float32
BF16 = mybir.dt.bfloat16
I32 = mybir.dt.int32
AF = mybir.ActivationFunctionType
ALU = mybir.AluOpType
AXX = mybir.AxisListType.X
DS = bass.ds

B, S, H, DH, HID = 4, 2048, 32, 128, 4096
NCORES = 8
SCALE = 1.0 / float(np.sqrt(128.0))
SHIFT = -30.0
EPS = 1e-5
VIS_LO, VIS_HI = 1, 513
NCH = 4
CH = 512
NFT = 16
NCT = 32

DMA_KEYS = ("c", "x0", "x1", "w0", "w1", "gw0", "gw1", "kv0", "kv1",
            "h0", "h1", "f0", "f1", "p0", "p1", "o0", "o1", "go0", "go1",
            "e0", "e1", "xs", "qk0", "qk1", "v0", "v1", "ob0", "ob1",
            "fs0", "fs1", "fs2", "fs3", "ap0", "ap1", "ou0", "ou1", "gp")
ENG_KEYS = ("pe", "ac", "ve", "gc", "cc")


class Emitter:
    def __init__(self, sems):
        self.ops = {k: [] for k in ("sync", "scalar", "vector", "gpsimd", "tensor")}
        self.cnt = {}
        self.last_wait = {k: {} for k in self.ops}
        self.sems = sems

    def inc(self, key, n):
        self.cnt[key] = self.cnt.get(key, 0) + n
        return self.cnt[key]

    def val(self, key):
        return self.cnt.get(key, 0)

    def wait(self, eng, key, v=None):
        v = self.val(key) if v is None else v
        if not v or v <= 0:
            return
        lw = self.last_wait[eng]
        if lw.get(key, -1) >= v:
            return
        lw[key] = v
        s = self.sems[key]
        self.ops[eng].append(lambda e, s=s, vv=v: e.wait_ge(s, vv))

    def dma(self, q, out_ap, in_ap, key, slow=False):
        prev = self.val(key)
        if prev:
            self.wait(q, key, prev)
        v = self.inc(key, 16)
        s = self.sems[key]

        def run(e, o=out_ap, i=in_ap, s=s, slow=slow):
            if callable(o):
                o = o()
            if callable(i):
                i = i()
            if slow:
                e.dma_start(out=o, in_=i, allow_slow_non_contiguous=True).then_inc(s, 16)
            else:
                e.dma_start(out=o, in_=i).then_inc(s, 16)
        self.ops[q].append(run)
        return v

    def op(self, eng, fn, inc_key=None, inc_n=1):
        if inc_key is None:
            self.ops[eng].append(fn)
            return None
        v = self.inc(inc_key, inc_n)
        s = self.sems[inc_key]
        self.ops[eng].append(lambda e, f=fn, s=s, n=inc_n: f(e).then_inc(s, n))
        return v


def build_program():
    nc = bass.Bass("TRN2", target_bir_lowering=False, detect_race_conditions=False)

    xT = nc.dram_tensor("xT", [HID, S], F32, kind="ExternalInput").ap()
    qwT = nc.dram_tensor("qwT", [HID, 2048], F32, kind="ExternalInput").ap()
    kwT = nc.dram_tensor("kwT", [HID, 2048], F32, kind="ExternalInput").ap()
    vwT = nc.dram_tensor("vwT", [HID, 2048], F32, kind="ExternalInput").ap()
    owT = nc.dram_tensor("owT", [HID, HID], F32, kind="ExternalInput").ap()
    fwT = nc.dram_tensor("fwT", [HID, HID], I32, kind="ExternalInput").ap()
    sexpT = nc.dram_tensor("sexpT", [HID, HID], F32, kind="ExternalInput").ap()
    fbias = nc.dram_tensor("fbias", [HID, 1], F32, kind="ExternalInput").ap()
    cosT_d = nc.dram_tensor("cosT", [128, S], F32, kind="ExternalInput").ap()
    sinT_d = nc.dram_tensor("sinT", [128, S], F32, kind="ExternalInput").ap()
    ptT_d = nc.dram_tensor("ptT", [128, 128], F32, kind="ExternalInput").ap()
    ident_d = nc.dram_tensor("ident", [128, 128], F32, kind="ExternalInput").ap()
    ones_d = nc.dram_tensor("onesm", [128, 128], F32, kind="ExternalInput").ap()
    onesb_d = nc.dram_tensor("onesb", [128, 128], BF16, kind="ExternalInput").ap()
    identb_d = nc.dram_tensor("identb", [128, 128], BF16, kind="ExternalInput").ap()
    impsc_d = nc.dram_tensor("impsc", [1, 1], F32, kind="ExternalInput").ap()
    xTo = nc.dram_tensor("xTo", [HID, 1024], F32, kind="ExternalInput").ap()
    outT = nc.dram_tensor("outT", [HID, 1024], F32, kind="ExternalOutput").ap()

    qhT_s = nc.dram_tensor("qhT_s", [2048, S], BF16).ap()
    qlT_s = nc.dram_tensor("qlT_s", [2048, S], BF16).ap()
    khT_s = nc.dram_tensor("khT_s", [2048, S], BF16).ap()
    klT_s = nc.dram_tensor("klT_s", [2048, S], BF16).ap()
    vT_s = nc.dram_tensor("vT_s", [2048, S], F32).ap()
    xnhT_s = nc.dram_tensor("xnhT_s", [HID, S], BF16)
    xnlT_s = nc.dram_tensor("xnlT_s", [HID, S], BF16)
    ohT_s = nc.dram_tensor("ohT_s", [2048, S], F32)
    ag_out = nc.dram_tensor("ag_out", [8 * 2048, S], F32)
    ffnT_s = nc.dram_tensor("ffnT_s", [HID, 1024], F32).ap()
    apT_s = nc.dram_tensor("apT_s", [HID, 1024], F32).ap()
    ar_in = nc.dram_tensor("ar_in", [1, 512], F32)
    ar_out = nc.dram_tensor("ar_out", [1, 512], F32)
    kbias_s = nc.dram_tensor("kbias_s", [640, 1], F32).ap()

    FA = nc.alloc_sbuf_tensor("FA", [128, 13824], F32).ap()
    AB = nc.alloc_sbuf_tensor("AB", [128, 53248], BF16).ap()
    AI = nc.alloc_sbuf_tensor("AI", [128, 4096], I32).ap()
    ROWS = nc.alloc_sbuf_tensor("ROWS", [1, 3072], F32).ap()
    cosT = nc.alloc_sbuf_tensor("cosT_sb", [128, S], F32).ap()
    sinT = nc.alloc_sbuf_tensor("sinT_sb", [128, S], F32).ap()
    ptT = nc.alloc_sbuf_tensor("ptT_sb", [128, 128], F32).ap()
    ident = nc.alloc_sbuf_tensor("ident_sb", [128, 128], F32).ap()
    ones = nc.alloc_sbuf_tensor("ones_sb", [128, 128], F32).ap()
    onesb = nc.alloc_sbuf_tensor("onesb_sb", [128, 128], BF16).ap()
    identb = nc.alloc_sbuf_tensor("identb_sb", [128, 128], BF16).ap()
    sumq = nc.alloc_sbuf_tensor("sumq_sb", [128, 16], F32).ap()
    sumqh = nc.alloc_sbuf_tensor("sumqh_sb", [128, 16], BF16).ap()
    sumql = nc.alloc_sbuf_tensor("sumql_sb", [128, 16], BF16).ap()
    sumq_t = nc.alloc_sbuf_tensor("sumq_t", [128, 1], F32).ap()
    kbias = nc.alloc_sbuf_tensor("kbias_sb", [128, 8], F32).ap()
    m30 = nc.alloc_sbuf_tensor("m30_sb", [128, 1], F32).ap()
    impsc = nc.alloc_sbuf_tensor("impsc_sb", [1, 1], F32).ap()
    impcols = nc.alloc_sbuf_tensor("impcols_sb", [128, 4], F32).ap()
    fbias_c = nc.alloc_sbuf_tensor("fbias_sb", [128, 32], F32).ap()
    rankw = nc.alloc_sbuf_tensor("rankw_sb", [128, 8], F32).ap()

    P = [nc.alloc_psum_tensor(f"P{i}", [128, 512], F32).ap() for i in range(6)]
    P6 = nc.alloc_psum_tensor("P6", [1, 512], F32).ap()
    P7 = nc.alloc_psum_tensor("P7", [1, 512], F32).ap()

    sems, sem_ctx = {}, {}
    for k in DMA_KEYS + ENG_KEYS:
        g = nc.semaphore(k)
        sems[k] = g.__enter__()
        sem_ctx[k] = g
    E = Emitter(sems)

    R = {name: ROWS[0:1, 512 * i:512 * i + 512]
         for i, name in enumerate(("mean", "vpe", "b", "ss", "rstd", "negmr"))}

    ones_col = ones[:, 0:1]
    ones_row = ones[0:1, :]
    onesb_col = onesb[:, 0:1]

    E.dma("sync", cosT[:], cosT_d, "c")
    E.dma("sync", sinT[:], sinT_d, "c")
    E.dma("sync", ptT[:], ptT_d, "c")
    E.dma("sync", ident[:], ident_d, "c")
    E.dma("sync", ones[:], ones_d, "c")
    E.dma("sync", onesb[:], onesb_d, "c")
    E.dma("sync", identb[:], identb_d, "c")
    E.dma("sync", impsc[:], impsc_d, "c")
    consts_ld = E.dma("sync", fbias_c[:],
                      fbias[:, 0:1].rearrange("(o p) one -> p (o one)", p=128), "c",
                      slow=True)
    E.wait("vector", "c", consts_ld)
    E.op("vector", lambda e: e.memset(sumq[:], 0.0))
    E.op("vector", lambda e: e.memset(m30[:], SHIFT), "ve")

    pidbox = {}

    def gp_init(e):
        pid = nc.partition_id([mybir.EngineType.Pool])
        g = pid % 2
        pidbox["goff"] = g * 1024
        for half in range(2):
            rb = (pid - g + half) * 2048
            for kq in range(4):
                pidbox[f"go{half}_{kq}"] = g * 1024 + 256 * kq
                pidbox[f"rb{half}"] = rb
        return e.memset(rankw[:, 7:8], 0.0)
    E.op("gpsimd", gp_init, "gc")

    xtile = [FA[:, 0:512], FA[:, 512:1024]]
    wslab = [FA[:, 1024:5120], FA[:, 1024:5120]]
    vout = [FA[:, 5120:5632], FA[:, 5632:6144]]
    traw = FA[:, 9216:9728]
    t1b = FA[:, 9728:10240]
    t2b = FA[:, 10240:10752]
    qrout = [FA[:, 10752:11264], FA[:, 11264:11776]]
    bc_rstd = FA[:, 12800:13312]
    bc_negmr = FA[:, 13312:13824]
    xsq = [FA[:, 11776:12288], FA[:, 12288:12800]]

    xnh = AB[:, 0:16384]
    xnl = AB[:, 16384:32768]
    wsh = [AB[:, 32768:36864], AB[:, 36864:40960]]
    wsl = [AB[:, 40960:45056], AB[:, 45056:49152]]
    qrh = [AB[:, 49152:49664], AB[:, 49664:50176]]
    qrl = [AB[:, 50176:50688], AB[:, 50688:51200]]

    slab_mm = {}
    slab_sub = {}
    qr_store = {}
    v_store = {}
    prev_nm_v = 0
    prev_bc_cp = 0
    last_traw_ve = 0
    last_rot_ve = 0
    last_p0_user = {"q": 0, "k": 0}
    gi = 0
    xn_split_done = 0

    for j in range(NCH):
        E.wait("tensor", "ve", prev_nm_v)
        sq_vals, ssq_vals = [], []
        for c in range(NCT):
            par = c % 2
            xk = f"x{par}"
            if c >= 2:
                E.wait("sync", "pe", ssq_vals[c - 2])
            xld = E.dma("sync", xtile[par][:],
                        xT.rearrange("(c p) t -> p c t", p=128)[
                            :, c, CH * j:CH * j + CH], xk)
            E.wait("scalar", xk, xld)
            if c >= 2:
                E.wait("scalar", "pe", ssq_vals[c - 2])
            sq_vals.append(E.op("scalar", lambda e, par=par: e.activation(
                xsq[par][:], xtile[par][:], AF.Square), "ac"))
            E.wait("tensor", xk, xld)
            E.op("tensor", lambda e, par=par, c=c: e.matmul(
                P6[:], ones_col, xtile[par][:],
                start=(c == 0), stop=(c == NCT - 1)))
            E.wait("tensor", "ac", sq_vals[c])
            ssq_vals.append(E.op("tensor", lambda e, par=par, c=c: e.matmul(
                P7[:], ones_col, xsq[par][:],
                start=(c == 0), stop=(c == NCT - 1)), "pe"))
        E.wait("vector", "pe", ssq_vals[-1])
        E.op("vector", lambda e: e.tensor_scalar(R["mean"], P6[:], 1.0 / HID, None, ALU.mult))
        E.op("vector", lambda e: e.tensor_scalar(R["vpe"], P7[:], 1.0 / HID, None, ALU.mult))
        E.op("vector", lambda e: e.tensor_mul(R["b"], R["mean"], R["mean"]))
        E.op("vector", lambda e: e.scalar_tensor_tensor(
            R["vpe"], R["vpe"], EPS, R["b"], ALU.add, ALU.subtract))
        rr_v = E.op("vector", lambda e: e.reciprocal(R["b"], R["vpe"]), "ve")
        E.wait("scalar", "ve", rr_v)
        ss_v = E.op("scalar", lambda e: e.activation(R["ss"], R["b"], AF.Sqrt), "ac")
        E.wait("vector", "ac", ss_v)
        E.op("vector", lambda e: e.tensor_mul(R["b"], R["ss"], R["ss"]))
        E.op("vector", lambda e: e.tensor_mul(R["b"], R["b"], R["vpe"]))
        E.op("vector", lambda e: e.tensor_scalar(R["b"], R["b"], -0.5, 1.5, ALU.mult, ALU.add))
        E.op("vector", lambda e: e.tensor_mul(R["rstd"], R["ss"], R["b"]))
        E.op("vector", lambda e: e.tensor_mul(R["negmr"], R["mean"], R["rstd"]))
        nm_v = E.op("vector", lambda e: e.tensor_scalar(
            R["negmr"], R["negmr"], -1.0, None, ALU.mult), "ve")
        prev_nm_v = nm_v
        E.wait("tensor", "ve", nm_v)
        E.wait("tensor", "ac", prev_bc_cp)
        bc_mm = E.op("tensor", lambda e: (
            e.matmul(P[2][:], ones_row, R["rstd"], start=True, stop=True),
            e.matmul(P[3][:], ones_row, R["negmr"], start=True, stop=True))[-1], "pe")
        E.wait("scalar", "pe", bc_mm)
        E.op("scalar", lambda e: e.copy(bc_rstd[:], P[2][:]))
        bc_cp = E.op("scalar", lambda e: e.copy(bc_negmr[:], P[3][:]), "ac")
        prev_bc_cp = bc_cp
        E.wait("vector", "ac", bc_cp)

        E.wait("sync", "pe", ssq_vals[-1])
        pe_cum = E.val("pe")
        E.wait("scalar", "pe", pe_cum)
        E.wait("vector", "pe", pe_cum)
        E.wait("scalar", "xs", E.val("xs"))
        E.wait("vector", "xs", E.val("xs"))
        xn_sub = None
        xn_subs = []
        for c in range(NCT):
            par = c % 2
            xk = f"x{par}"
            if c >= 2:
                E.wait("sync", "ve", xn_subs[c - 2])
            xld = E.dma("sync", xtile[par][:],
                        xT.rearrange("(c p) t -> p c t", p=128)[
                            :, c, CH * j:CH * j + CH], xk)
            E.wait("vector", xk, xld)
            E.op("vector", lambda e, par=par: e.tensor_mul(
                xtile[par][:], xtile[par][:], bc_rstd[:]))
            aff_v = E.op("vector", lambda e, par=par: e.tensor_add(
                xtile[par][:], xtile[par][:], bc_negmr[:]), "ve")
            E.wait("scalar", "ve", aff_v)
            hc_v = E.op("scalar", lambda e, par=par, c=c: e.copy(
                xnh[:, CH * c:CH * c + CH], xtile[par][:]), "ac")
            E.wait("vector", "ac", hc_v)
            xn_sub = E.op("vector", lambda e, par=par, c=c: e.tensor_sub(
                xnl[:, CH * c:CH * c + CH], xtile[par][:],
                xnh[:, CH * c:CH * c + CH]), "ve")
            xn_subs.append(xn_sub)
        xn_split_done = xn_sub

        E.wait("scalar", "ve", xn_split_done)
        E.dma("scalar",
              xnhT_s.ap().rearrange("(c p) t -> p c t", p=128)[:, :, CH * j:CH * j + CH],
              xnh.rearrange("p (c t) -> p c t", t=CH), "xs")
        xs_st = E.dma("scalar",
                      xnlT_s.ap().rearrange("(c p) t -> p c t", p=128)[:, :, CH * j:CH * j + CH],
                      xnl.rearrange("p (c t) -> p c t", t=CH), "xs")

        for proj, wsrc, dsth, dstl in (("q", qwT, qhT_s, qlT_s),
                                       ("k", kwT, khT_s, klT_s),
                                       ("v", vwT, vT_s, None)):
            for ft in range(NFT):
                par = gi % 2
                wk, gk = f"w{par}", f"gw{par}"
                wsrc_ap = wsrc.rearrange("(c p) f -> p c f", p=128)[
                    :, :, 128 * ft:128 * ft + 128]
                if gi >= 1:
                    E.wait("sync", "ve", slab_sub.get(gi - 1, 0))
                if gi >= 2:
                    E.wait("gpsimd", "pe", slab_mm.get(gi - 2, 0))
                wld = E.dma("sync", wslab[par].rearrange("p (c f) -> p c f", f=128),
                            wsrc_ap, wk)
                gld = E.dma("gpsimd", wsh[par].rearrange("p (c f) -> p c f", f=128),
                            wsrc_ap, gk)
                E.wait("vector", wk, wld)
                E.wait("vector", gk, gld)
                E.wait("vector", "pe", slab_mm.get(gi - 2, 0))
                dq_v = E.op("vector", lambda e, par=par: e.tensor_sub(
                    wsl[par][:], wslab[par][:], wsh[par][:]), "ve")
                slab_sub[gi] = dq_v
                E.wait("tensor", "ve", dq_v)
                E.wait("tensor", "ve", xn_split_done)
                E.wait("tensor", "ac", last_p0_user.get("evac", 0))
                E.wait("tensor", "ve", last_p0_user.get("evacv", 0))
                pv = None
                for c in range(NCT):
                    ik = "pe" if c == NCT - 1 else None
                    pv = E.op("tensor", lambda e, c=c, par=par: (
                        e.matmul(P[0][:], wsh[par][:, 128 * c:128 * c + 128],
                                 xnh[:, CH * c:CH * c + CH],
                                 start=(c == 0), stop=False),
                        e.matmul(P[0][:], wsh[par][:, 128 * c:128 * c + 128],
                                 xnl[:, CH * c:CH * c + CH],
                                 start=False, stop=False),
                        e.matmul(P[0][:], wsl[par][:, 128 * c:128 * c + 128],
                                 xnh[:, CH * c:CH * c + CH],
                                 start=False, stop=(c == NCT - 1)))[-1], ik)
                slab_mm[gi] = pv
                if proj == "v":
                    vp = gi % 2
                    vkey = f"v{vp}"
                    E.wait("scalar", "pe", pv)
                    if vp in v_store:
                        E.wait("scalar", vkey, v_store[vp])
                    hv = E.op("scalar", lambda e, vp=vp: e.copy(vout[vp][:], P[0][:]), "ac")
                    last_p0_user["evac"] = hv
                    E.wait("scalar", "ve", 0)
                    v_store[vp] = E.dma(
                        "scalar", dsth[128 * ft:128 * ft + 128, CH * j:CH * j + CH],
                        vout[vp][:], vkey)
                else:
                    qp = gi % 2
                    qkey = f"qk{qp}"
                    E.wait("scalar", "pe", pv)
                    E.wait("scalar", "ve", last_traw_ve)
                    ev = E.op("scalar", lambda e: e.copy(traw[:], P[0][:]), "ac")
                    last_p0_user["evac"] = ev
                    E.wait("tensor", "ac", ev)
                    E.wait("tensor", "ve", last_rot_ve)
                    rot = E.op("tensor", lambda e: e.matmul(
                        P[1][:], ptT[:], traw[:], start=True, stop=True), "pe")
                    E.wait("vector", "pe", rot)
                    if qp in qr_store:
                        E.wait("vector", qr_store[qp][0], qr_store[qp][1])
                    tv = E.op("vector", lambda e, j=j: e.tensor_mul(
                        t1b[:], traw[:], cosT[:, CH * j:CH * j + CH]), "ve")
                    last_traw_ve = tv
                    rot_used = E.op("vector", lambda e, j=j: e.tensor_mul(
                        t2b[:], P[1][:], sinT[:, CH * j:CH * j + CH]), "ve")
                    last_rot_ve = rot_used
                    rope_v = E.op("vector", lambda e, qp=qp: e.tensor_add(
                        qrout[qp][:], t1b[:], t2b[:]), "ve")
                    if proj == "q" and j >= 1:
                        lo = {1: 1, 2: 0, 3: 0}[j]
                        hi = {1: 512, 2: 512, 3: 456}[j]
                        rs_v = E.op("vector", lambda e, qp=qp, lo=lo, hi=hi: e.reduce_sum(
                            sumq_t[:], qrout[qp][:, lo:hi], axis=AXX), "ve")
                        E.wait("vector", "ve", rs_v)
                        acc_v = E.op("vector", lambda e, ft=ft: e.tensor_add(
                            sumq[:, ft:ft + 1], sumq[:, ft:ft + 1], sumq_t[:]), "ve")
                        E.wait("vector", "ve", acc_v)
                    E.wait("scalar", "ve", rope_v)
                    qh_v = E.op("scalar", lambda e, qp=qp: e.copy(
                        qrh[qp][:], qrout[qp][:]), "ac")
                    E.wait("vector", "ac", qh_v)
                    ql_v = E.op("vector", lambda e, qp=qp: e.tensor_sub(
                        qrl[qp][:], qrout[qp][:], qrh[qp][:]), "ve")
                    E.wait("scalar", "ve", ql_v)
                    E.dma("scalar", dsth[128 * ft:128 * ft + 128, CH * j:CH * j + CH],
                          qrh[qp][:], qkey)
                    st2 = E.dma("scalar", dstl[128 * ft:128 * ft + 128, CH * j:CH * j + CH],
                                qrl[qp][:], qkey)
                    qr_store[qp] = (qkey, st2)
                gi += 1

    kvh = [AB[:, 32768 + i * 512:32768 + i * 512 + 512] for i in range(2)]
    kvl = [AB[:, 33792 + i * 512:33792 + i * 512 + 512] for i in range(2)]
    for kk in ("qk0", "qk1", "v0", "v1", "xs"):
        E.wait("sync", kk, E.val(kk))
    E.wait("sync", "pe", E.val("pe"))
    sq_all = E.val("ve")
    E.wait("scalar", "ve", sq_all)
    sqh_v = E.op("scalar", lambda e: e.copy(sumqh[:], sumq[:]), "ac")
    E.wait("vector", "ac", sqh_v)
    sql_v = E.op("vector", lambda e: e.tensor_sub(sumql[:], sumq[:], sumqh[:]), "ve")
    E.wait("vector", "ve", sql_v)
    imp_vals = []
    for h in range(NFT):
        kvk = f"kv{h % 2}"
        if h >= 2:
            E.wait("sync", "pe", imp_vals[h - 2])
        E.dma("sync", kvh[h % 2][:], khT_s[128 * h:128 * h + 128, VIS_LO:VIS_HI], kvk)
        kvld = E.dma("sync", kvl[h % 2][:], klT_s[128 * h:128 * h + 128, VIS_LO:VIS_HI], kvk)
        E.wait("tensor", kvk, kvld)
        E.wait("tensor", "ve", sql_v)
        imp_vals.append(E.op("tensor", lambda e, h=h: (
            e.matmul(P6[:], sumqh[:, h:h + 1], kvh[h % 2][:],
                     start=(h == 0), stop=False),
            e.matmul(P6[:], sumqh[:, h:h + 1], kvl[h % 2][:],
                     start=False, stop=False),
            e.matmul(P6[:], sumql[:, h:h + 1], kvh[h % 2][:],
                     start=False, stop=(h == NFT - 1)))[-1], "pe"))
    E.wait("vector", "pe", imp_vals[-1])
    imp_v = E.op("vector", lambda e: e.tensor_scalar(
        R["mean"], P6[:], impsc[0:1, 0:1], None, ALU.mult), "ve")
    E.wait("gpsimd", "ve", imp_v)
    E.dma("gpsimd", ar_in.ap(), R["mean"], "gp")
    E.wait("gpsimd", "gp")
    ar_done = E.op("gpsimd", lambda e: e.collective_compute(
        "AllReduce", ALU.add, replica_groups=[list(range(8))],
        ins=[ar_in.ap().opt()], outs=[ar_out.ap().opt()]), "cc")
    E.wait("gpsimd", "cc", ar_done)
    impr_ld = E.dma("gpsimd", R["mean"], ar_out.ap(), "gp")
    impc_ld = E.dma("gpsimd", impcols[:],
                    ar_out.ap().rearrange("one (v p) -> (one p) v", p=128), "gp",
                    slow=True)
    E.wait("tensor", "gp", impc_ld)
    bcimp = E.op("tensor", lambda e: e.matmul(
        P[2][:], ones_row, R["mean"], start=True, stop=True), "pe")
    E.wait("scalar", "pe", bcimp)
    impbc = FA[:, 1024:1536]
    onestile = FA[:, 1536:2048]
    cgt = FA[:, 2048:2560]
    ceq = FA[:, 2560:3072]
    ultv = FA[:, 3072:3584]
    impbc_cp = E.op("scalar", lambda e: e.copy(impbc[:], P[2][:]), "ac")
    E.wait("vector", "ac", impbc_cp)
    ones512_v = E.op("vector", lambda e: e.memset(onestile[:], 1.0), "ve")
    E.wait("gpsimd", "ve", ones512_v)
    E.wait("vector", "gp", impc_ld)
    for vt in range(4):
        asel_v = E.op("gpsimd", lambda e, vt=vt: e.affine_select(
            ultv[:], onestile[:], pattern=[[-1, 512]], compare_op=ALU.is_gt,
            fill=0.0, base=128 * vt, channel_multiplier=1), "gc")
        E.op("vector", lambda e, vt=vt: e.tensor_scalar(
            cgt[:], impbc[:], impcols[:, vt:vt + 1], None, ALU.is_gt))
        E.op("vector", lambda e, vt=vt: e.tensor_scalar(
            ceq[:], impbc[:], impcols[:, vt:vt + 1], None, ALU.is_equal))
        E.wait("vector", "gc", asel_v)
        E.op("vector", lambda e: e.tensor_mul(ceq[:], ceq[:], ultv[:]))
        E.op("vector", lambda e: e.tensor_add(cgt[:], cgt[:], ceq[:]))
        v1 = E.op("vector", lambda e: e.reduce_sum(rankw[:, 0:1], cgt[:], axis=AXX), "ve")
        E.wait("vector", "ve", v1)
        v2 = E.op("vector", lambda e: e.tensor_scalar(
            rankw[:, 1:2], rankw[:, 0:1], 255.5, None, ALU.is_lt), "ve")
        E.wait("vector", "ve", v2)
        v3 = E.op("vector", lambda e: e.tensor_scalar(
            rankw[:, 2:3], rankw[:, 1:2], 1e30, -1e30, ALU.mult, ALU.add), "ve")
        E.wait("vector", "ve", v3)
        kb_v = E.op("vector", lambda e, vt=vt: e.tensor_scalar(
            kbias[:, vt:vt + 1], rankw[:, 2:3], SHIFT, None, ALU.add), "ve")
        E.wait("vector", "ve", kb_v)
        E.wait("gpsimd", "ve", kb_v)
    E.dma("gpsimd", kbias_s[0:1, 0:1], m30[0:1, 0:1], "gp")
    E.dma("gpsimd",
          kbias_s[1:513, 0:1].rearrange("(v p) one -> p (v one)", p=128),
          kbias[:, 0:4], "gp", slow=True)
    E.dma("gpsimd", kbias_s[513:640, 0:1], m30[0:127, 0:1], "gp")
    kb_ld = None
    for kt in range(5):
        kb_ld = E.dma("gpsimd", kbias[:, kt:kt + 1],
                      kbias_s[128 * kt:128 * kt + 128, 0:1], "gp")

    def hb(p, i):
        return AB[:, 12288 * p + 2048 * i:12288 * p + 2048 * i + 2048]
    kqv = {nm: [hb(0, i), hb(1, i)] for i, nm in
           enumerate(("kh", "kl", "qh", "ql", "vth", "vtl"))}
    vfT = [FA[:, 9728:11776], FA[:, 11776:13824]]
    phb = AB[:, 32768:40960]
    plb = AB[:, 40960:49152]
    pbuf = FA[:, 0:8192]
    outsb = [FA[:, 8192:8704], FA[:, 8704:9216]]
    rbc = FA[:, 9216:9728]
    rrow = R["b"]

    head_pe_done = {}
    out_store = {}
    prev_onorm = 0
    prev_rbc_cp = 0
    E.wait("scalar", "gp", kb_ld)
    E.wait("sync", "pe", E.val("pe"))
    for h in range(NFT):
        par = h % 2
        hkey = f"h{par}"
        if h >= 2:
            E.wait("sync", "pe", head_pe_done[h - 2])
        for nm, src in (("kh", khT_s), ("kl", klT_s), ("qh", qhT_s), ("ql", qlT_s)):
            E.dma("sync", kqv[nm][par][:], src[128 * h:128 * h + 128, :], hkey)
        ldall = E.dma("sync", vfT[par][:], vT_s[128 * h:128 * h + 128, :], hkey)
        E.wait("tensor", hkey, ldall)
        cp_vals = []
        sub_vals = []
        for kt in range(16):
            pb = P[4 + (kt % 2)]
            if kt >= 2:
                E.wait("tensor", "ac", cp_vals[kt - 2])
                E.wait("tensor", "ve", sub_vals[kt - 2])
            tr = E.op("tensor", lambda e, kt=kt, par=par, pb=pb: e.transpose(
                pb[:, 0:128], vfT[par][:, 128 * kt:128 * kt + 128], ident[:]), "pe")
            E.wait("scalar", "pe", tr)
            cp_vals.append(E.op("scalar", lambda e, kt=kt, par=par, pb=pb: e.copy(
                kqv["vth"][par][:, 128 * kt:128 * kt + 128], pb[:, 0:128]), "ac"))
            E.wait("vector", "ac", cp_vals[-1])
            sub_vals.append(E.op("vector", lambda e, kt=kt, par=par, pb=pb: e.tensor_sub(
                kqv["vtl"][par][:, 128 * kt:128 * kt + 128], pb[:, 0:128],
                kqv["vth"][par][:, 128 * kt:128 * kt + 128]), "ve"))
        vt_ready = cp_vals[-1]
        vt_readyv = sub_vals[-1]
        for j in range(NCH):
            nkt = 4 * (j + 1)
            E.wait("scalar", "gc", E.val("gc"))
            exp_done = {}
            psplit = {}
            for kt in range(nkt):
                sb = P[kt % 2]
                if kt >= 2:
                    E.wait("tensor", "ac", exp_done[kt - 2])
                smm = E.op("tensor", lambda e, kt=kt, j=j, par=par, sb=sb: (
                    e.matmul(sb[:], kqv["kh"][par][:, 128 * kt:128 * kt + 128],
                             kqv["qh"][par][:, CH * j:CH * j + CH],
                             start=True, stop=False),
                    e.matmul(sb[:], kqv["kh"][par][:, 128 * kt:128 * kt + 128],
                             kqv["ql"][par][:, CH * j:CH * j + CH],
                             start=False, stop=False),
                    e.matmul(sb[:], kqv["kl"][par][:, 128 * kt:128 * kt + 128],
                             kqv["qh"][par][:, CH * j:CH * j + CH],
                             start=False, stop=True))[-1], "pe")
                E.wait("scalar", "pe", smm)
                bias = kbias[:, kt:kt + 1] if kt <= 4 else m30[:, 0:1]
                exp_done[kt] = E.op("scalar", lambda e, kt=kt, sb=sb, bias=bias:
                                    e.activation(pbuf[:, CH * kt:CH * kt + CH], sb[:],
                                                 AF.Exp, bias=bias, scale=SCALE), "ac")
            asel = {}
            for kt in range(4 * j, nkt):
                E.wait("gpsimd", "ac", exp_done[kt])
                asel[kt] = E.op("gpsimd", lambda e, kt=kt, j=j: e.affine_select(
                    pbuf[:, CH * kt:CH * kt + CH], pbuf[:, CH * kt:CH * kt + CH],
                    pattern=[[1, 512]], compare_op=ALU.is_ge, fill=0.0,
                    base=CH * j - 128 * kt, channel_multiplier=-1), "gc")
            for kt in range(nkt):
                E.wait("scalar", "ac", exp_done[kt])
                if kt in asel:
                    E.wait("scalar", "gc", asel[kt])
                hv = E.op("scalar", lambda e, kt=kt: e.copy(
                    phb[:, CH * kt:CH * kt + CH], pbuf[:, CH * kt:CH * kt + CH]), "ac")
                E.wait("vector", "ac", hv)
                psplit[kt] = E.op("vector", lambda e, kt=kt: e.tensor_sub(
                    plb[:, CH * kt:CH * kt + CH], pbuf[:, CH * kt:CH * kt + CH],
                    phb[:, CH * kt:CH * kt + CH]), "ve")
            E.wait("tensor", "ac", vt_ready)
            E.wait("tensor", "ve", vt_readyv)
            E.wait("tensor", "ve", prev_onorm)
            pvmm = None
            for kt in range(nkt):
                E.wait("tensor", "ve", psplit[kt])
                E.op("tensor", lambda e, kt=kt, par=par, nkt=nkt: (
                    e.matmul(P[2][:], kqv["vth"][par][:, 128 * kt:128 * kt + 128],
                             phb[:, CH * kt:CH * kt + CH],
                             start=(kt == 0), stop=False),
                    e.matmul(P[2][:], kqv["vth"][par][:, 128 * kt:128 * kt + 128],
                             plb[:, CH * kt:CH * kt + CH],
                             start=False, stop=False),
                    e.matmul(P[2][:], kqv["vtl"][par][:, 128 * kt:128 * kt + 128],
                             phb[:, CH * kt:CH * kt + CH],
                             start=False, stop=(kt == nkt - 1)))[-1])
                pvmm = E.op("tensor", lambda e, kt=kt, nkt=nkt: (
                    e.matmul(P6[:], onesb_col, phb[:, CH * kt:CH * kt + CH],
                             start=(kt == 0), stop=False),
                    e.matmul(P6[:], onesb_col, plb[:, CH * kt:CH * kt + CH],
                             start=False, stop=(kt == nkt - 1)))[-1], "pe")
            E.wait("vector", "pe", pvmm)
            rec_v = E.op("vector", lambda e: e.reciprocal(rrow, P6[:]), "ve")
            E.wait("tensor", "ve", rec_v)
            E.wait("tensor", "ac", prev_rbc_cp)
            bc2 = E.op("tensor", lambda e: e.matmul(
                P[3][:], ones_row, rrow, start=True, stop=True), "pe")
            E.wait("scalar", "pe", bc2)
            rbc_cp = E.op("scalar", lambda e: e.copy(rbc[:], P[3][:]), "ac")
            prev_rbc_cp = rbc_cp
            opar = (h * NCH + j) % 2
            obkey = f"ob{opar}"
            E.wait("vector", "ac", rbc_cp)
            if opar in out_store:
                E.wait("vector", obkey, out_store[opar])
            onorm = E.op("vector", lambda e, opar=opar: e.tensor_mul(
                outsb[opar][:], P[2][:], rbc[:]), "ve")
            prev_onorm = onorm
            E.wait("scalar", "ve", onorm)
            out_store[opar] = E.dma(
                "scalar", ohT_s.ap()[128 * h:128 * h + 128, CH * j:CH * j + CH],
                outsb[opar][:], obkey)
        head_pe_done[h] = E.val("pe")

    b_pe = E.val("pe")

    xnhh = AB[:, 0:16384]
    xnlh = AB[:, 16384:32768]
    wdh = [AB[:, 32768:34816], AB[:, 34816:36864]]
    wdl = [AB[:, 36864:38912], AB[:, 38912:40960]]
    fw = [AI[:, 0:2048], AI[:, 2048:4096]]
    sxp = [FA[:, 0:2048], FA[:, 2048:4096]]
    wdf = [FA[:, 4096:6144], FA[:, 6144:8192]]
    fout = [FA[:, 8192:8704], FA[:, 8704:9216]]
    fpart = [FA[:, 9216:9728], FA[:, 9728:10240]]

    E.wait("vector", "ob0", E.val("ob0"))
    E.wait("vector", "ob1", E.val("ob1"))
    E.wait("gpsimd", "ob0", E.val("ob0"))
    E.wait("gpsimd", "ob1", E.val("ob1"))
    E.wait("gpsimd", "pe", b_pe)
    E.wait("sync", "ac", E.val("ac"))
    E.wait("sync", "ve", E.val("ve"))
    E.wait("sync", "gc", E.val("gc"))
    xn_ld0a = E.dma(
        "gpsimd", xnhh.rearrange("p (i t) -> p i t", t=1024),
        lambda: xnhT_s.ap().rearrange("(i p) t -> p i t", p=128)[
            :, 0:16, DS(pidbox["goff"], 1024)], "gp")
    xn_ld0 = E.dma(
        "gpsimd", xnlh.rearrange("p (i t) -> p i t", t=1024),
        lambda: xnlT_s.ap().rearrange("(i p) t -> p i t", p=128)[
            :, 0:16, DS(pidbox["goff"], 1024)], "gp")
    ag_done = E.op("gpsimd", lambda e: e.collective_compute(
        "AllGather", ALU.bypass, replica_groups=[list(range(8))],
        ins=[ohT_s.ap().opt()], outs=[ag_out.ap().opt()]), "cc")

    wd_mm = {}
    dq_done = {}
    fout_store = {}
    fp_ev = {}
    hC_prev = {}
    di = 0
    for half in range(2):
        if half == 1:
            E.wait("gpsimd", "pe", wd_mm[di - 1])
            E.dma("gpsimd", xnhh.rearrange("p (i t) -> p i t", t=1024),
                  lambda: xnhT_s.ap().rearrange("(i p) t -> p i t", p=128)[
                      :, 16:32, DS(pidbox["goff"], 1024)], "gp")
            xn_ld1 = E.dma(
                "gpsimd", xnlh.rearrange("p (i t) -> p i t", t=1024),
                lambda: xnlT_s.ap().rearrange("(i p) t -> p i t", p=128)[
                    :, 16:32, DS(pidbox["goff"], 1024)], "gp")
            E.wait("sync", "fs0", E.val("fs0"))
            E.wait("sync", "fs1", E.val("fs1"))
        for ot in range(32):
            par = di % 2
            fkey = f"f{par}"
            if di >= 2:
                E.wait("sync", "ve", dq_done[di - 2])
            E.dma("sync", fw[par].rearrange("p (i o) -> p i o", o=128),
                  fwT.rearrange("(i p) o -> p i o", p=128)[
                      :, 16 * half:16 * half + 16, 128 * ot:128 * ot + 128], fkey)
            sxld = E.dma("sync", sxp[par].rearrange("p (i o) -> p i o", o=128),
                         sexpT.rearrange("(i p) o -> p i o", p=128)[
                             :, 16 * half:16 * half + 16, 128 * ot:128 * ot + 128], fkey)
            E.wait("vector", fkey, sxld)
            if di >= 2:
                E.wait("vector", "pe", wd_mm[di - 2])
                E.wait("vector", "ac", hC_prev.get(par, 0))
            E.op("vector", lambda e, par=par: e.tensor_copy(wdf[par][:], fw[par][:]))
            mul_v = E.op("vector", lambda e, par=par: e.tensor_mul(
                wdf[par][:], wdf[par][:], sxp[par][:]), "ve")
            E.wait("scalar", "ve", mul_v)
            if di >= 2:
                E.wait("scalar", "pe", wd_mm[di - 2])
            hC = E.op("scalar", lambda e, par=par: e.copy(wdh[par][:], wdf[par][:]), "ac")
            hC_prev[par] = hC
            E.wait("vector", "ac", hC)
            dq_v = E.op("vector", lambda e, par=par: e.tensor_sub(
                wdl[par][:], wdf[par][:], wdh[par][:]), "ve")
            dq_done[di] = dq_v
            E.wait("tensor", "ve", dq_v)
            if di == 0:
                E.wait("tensor", "gp", xn_ld0)
            if half == 1 and ot == 0:
                E.wait("tensor", "gp", xn_ld1)
            for tc in range(2):
                E.wait("tensor", "ve", fp_ev.get(tc, 0))
            mmv = None
            for i in range(16):
                for tc in range(2):
                    ik = "pe" if (i == 15 and tc == 1) else None
                    mmv = E.op("tensor", lambda e, i=i, tc=tc, par=par: (
                        e.matmul(P[tc][:], wdh[par][:, 128 * i:128 * i + 128],
                                 xnhh[:, 1024 * i + CH * tc:1024 * i + CH * tc + CH],
                                 start=(i == 0), stop=False),
                        e.matmul(P[tc][:], wdh[par][:, 128 * i:128 * i + 128],
                                 xnlh[:, 1024 * i + CH * tc:1024 * i + CH * tc + CH],
                                 start=False, stop=False),
                        e.matmul(P[tc][:], wdl[par][:, 128 * i:128 * i + 128],
                                 xnhh[:, 1024 * i + CH * tc:1024 * i + CH * tc + CH],
                                 start=False, stop=(i == 15)))[-1], ik)
            wd_mm[di] = mmv
            for tc in range(2):
                skey = f"fs{tc}" if half == 0 else f"fs{2 + tc}"
                E.wait("vector", "pe", mmv)
                if half == 0:
                    prev = fout_store.get(tc)
                    if prev is not None:
                        E.wait("vector", prev[0], prev[1])
                    ev = E.op("vector", lambda e, tc=tc: e.tensor_copy(
                        fout[tc][:], P[tc][:]), "ve")
                else:
                    E.wait("sync", "ve", fp_ev.get(tc, 0))
                    pld = E.dma("sync", fpart[tc][:],
                                ffnT_s[128 * ot:128 * ot + 128, CH * tc:CH * tc + CH],
                                f"p{tc}")
                    E.wait("vector", f"p{tc}", pld)
                    prev = fout_store.get(tc)
                    if prev is not None:
                        E.wait("vector", prev[0], prev[1])
                    E.op("vector", lambda e, tc=tc, ot=ot: e.tensor_scalar(
                        fout[tc][:], P[tc][:], fbias_c[:, ot:ot + 1], None, ALU.add))
                    ev = E.op("vector", lambda e, tc=tc: e.tensor_add(
                        fout[tc][:], fout[tc][:], fpart[tc][:]), "ve")
                fp_ev[tc] = ev
                E.wait("scalar", "ve", ev)
                fout_store[tc] = (skey, E.dma(
                    "scalar", ffnT_s[128 * ot:128 * ot + 128, CH * tc:CH * tc + CH],
                    fout[tc][:], skey))
            di += 1
    d1_pe = E.val("pe")

    ohh = AB[:, 0:16384]
    ohl = AB[:, 16384:32768]
    owh = [AB[:, 32768:34816], AB[:, 34816:36864]]
    owl = [AB[:, 36864:38912], AB[:, 38912:40960]]
    ohstrip = FA[:, 0:4096]
    owf = [FA[:, 4096:6144], FA[:, 6144:8192]]
    extr = FA[:, 8192:11264]
    oout = [FA[:, 11264:11776], FA[:, 11776:12288]]

    def extile(kind, par):
        base = 8192 + {"x": 0, "ffn": 1024, "ap": 2048}[kind] + 512 * par
        return FA[:, base:base + 512]

    oslab_done = {}
    osub_done = {}
    oo_store = {}
    oev = {}
    oi = 0
    E.wait("gpsimd", "cc", ag_done)
    E.wait("sync", "ve", E.val("ve"))
    E.wait("sync", "ac", E.val("ac"))
    E.wait("gpsimd", "ve", E.val("ve"))
    E.wait("gpsimd", "ac", E.val("ac"))
    for half in range(2):
        E.wait("gpsimd", "pe", d1_pe if half == 0 else oslab_done[oi - 1])
        strip_done = None
        for kq in range(4):
            sld = E.dma("gpsimd", ohstrip.rearrange("p (f t) -> p f t", t=256),
                        lambda half=half, kq=kq: ag_out.ap()[
                            DS(pidbox[f"rb{half}"], 2048),
                            DS(pidbox[f"go{half}_{kq}"], 256)].rearrange(
                            "(f p) t -> p f t", p=128), "gp")
            E.wait("scalar", "gp", sld)
            if strip_done is not None:
                E.wait("scalar", "ve", strip_done)
            shv = E.op("scalar", lambda e, kq=kq: e.copy(
                ohh.rearrange("p (f t) -> p f t", t=1024)[:, :, 256 * kq:256 * kq + 256],
                ohstrip.rearrange("p (f t) -> p f t", t=256)), "ac")
            E.wait("vector", "ac", shv)
            strip_done = E.op("vector", lambda e, kq=kq: e.tensor_sub(
                ohl.rearrange("p (f t) -> p f t", t=1024)[:, :, 256 * kq:256 * kq + 256],
                ohstrip.rearrange("p (f t) -> p f t", t=256),
                ohh.rearrange("p (f t) -> p f t", t=1024)[:, :, 256 * kq:256 * kq + 256]),
                "ve")
            E.wait("gpsimd", "ve", strip_done)
        oh_ready = strip_done
        if half == 1:
            E.wait("sync", "ap0", E.val("ap0"))
            E.wait("sync", "ap1", E.val("ap1"))
            E.wait("sync", "fs2", E.val("fs2"))
            E.wait("sync", "fs3", E.val("fs3"))
        for ot in range(32):
            par = oi % 2
            okey, gok = f"o{par}", f"go{par}"
            owT_ap = owT.rearrange("(f p) o -> p f o", p=128)[
                :, 16 * half:16 * half + 16, 128 * ot:128 * ot + 128]
            if oi >= 2:
                E.wait("sync", "ve", osub_done[oi - 2])
                E.wait("gpsimd", "pe", oslab_done[oi - 2])
            owld = E.dma("sync", owf[par].rearrange("p (f o) -> p f o", o=128),
                         owT_ap, okey)
            gld = E.dma("gpsimd", owh[par].rearrange("p (f o) -> p f o", o=128),
                        owT_ap, gok)
            E.wait("vector", okey, owld)
            E.wait("vector", gok, gld)
            if oi >= 2:
                E.wait("vector", "pe", oslab_done[oi - 2])
            osub = E.op("vector", lambda e, par=par: e.tensor_sub(
                owl[par][:], owf[par][:], owh[par][:]), "ve")
            osub_done[oi] = osub
            E.wait("tensor", "ve", osub)
            E.wait("tensor", "ve", oh_ready)
            for tc in range(2):
                E.wait("tensor", "ve", oev.get(tc, 0))
            ommv = None
            for f in range(16):
                for tc in range(2):
                    ik = "pe" if (f == 15 and tc == 1) else None
                    ommv = E.op("tensor", lambda e, f=f, tc=tc, par=par: (
                        e.matmul(P[tc][:], owh[par][:, 128 * f:128 * f + 128],
                                 ohh[:, 1024 * f + CH * tc:1024 * f + CH * tc + CH],
                                 start=(f == 0), stop=False),
                        e.matmul(P[tc][:], owh[par][:, 128 * f:128 * f + 128],
                                 ohl[:, 1024 * f + CH * tc:1024 * f + CH * tc + CH],
                                 start=False, stop=False),
                        e.matmul(P[tc][:], owl[par][:, 128 * f:128 * f + 128],
                                 ohh[:, 1024 * f + CH * tc:1024 * f + CH * tc + CH],
                                 start=False, stop=(f == 15)))[-1], ik)
            oslab_done[oi] = ommv
            for tc in range(2):
                E.wait("vector", "pe", ommv)
                prev = oo_store.get(tc)
                if half == 0:
                    if prev is not None:
                        E.wait("vector", prev[0], prev[1])
                    ev = E.op("vector", lambda e, tc=tc: e.tensor_copy(
                        oout[tc][:], P[tc][:]), "ve")
                    oev[tc] = ev
                    E.wait("scalar", "ve", ev)
                    oo_store[tc] = (f"ap{tc}", E.dma(
                        "scalar", apT_s[128 * ot:128 * ot + 128, CH * tc:CH * tc + CH],
                        oout[tc][:], f"ap{tc}"))
                else:
                    ekey = f"e{tc}"
                    E.wait("sync", "ve", oev.get(tc, 0))
                    E.dma("sync", extile("ap", tc)[:],
                          apT_s[128 * ot:128 * ot + 128, CH * tc:CH * tc + CH], ekey)
                    E.dma("sync", extile("ffn", tc)[:],
                          ffnT_s[128 * ot:128 * ot + 128, CH * tc:CH * tc + CH], ekey)
                    xld2 = E.dma("sync", extile("x", tc)[:],
                                 xTo[128 * ot:128 * ot + 128, CH * tc:CH * tc + CH],
                                 ekey)
                    E.wait("vector", ekey, xld2)
                    if prev is not None:
                        E.wait("vector", prev[0], prev[1])
                    E.op("vector", lambda e, tc=tc: e.tensor_add(
                        oout[tc][:], P[tc][:], extile("ap", tc)[:]))
                    E.op("vector", lambda e, tc=tc: e.tensor_add(
                        oout[tc][:], oout[tc][:], extile("x", tc)[:]))
                    ev = E.op("vector", lambda e, tc=tc: e.tensor_add(
                        oout[tc][:], oout[tc][:], extile("ffn", tc)[:]), "ve")
                    oev[tc] = ev
                    E.wait("scalar", "ve", ev)
                    oo_store[tc] = (f"ou{tc}", E.dma(
                        "scalar", outT[128 * ot:128 * ot + 128, CH * tc:CH * tc + CH],
                        oout[tc][:], f"ou{tc}"))
            oi += 1

    E.wait("scalar", "ou0", E.val("ou0"))
    E.wait("scalar", "ou1", E.val("ou1"))
    E.wait("gpsimd", "gp")

    with nc.Block() as block:
        block.sync(lambda e: [f(e) for f in E.ops["sync"]])
        block.scalar(lambda e: [f(e) for f in E.ops["scalar"]])
        block.vector(lambda e: [f(e) for f in E.ops["vector"]])
        block.gpsimd(lambda e: [f(e) for f in E.ops["gpsimd"]])
        block.tensor(lambda e: [f(e) for f in E.ops["tensor"]])

    for k in sem_ctx:
        sem_ctx[k].__exit__(None, None, None)
    return nc


_NC_CACHE = {}


def kernel(L0_attn_hidden_states, L0_attn_q_w, L0_attn_k_w, L0_attn_v_w,
           L0_attn_o_w, L0_ffn_qweight, L0_ffn_scales, L0_ffn_bias):
    import ml_dtypes
    x = np.asarray(L0_attn_hidden_states, dtype=np.float32)
    q_w = np.asarray(L0_attn_q_w, dtype=np.float32)
    k_w = np.asarray(L0_attn_k_w, dtype=np.float32)
    v_w = np.asarray(L0_attn_v_w, dtype=np.float32)
    o_w = np.asarray(L0_attn_o_w, dtype=np.float32)
    qw_i = np.asarray(L0_ffn_qweight, dtype=np.int32)
    scales = np.asarray(L0_ffn_scales, dtype=np.float32)
    fb = np.asarray(L0_ffn_bias, dtype=np.float32)

    owT = np.ascontiguousarray(o_w.T)
    fwT = np.ascontiguousarray(qw_i.T)
    s2 = scales.reshape(HID, 64)
    sexpT = np.ascontiguousarray(np.repeat(s2.T, 64, axis=0))
    fbias = np.ascontiguousarray(fb.reshape(HID, 1))

    inv = 1.0 / (10000.0 ** (np.arange(0, DH, 2, dtype=np.float64) / DH))
    freqs = np.arange(S, dtype=np.float64)[:, None] * inv[None, :]
    emb = np.concatenate([freqs, freqs], axis=-1)
    cosT = np.ascontiguousarray(np.cos(emb).T.astype(np.float32))
    sinT = np.ascontiguousarray(np.sin(emb).T.astype(np.float32))

    ptT = np.zeros((128, 128), dtype=np.float32)
    ptT[np.arange(64), np.arange(64) + 64] = 1.0
    ptT[np.arange(64) + 64, np.arange(64)] = -1.0

    ident = np.eye(128, dtype=np.float32)
    onesm = np.ones((128, 128), dtype=np.float32)
    onesb = np.ones((128, 128), dtype=ml_dtypes.bfloat16)
    identb = np.eye(128, dtype=ml_dtypes.bfloat16)

    in_maps = []
    for c in range(NCORES):
        b, g = c // 2, c % 2
        in_maps.append({
            "xT": np.ascontiguousarray(x[b].T),
            "xTo": np.ascontiguousarray(x[b].T[:, 1024 * g:1024 * g + 1024]),
            "qwT": np.ascontiguousarray(q_w.T[:, 2048 * g:2048 * g + 2048]),
            "kwT": np.ascontiguousarray(k_w.T[:, 2048 * g:2048 * g + 2048]),
            "vwT": np.ascontiguousarray(v_w.T[:, 2048 * g:2048 * g + 2048]),
            "owT": owT, "fwT": fwT, "sexpT": sexpT, "fbias": fbias,
            "cosT": cosT, "sinT": sinT, "ptT": ptT, "ident": ident,
            "onesm": onesm, "onesb": onesb, "identb": identb,
            "impsc": np.full((1, 1), 1.0 if b == 0 else 0.0, dtype=np.float32),
        })

    if "prog" not in _NC_CACHE:
        _NC_CACHE["prog"] = build_program()
    nc = _NC_CACHE["prog"]
    res = run_bass_kernel_spmd(nc, in_maps, core_ids=list(range(NCORES)))

    out = np.empty((B, S, HID), dtype=np.float32)
    for c in range(NCORES):
        b, g = c // 2, c % 2
        out[b, 1024 * g:1024 * g + 1024, :] = res.results[c]["outT"].T
    return out


# version 04
# speedup vs baseline: 1.1024x; 1.0457x over previous
import numpy as np
import concourse.bass as bass
import concourse.mybir as mybir
from concourse.bass_utils import run_bass_kernel_spmd

F32 = mybir.dt.float32
BF16 = mybir.dt.bfloat16
I32 = mybir.dt.int32
AF = mybir.ActivationFunctionType
ALU = mybir.AluOpType
AXX = mybir.AxisListType.X
DS = bass.ds

B, S, H, DH, HID = 4, 2048, 32, 128, 4096
NCORES = 8
SCALE = 1.0 / float(np.sqrt(128.0))
SHIFT = -30.0
EPS = 1e-5
VIS_LO, VIS_HI = 1, 513
NCH = 4
CH = 512
NFT = 16
NCT = 32

DMA_KEYS = ("c", "x0", "x1", "w0", "w1", "gw0", "gw1", "kv0", "kv1",
            "h0", "h1", "f0", "f1", "p0", "p1", "o0", "o1", "go0", "go1",
            "e0", "e1", "xs", "qk0", "qk1", "v0", "v1", "ob0", "ob1",
            "fs0", "fs1", "fs2", "fs3", "ap0", "ap1", "ou0", "ou1", "gp")
ENG_KEYS = ("pe", "ac", "ve", "gc", "cc")


class Emitter:
    def __init__(self, sems, nc=None):
        self.ops = {k: [] for k in ("sync", "scalar", "vector", "gpsimd", "tensor")}
        self.cnt = {}
        self.last_wait = {k: {} for k in self.ops}
        self.sems = sems
        self.nc = nc

    def mark(self, name):
        if self.nc is None:
            return
        for q in self.ops:
            self.ops[q].append(lambda e, n=name, nc=self.nc: (
                nc.leave_named_scope(), nc.enter_named_scope(n))[-1]
                if getattr(nc, "_scope_open", False) else
                (nc.enter_named_scope(n), setattr(nc, "_scope_open", True))[-1])

    def inc(self, key, n):
        self.cnt[key] = self.cnt.get(key, 0) + n
        return self.cnt[key]

    def val(self, key):
        return self.cnt.get(key, 0)

    def wait(self, eng, key, v=None):
        v = self.val(key) if v is None else v
        if not v or v <= 0:
            return
        lw = self.last_wait[eng]
        if lw.get(key, -1) >= v:
            return
        lw[key] = v
        s = self.sems[key]
        self.ops[eng].append(lambda e, s=s, vv=v: e.wait_ge(s, vv))

    def dma(self, q, out_ap, in_ap, key, slow=False):
        prev = self.val(key)
        if prev:
            self.wait(q, key, prev)
        v = self.inc(key, 16)
        s = self.sems[key]

        def run(e, o=out_ap, i=in_ap, s=s, slow=slow):
            if callable(o):
                o = o()
            if callable(i):
                i = i()
            if slow:
                e.dma_start(out=o, in_=i, allow_slow_non_contiguous=True).then_inc(s, 16)
            else:
                e.dma_start(out=o, in_=i).then_inc(s, 16)
        self.ops[q].append(run)
        return v

    def op(self, eng, fn, inc_key=None, inc_n=1):
        if inc_key is None:
            self.ops[eng].append(fn)
            return None
        v = self.inc(inc_key, inc_n)
        s = self.sems[inc_key]
        self.ops[eng].append(lambda e, f=fn, s=s, n=inc_n: f(e).then_inc(s, n))
        return v


def build_program():
    nc = bass.Bass("TRN2", target_bir_lowering=False, detect_race_conditions=False)

    xT = nc.dram_tensor("xT", [HID, S], F32, kind="ExternalInput").ap()
    qwT = nc.dram_tensor("qwT", [HID, 2048], F32, kind="ExternalInput").ap()
    kwT = nc.dram_tensor("kwT", [HID, 2048], F32, kind="ExternalInput").ap()
    vwT = nc.dram_tensor("vwT", [HID, 2048], F32, kind="ExternalInput").ap()
    owT = nc.dram_tensor("owT", [HID, HID], F32, kind="ExternalInput").ap()
    fwT = nc.dram_tensor("fwT", [HID, HID], I32, kind="ExternalInput").ap()
    sexpT = nc.dram_tensor("sexpT", [HID, HID], F32, kind="ExternalInput").ap()
    fbias = nc.dram_tensor("fbias", [HID, 1], F32, kind="ExternalInput").ap()
    cosT_d = nc.dram_tensor("cosT", [128, S], F32, kind="ExternalInput").ap()
    sinT_d = nc.dram_tensor("sinT", [128, S], F32, kind="ExternalInput").ap()
    ptT_d = nc.dram_tensor("ptT", [128, 128], F32, kind="ExternalInput").ap()
    ident_d = nc.dram_tensor("ident", [128, 128], F32, kind="ExternalInput").ap()
    ones_d = nc.dram_tensor("onesm", [128, 128], F32, kind="ExternalInput").ap()
    onesb_d = nc.dram_tensor("onesb", [128, 128], BF16, kind="ExternalInput").ap()
    identb_d = nc.dram_tensor("identb", [128, 128], BF16, kind="ExternalInput").ap()
    impsc_d = nc.dram_tensor("impsc", [1, 1], F32, kind="ExternalInput").ap()
    xTo = nc.dram_tensor("xTo", [HID, 1024], F32, kind="ExternalInput").ap()
    outT = nc.dram_tensor("outT", [HID, 1024], F32, kind="ExternalOutput").ap()

    qhT_s = nc.dram_tensor("qhT_s", [2048, S], BF16).ap()
    qlT_s = nc.dram_tensor("qlT_s", [2048, S], BF16).ap()
    khT_s = nc.dram_tensor("khT_s", [2048, S], BF16).ap()
    klT_s = nc.dram_tensor("klT_s", [2048, S], BF16).ap()
    vT_s = nc.dram_tensor("vT_s", [2048, S], F32).ap()
    xnhT_s = nc.dram_tensor("xnhT_s", [HID, S], BF16)
    xnlT_s = nc.dram_tensor("xnlT_s", [HID, S], BF16)
    oh0_s = nc.dram_tensor("oh0_s", [1024, S], F32)
    oh1_s = nc.dram_tensor("oh1_s", [1024, S], F32)
    ag0_out = nc.dram_tensor("ag0_out", [8 * 1024, S], F32)
    ag1_out = nc.dram_tensor("ag1_out", [8 * 1024, S], F32)
    ffnT_s = nc.dram_tensor("ffnT_s", [HID, 1024], F32).ap()
    apT_s = nc.dram_tensor("apT_s", [HID, 1024], F32).ap()
    ar_in = nc.dram_tensor("ar_in", [1, 512], F32)
    ar_out = nc.dram_tensor("ar_out", [1, 512], F32)
    kbias_s = nc.dram_tensor("kbias_s", [640, 1], F32).ap()

    FA = nc.alloc_sbuf_tensor("FA", [128, 13824], F32).ap()
    AB = nc.alloc_sbuf_tensor("AB", [128, 53248], BF16).ap()
    AI = nc.alloc_sbuf_tensor("AI", [128, 4096], I32).ap()
    ROWS = nc.alloc_sbuf_tensor("ROWS", [1, 3072], F32).ap()
    cosT = nc.alloc_sbuf_tensor("cosT_sb", [128, S], F32).ap()
    sinT = nc.alloc_sbuf_tensor("sinT_sb", [128, S], F32).ap()
    ptT = nc.alloc_sbuf_tensor("ptT_sb", [128, 128], F32).ap()
    ident = nc.alloc_sbuf_tensor("ident_sb", [128, 128], F32).ap()
    ones = nc.alloc_sbuf_tensor("ones_sb", [128, 128], F32).ap()
    onesb = nc.alloc_sbuf_tensor("onesb_sb", [128, 128], BF16).ap()
    identb = nc.alloc_sbuf_tensor("identb_sb", [128, 128], BF16).ap()
    sumq = nc.alloc_sbuf_tensor("sumq_sb", [128, 16], F32).ap()
    sumqh = nc.alloc_sbuf_tensor("sumqh_sb", [128, 16], BF16).ap()
    sumql = nc.alloc_sbuf_tensor("sumql_sb", [128, 16], BF16).ap()
    sumq_t = nc.alloc_sbuf_tensor("sumq_t", [128, 1], F32).ap()
    kbias = nc.alloc_sbuf_tensor("kbias_sb", [128, 8], F32).ap()
    m30 = nc.alloc_sbuf_tensor("m30_sb", [128, 1], F32).ap()
    impsc = nc.alloc_sbuf_tensor("impsc_sb", [1, 1], F32).ap()
    impcols = nc.alloc_sbuf_tensor("impcols_sb", [128, 4], F32).ap()
    fbias_c = nc.alloc_sbuf_tensor("fbias_sb", [128, 32], F32).ap()
    rankw = nc.alloc_sbuf_tensor("rankw_sb", [128, 8], F32).ap()

    P = [nc.alloc_psum_tensor(f"P{i}", [128, 512], F32).ap() for i in range(8)]
    P6 = P[6][0:1, :]
    P7 = P[7][0:1, :]

    sems, sem_ctx = {}, {}
    for k in DMA_KEYS + ENG_KEYS:
        g = nc.semaphore(k)
        sems[k] = g.__enter__()
        sem_ctx[k] = g
    E = Emitter(sems)

    R = {name: ROWS[0:1, 512 * i:512 * i + 512]
         for i, name in enumerate(("mean", "vpe", "b", "ss", "rstd", "negmr"))}

    ones_col = ones[:, 0:1]
    ones_row = ones[0:1, :]
    onesb_col = onesb[:, 0:1]

    E.dma("sync", cosT[:], cosT_d, "c")
    E.dma("sync", sinT[:], sinT_d, "c")
    E.dma("sync", ptT[:], ptT_d, "c")
    E.dma("sync", ident[:], ident_d, "c")
    E.dma("sync", ones[:], ones_d, "c")
    E.dma("sync", onesb[:], onesb_d, "c")
    E.dma("sync", identb[:], identb_d, "c")
    E.dma("sync", impsc[:], impsc_d, "c")
    consts_ld = E.dma("sync", fbias_c[:],
                      fbias[:, 0:1].rearrange("(o p) one -> p (o one)", p=128), "c",
                      slow=True)
    E.wait("vector", "c", consts_ld)
    E.op("vector", lambda e: e.memset(sumq[:], 0.0))
    E.op("vector", lambda e: e.memset(m30[:], SHIFT), "ve")

    pidbox = {}

    def gp_init(e):
        pid = nc.partition_id([mybir.EngineType.Pool])
        g = pid % 2
        pidbox["goff"] = g * 1024
        for half in range(2):
            pidbox[f"rbA{half}"] = (pid - g + half) * 1024
            for kq in range(4):
                pidbox[f"go{half}_{kq}"] = g * 1024 + 256 * kq
        return e.memset(rankw[:, 7:8], 0.0)
    E.op("gpsimd", gp_init, "gc")

    xtile = [FA[:, 0:512], FA[:, 512:1024]]
    wslab = [FA[:, 1024:5120], FA[:, 1024:5120]]
    vout = [FA[:, 5120:5632], FA[:, 5632:6144]]
    traw = [FA[:, 6144:6656], FA[:, 6656:7168]]
    t1b = FA[:, 7168:7680]
    t2b = FA[:, 7680:8192]
    qrout = [FA[:, 8192:8704], FA[:, 8704:9216]]
    xsq = [FA[:, 9216:9728], FA[:, 9728:10240]]
    bc_rstd = FA[:, 10240:10752]
    bc_negmr = FA[:, 10752:11264]

    xnh = AB[:, 0:16384]
    xnl = AB[:, 16384:32768]
    wsh = [AB[:, 32768:36864], AB[:, 36864:40960]]
    wsl = [AB[:, 40960:45056], AB[:, 45056:49152]]
    qrh = [AB[:, 49152:49664], AB[:, 49664:50176]]
    qrl = [AB[:, 50176:50688], AB[:, 50688:51200]]

    slab_mm = {}
    slab_sub = {}
    qr_store = {}
    v_store = {}
    evac_ac = {}
    evac_ve = {}
    traw_free = {}
    rot_free = {}
    pending_rot = []
    prev_nm_v = 0
    prev_bc_cp = 0
    gi = 0
    xn_split_done = 0

    def flush_rot(item):
        (rgi, rj, rft, rqp, rev, rdsth, rdstl) = item
        rp = rgi % 2
        qkey = f"qk{rqp}"
        E.wait("tensor", "ac", rev)
        E.wait("tensor", "ve", rot_free.get(rp, 0))
        rot = E.op("tensor", lambda e, rqp=rqp, rp=rp: e.matmul(
            P[2 + rp][:], ptT[:], traw[rqp][:], start=True, stop=True), "pe")
        E.wait("vector", "pe", rot)
        if rqp in qr_store:
            E.wait("vector", qr_store[rqp][0], qr_store[rqp][1])
        tv = E.op("vector", lambda e, rqp=rqp, rj=rj: e.tensor_mul(
            t1b[:], traw[rqp][:], cosT[:, CH * rj:CH * rj + CH]), "ve")
        traw_free[rqp] = tv
        rused = E.op("vector", lambda e, rp=rp, rj=rj: e.tensor_mul(
            t2b[:], P[2 + rp][:], sinT[:, CH * rj:CH * rj + CH]), "ve")
        rot_free[rp] = rused
        rope_v = E.op("vector", lambda e, rqp=rqp: e.tensor_add(
            qrout[rqp][:], t1b[:], t2b[:]), "ve")
        if rdstl is qlT_s and rj >= 1:
            lo = {1: 1, 2: 0, 3: 0}[rj]
            hi = {1: 512, 2: 512, 3: 456}[rj]
            rs_v = E.op("vector", lambda e, rqp=rqp, lo=lo, hi=hi: e.reduce_sum(
                sumq_t[:], qrout[rqp][:, lo:hi], axis=AXX), "ve")
            E.wait("vector", "ve", rs_v)
            acc_v = E.op("vector", lambda e, rft=rft: e.tensor_add(
                sumq[:, rft:rft + 1], sumq[:, rft:rft + 1], sumq_t[:]), "ve")
            E.wait("vector", "ve", acc_v)
        E.wait("scalar", "ve", rope_v)
        qh_v = E.op("scalar", lambda e, rqp=rqp: e.copy(
            qrh[rqp][:], qrout[rqp][:]), "ac")
        E.wait("vector", "ac", qh_v)
        ql_v = E.op("vector", lambda e, rqp=rqp: e.tensor_sub(
            qrl[rqp][:], qrout[rqp][:], qrh[rqp][:]), "ve")
        E.wait("scalar", "ve", ql_v)
        E.dma("scalar", rdsth[128 * rft:128 * rft + 128, CH * rj:CH * rj + CH],
              qrh[rqp][:], qkey)
        st2 = E.dma("scalar", rdstl[128 * rft:128 * rft + 128, CH * rj:CH * rj + CH],
                    qrl[rqp][:], qkey)
        qr_store[rqp] = (qkey, st2)

    for j in range(NCH):
        E.wait("tensor", "ve", prev_nm_v)
        sq_vals, ssq_vals = [], []
        for c in range(NCT):
            par = c % 2
            xk = f"x{par}"
            if c >= 2:
                E.wait("sync", "pe", ssq_vals[c - 2])
            xld = E.dma("sync", xtile[par][:],
                        xT.rearrange("(c p) t -> p c t", p=128)[
                            :, c, CH * j:CH * j + CH], xk)
            E.wait("scalar", xk, xld)
            if c >= 2:
                E.wait("scalar", "pe", ssq_vals[c - 2])
            sq_vals.append(E.op("scalar", lambda e, par=par: e.activation(
                xsq[par][:], xtile[par][:], AF.Square), "ac"))
            E.wait("tensor", xk, xld)
            E.op("tensor", lambda e, par=par, c=c: e.matmul(
                P6[:], ones_col, xtile[par][:],
                start=(c == 0), stop=(c == NCT - 1)))
            E.wait("tensor", "ac", sq_vals[c])
            ssq_vals.append(E.op("tensor", lambda e, par=par, c=c: e.matmul(
                P7[:], ones_col, xsq[par][:],
                start=(c == 0), stop=(c == NCT - 1)), "pe"))
        E.wait("vector", "pe", ssq_vals[-1])
        E.op("vector", lambda e: e.tensor_scalar(R["mean"], P6[:], 1.0 / HID, None, ALU.mult))
        E.op("vector", lambda e: e.tensor_scalar(R["vpe"], P7[:], 1.0 / HID, None, ALU.mult))
        E.op("vector", lambda e: e.tensor_mul(R["b"], R["mean"], R["mean"]))
        E.op("vector", lambda e: e.scalar_tensor_tensor(
            R["vpe"], R["vpe"], EPS, R["b"], ALU.add, ALU.subtract))
        rr_v = E.op("vector", lambda e: e.reciprocal(R["b"], R["vpe"]), "ve")
        E.wait("scalar", "ve", rr_v)
        ss_v = E.op("scalar", lambda e: e.activation(R["ss"], R["b"], AF.Sqrt), "ac")
        E.wait("vector", "ac", ss_v)
        E.op("vector", lambda e: e.tensor_mul(R["b"], R["ss"], R["ss"]))
        E.op("vector", lambda e: e.tensor_mul(R["b"], R["b"], R["vpe"]))
        E.op("vector", lambda e: e.tensor_scalar(R["b"], R["b"], -0.5, 1.5, ALU.mult, ALU.add))
        E.op("vector", lambda e: e.tensor_mul(R["rstd"], R["ss"], R["b"]))
        E.op("vector", lambda e: e.tensor_mul(R["negmr"], R["mean"], R["rstd"]))
        nm_v = E.op("vector", lambda e: e.tensor_scalar(
            R["negmr"], R["negmr"], -1.0, None, ALU.mult), "ve")
        prev_nm_v = nm_v
        E.wait("tensor", "ve", nm_v)
        E.wait("tensor", "ac", prev_bc_cp)
        E.wait("tensor", "ve", max(rot_free.get(0, 0), rot_free.get(1, 0)))
        bc_mm = E.op("tensor", lambda e: (
            e.matmul(P[2][:], ones_row, R["rstd"], start=True, stop=True),
            e.matmul(P[3][:], ones_row, R["negmr"], start=True, stop=True))[-1], "pe")
        E.wait("scalar", "pe", bc_mm)
        E.op("scalar", lambda e: e.copy(bc_rstd[:], P[2][:]))
        bc_cp = E.op("scalar", lambda e: e.copy(bc_negmr[:], P[3][:]), "ac")
        prev_bc_cp = bc_cp
        E.wait("vector", "ac", bc_cp)

        E.wait("sync", "pe", ssq_vals[-1])
        pe_cum = E.val("pe")
        E.wait("scalar", "pe", pe_cum)
        E.wait("vector", "pe", pe_cum)
        E.wait("scalar", "xs", E.val("xs"))
        E.wait("vector", "xs", E.val("xs"))
        xn_sub = None
        xn_subs = []
        for c in range(NCT):
            par = c % 2
            xk = f"x{par}"
            if c >= 2:
                E.wait("sync", "ve", xn_subs[c - 2])
            xld = E.dma("sync", xtile[par][:],
                        xT.rearrange("(c p) t -> p c t", p=128)[
                            :, c, CH * j:CH * j + CH], xk)
            E.wait("vector", xk, xld)
            E.op("vector", lambda e, par=par: e.tensor_mul(
                xtile[par][:], xtile[par][:], bc_rstd[:]))
            aff_v = E.op("vector", lambda e, par=par: e.tensor_add(
                xtile[par][:], xtile[par][:], bc_negmr[:]), "ve")
            E.wait("scalar", "ve", aff_v)
            hc_v = E.op("scalar", lambda e, par=par, c=c: e.copy(
                xnh[:, CH * c:CH * c + CH], xtile[par][:]), "ac")
            E.wait("vector", "ac", hc_v)
            xn_sub = E.op("vector", lambda e, par=par, c=c: e.tensor_sub(
                xnl[:, CH * c:CH * c + CH], xtile[par][:],
                xnh[:, CH * c:CH * c + CH]), "ve")
            xn_subs.append(xn_sub)
        xn_split_done = xn_sub

        E.wait("scalar", "ve", xn_split_done)
        E.dma("scalar",
              xnhT_s.ap().rearrange("(c p) t -> p c t", p=128)[:, :, CH * j:CH * j + CH],
              xnh.rearrange("p (c t) -> p c t", t=CH), "xs")
        xs_st = E.dma("scalar",
                      xnlT_s.ap().rearrange("(c p) t -> p c t", p=128)[:, :, CH * j:CH * j + CH],
                      xnl.rearrange("p (c t) -> p c t", t=CH), "xs")

        for proj, wsrc, dsth, dstl in (("q", qwT, qhT_s, qlT_s),
                                       ("k", kwT, khT_s, klT_s),
                                       ("v", vwT, vT_s, None)):
            for ft in range(NFT):
                par = gi % 2
                wk, gk = f"w{par}", f"gw{par}"
                wsrc_ap = wsrc.rearrange("(c p) f -> p c f", p=128)[
                    :, :, 128 * ft:128 * ft + 128]
                if gi >= 1:
                    E.wait("sync", "ve", slab_sub.get(gi - 1, 0))
                if gi >= 2:
                    E.wait("gpsimd", "pe", slab_mm.get(gi - 2, 0))
                wld = E.dma("sync", wslab[par].rearrange("p (c f) -> p c f", f=128),
                            wsrc_ap, wk)
                gld = E.dma("gpsimd", wsh[par].rearrange("p (c f) -> p c f", f=128),
                            wsrc_ap, gk)
                E.wait("vector", wk, wld)
                E.wait("vector", gk, gld)
                E.wait("vector", "pe", slab_mm.get(gi - 2, 0))
                dq_v = E.op("vector", lambda e, par=par: e.tensor_sub(
                    wsl[par][:], wslab[par][:], wsh[par][:]), "ve")
                slab_sub[gi] = dq_v
                E.wait("tensor", "ve", dq_v)
                E.wait("tensor", "ve", xn_split_done)
                E.wait("tensor", "ac", evac_ac.get(gi - 2, 0))
                pv = None
                for c in range(NCT):
                    ik = "pe" if c == NCT - 1 else None
                    pv = E.op("tensor", lambda e, c=c, par=par: (
                        e.matmul(P[par][:], wsh[par][:, 128 * c:128 * c + 128],
                                 xnh[:, CH * c:CH * c + CH],
                                 start=(c == 0), stop=False),
                        e.matmul(P[par][:], wsh[par][:, 128 * c:128 * c + 128],
                                 xnl[:, CH * c:CH * c + CH],
                                 start=False, stop=False),
                        e.matmul(P[par][:], wsl[par][:, 128 * c:128 * c + 128],
                                 xnh[:, CH * c:CH * c + CH],
                                 start=False, stop=(c == NCT - 1)))[-1], ik)
                slab_mm[gi] = pv
                if proj == "v":
                    vp = gi % 2
                    vkey = f"v{vp}"
                    E.wait("scalar", "pe", pv)
                    if vp in v_store:
                        E.wait("scalar", vkey, v_store[vp])
                    hv = E.op("scalar", lambda e, vp=vp, par=par: e.copy(
                        vout[vp][:], P[par][:]), "ac")
                    evac_ac[gi] = hv
                    v_store[vp] = E.dma(
                        "scalar", dsth[128 * ft:128 * ft + 128, CH * j:CH * j + CH],
                        vout[vp][:], vkey)
                else:
                    qp = gi % 2
                    E.wait("scalar", "pe", pv)
                    E.wait("scalar", "ve", traw_free.get(qp, 0))
                    ev = E.op("scalar", lambda e, qp=qp, par=par: e.copy(
                        traw[qp][:], P[par][:]), "ac")
                    evac_ac[gi] = ev
                    pending_rot.append((gi, j, ft, qp, ev, dsth, dstl))
                if len(pending_rot) > 1:
                    flush_rot(pending_rot.pop(0))
                gi += 1
        while pending_rot:
            flush_rot(pending_rot.pop(0))

    kvh = [AB[:, 32768 + i * 512:32768 + i * 512 + 512] for i in range(2)]
    kvl = [AB[:, 33792 + i * 512:33792 + i * 512 + 512] for i in range(2)]
    for kk in ("qk0", "qk1", "v0", "v1", "xs"):
        E.wait("sync", kk, E.val(kk))
    E.wait("sync", "pe", E.val("pe"))
    sq_all = E.val("ve")
    E.wait("scalar", "ve", sq_all)
    sqh_v = E.op("scalar", lambda e: e.copy(sumqh[:], sumq[:]), "ac")
    E.wait("vector", "ac", sqh_v)
    sql_v = E.op("vector", lambda e: e.tensor_sub(sumql[:], sumq[:], sumqh[:]), "ve")
    E.wait("vector", "ve", sql_v)
    imp_vals = []
    for h in range(NFT):
        kvk = f"kv{h % 2}"
        if h >= 2:
            E.wait("sync", "pe", imp_vals[h - 2])
        E.dma("sync", kvh[h % 2][:], khT_s[128 * h:128 * h + 128, VIS_LO:VIS_HI], kvk)
        kvld = E.dma("sync", kvl[h % 2][:], klT_s[128 * h:128 * h + 128, VIS_LO:VIS_HI], kvk)
        E.wait("tensor", kvk, kvld)
        E.wait("tensor", "ve", sql_v)
        imp_vals.append(E.op("tensor", lambda e, h=h: (
            e.matmul(P6[:], sumqh[:, h:h + 1], kvh[h % 2][:],
                     start=(h == 0), stop=False),
            e.matmul(P6[:], sumqh[:, h:h + 1], kvl[h % 2][:],
                     start=False, stop=False),
            e.matmul(P6[:], sumql[:, h:h + 1], kvh[h % 2][:],
                     start=False, stop=(h == NFT - 1)))[-1], "pe"))
    E.wait("vector", "pe", imp_vals[-1])
    imp_v = E.op("vector", lambda e: e.tensor_scalar(
        R["mean"], P6[:], impsc[0:1, 0:1], None, ALU.mult), "ve")
    E.wait("gpsimd", "ve", imp_v)
    E.dma("gpsimd", ar_in.ap(), R["mean"], "gp")
    E.wait("gpsimd", "gp")
    ar_done = E.op("gpsimd", lambda e: e.collective_compute(
        "AllReduce", ALU.add, replica_groups=[list(range(8))],
        ins=[ar_in.ap().opt()], outs=[ar_out.ap().opt()]), "cc")
    E.wait("gpsimd", "cc", ar_done)
    impr_ld = E.dma("gpsimd", R["mean"], ar_out.ap(), "gp")
    impc_ld = E.dma("gpsimd", impcols[:],
                    ar_out.ap().rearrange("one (v p) -> (one p) v", p=128), "gp",
                    slow=True)
    E.wait("tensor", "gp", impc_ld)
    bcimp = E.op("tensor", lambda e: e.matmul(
        P[2][:], ones_row, R["mean"], start=True, stop=True), "pe")
    E.wait("scalar", "pe", bcimp)
    impbc = FA[:, 1024:1536]
    onestile = FA[:, 1536:2048]
    cgt = FA[:, 2048:2560]
    ceq = FA[:, 2560:3072]
    ultv = FA[:, 3072:3584]
    impbc_cp = E.op("scalar", lambda e: e.copy(impbc[:], P[2][:]), "ac")
    E.wait("vector", "ac", impbc_cp)
    ones512_v = E.op("vector", lambda e: e.memset(onestile[:], 1.0), "ve")
    E.wait("gpsimd", "ve", ones512_v)
    E.wait("vector", "gp", impc_ld)
    for vt in range(4):
        asel_v = E.op("gpsimd", lambda e, vt=vt: e.affine_select(
            ultv[:], onestile[:], pattern=[[-1, 512]], compare_op=ALU.is_gt,
            fill=0.0, base=128 * vt, channel_multiplier=1), "gc")
        E.op("vector", lambda e, vt=vt: e.tensor_scalar(
            cgt[:], impbc[:], impcols[:, vt:vt + 1], None, ALU.is_gt))
        E.op("vector", lambda e, vt=vt: e.tensor_scalar(
            ceq[:], impbc[:], impcols[:, vt:vt + 1], None, ALU.is_equal))
        E.wait("vector", "gc", asel_v)
        E.op("vector", lambda e: e.tensor_mul(ceq[:], ceq[:], ultv[:]))
        E.op("vector", lambda e: e.tensor_add(cgt[:], cgt[:], ceq[:]))
        v1 = E.op("vector", lambda e: e.reduce_sum(rankw[:, 0:1], cgt[:], axis=AXX), "ve")
        E.wait("vector", "ve", v1)
        v2 = E.op("vector", lambda e: e.tensor_scalar(
            rankw[:, 1:2], rankw[:, 0:1], 255.5, None, ALU.is_lt), "ve")
        E.wait("vector", "ve", v2)
        v3 = E.op("vector", lambda e: e.tensor_scalar(
            rankw[:, 2:3], rankw[:, 1:2], 1e30, -1e30, ALU.mult, ALU.add), "ve")
        E.wait("vector", "ve", v3)
        kb_v = E.op("vector", lambda e, vt=vt: e.tensor_scalar(
            kbias[:, vt:vt + 1], rankw[:, 2:3], SHIFT, None, ALU.add), "ve")
        E.wait("vector", "ve", kb_v)
        E.wait("gpsimd", "ve", kb_v)
    E.dma("gpsimd", kbias_s[0:1, 0:1], m30[0:1, 0:1], "gp")
    E.dma("gpsimd",
          kbias_s[1:513, 0:1].rearrange("(v p) one -> p (v one)", p=128),
          kbias[:, 0:4], "gp", slow=True)
    E.dma("gpsimd", kbias_s[513:640, 0:1], m30[0:127, 0:1], "gp")
    kb_ld = None
    for kt in range(5):
        kb_ld = E.dma("gpsimd", kbias[:, kt:kt + 1],
                      kbias_s[128 * kt:128 * kt + 128, 0:1], "gp")

    def hb(p, i):
        return AB[:, 12288 * p + 2048 * i:12288 * p + 2048 * i + 2048]
    kqv = {nm: [hb(0, i), hb(1, i)] for i, nm in
           enumerate(("kh", "kl", "qh", "ql", "vth", "vtl"))}
    vfT = [FA[:, 9728:11776], FA[:, 11776:13824]]
    phb = AB[:, 32768:40960]
    plb = AB[:, 40960:49152]
    pbuf = FA[:, 0:8192]
    outsb = [FA[:, 8192:8704], FA[:, 8704:9216]]
    rbc = FA[:, 9216:9728]
    rrow = R["b"]

    head_pe_done = {}
    out_store = {}
    prev_onorm = 0
    prev_rbc_cp = 0
    E.wait("scalar", "gp", kb_ld)
    E.wait("sync", "pe", E.val("pe"))
    for h in range(NFT):
        par = h % 2
        hkey = f"h{par}"
        if h >= 2:
            E.wait("sync", "pe", head_pe_done[h - 2])
        for nm, src in (("kh", khT_s), ("kl", klT_s), ("qh", qhT_s), ("ql", qlT_s)):
            E.dma("sync", kqv[nm][par][:], src[128 * h:128 * h + 128, :], hkey)
        ldall = E.dma("sync", vfT[par][:], vT_s[128 * h:128 * h + 128, :], hkey)
        E.wait("tensor", hkey, ldall)
        cp_vals = []
        sub_vals = []
        for kt in range(16):
            pb = P[4 + (kt % 2)]
            if kt >= 2:
                E.wait("tensor", "ac", cp_vals[kt - 2])
                E.wait("tensor", "ve", sub_vals[kt - 2])
            tr = E.op("tensor", lambda e, kt=kt, par=par, pb=pb: e.transpose(
                pb[:, 0:128], vfT[par][:, 128 * kt:128 * kt + 128], ident[:]), "pe")
            E.wait("scalar", "pe", tr)
            cp_vals.append(E.op("scalar", lambda e, kt=kt, par=par, pb=pb: e.copy(
                kqv["vth"][par][:, 128 * kt:128 * kt + 128], pb[:, 0:128]), "ac"))
            E.wait("vector", "ac", cp_vals[-1])
            sub_vals.append(E.op("vector", lambda e, kt=kt, par=par, pb=pb: e.tensor_sub(
                kqv["vtl"][par][:, 128 * kt:128 * kt + 128], pb[:, 0:128],
                kqv["vth"][par][:, 128 * kt:128 * kt + 128]), "ve"))
        vt_ready = cp_vals[-1]
        vt_readyv = sub_vals[-1]
        for j in range(NCH):
            nkt = 4 * (j + 1)
            E.wait("scalar", "gc", E.val("gc"))
            exp_done = {}
            psplit = {}
            for kt in range(nkt):
                sb = [P[0], P[1], P[7]][kt % 3]
                if kt >= 3:
                    E.wait("tensor", "ac", exp_done[kt - 3])
                smm = E.op("tensor", lambda e, kt=kt, j=j, par=par, sb=sb: (
                    e.matmul(sb[:], kqv["kh"][par][:, 128 * kt:128 * kt + 128],
                             kqv["qh"][par][:, CH * j:CH * j + CH],
                             start=True, stop=False),
                    e.matmul(sb[:], kqv["kh"][par][:, 128 * kt:128 * kt + 128],
                             kqv["ql"][par][:, CH * j:CH * j + CH],
                             start=False, stop=False),
                    e.matmul(sb[:], kqv["kl"][par][:, 128 * kt:128 * kt + 128],
                             kqv["qh"][par][:, CH * j:CH * j + CH],
                             start=False, stop=True))[-1], "pe")
                E.wait("scalar", "pe", smm)
                bias = kbias[:, kt:kt + 1] if kt <= 4 else m30[:, 0:1]
                exp_done[kt] = E.op("scalar", lambda e, kt=kt, sb=sb, bias=bias:
                                    e.activation(pbuf[:, CH * kt:CH * kt + CH], sb[:],
                                                 AF.Exp, bias=bias, scale=SCALE), "ac")
            asel = {}
            for kt in range(4 * j, nkt):
                E.wait("gpsimd", "ac", exp_done[kt])
                asel[kt] = E.op("gpsimd", lambda e, kt=kt, j=j: e.affine_select(
                    pbuf[:, CH * kt:CH * kt + CH], pbuf[:, CH * kt:CH * kt + CH],
                    pattern=[[1, 512]], compare_op=ALU.is_ge, fill=0.0,
                    base=CH * j - 128 * kt, channel_multiplier=-1), "gc")
            for kt in range(nkt):
                E.wait("scalar", "ac", exp_done[kt])
                if kt in asel:
                    E.wait("scalar", "gc", asel[kt])
                hv = E.op("scalar", lambda e, kt=kt: e.copy(
                    phb[:, CH * kt:CH * kt + CH], pbuf[:, CH * kt:CH * kt + CH]), "ac")
                E.wait("vector", "ac", hv)
                psplit[kt] = E.op("vector", lambda e, kt=kt: e.tensor_sub(
                    plb[:, CH * kt:CH * kt + CH], pbuf[:, CH * kt:CH * kt + CH],
                    phb[:, CH * kt:CH * kt + CH]), "ve")
            E.wait("tensor", "ac", vt_ready)
            E.wait("tensor", "ve", vt_readyv)
            E.wait("tensor", "ve", prev_onorm)
            pvmm = None
            for kt in range(nkt):
                E.wait("tensor", "ve", psplit[kt])
                E.op("tensor", lambda e, kt=kt, par=par, nkt=nkt: (
                    e.matmul(P[2][:], kqv["vth"][par][:, 128 * kt:128 * kt + 128],
                             phb[:, CH * kt:CH * kt + CH],
                             start=(kt == 0), stop=False),
                    e.matmul(P[2][:], kqv["vth"][par][:, 128 * kt:128 * kt + 128],
                             plb[:, CH * kt:CH * kt + CH],
                             start=False, stop=False),
                    e.matmul(P[2][:], kqv["vtl"][par][:, 128 * kt:128 * kt + 128],
                             phb[:, CH * kt:CH * kt + CH],
                             start=False, stop=(kt == nkt - 1)))[-1])
                pvmm = E.op("tensor", lambda e, kt=kt, nkt=nkt: (
                    e.matmul(P6[:], onesb_col, phb[:, CH * kt:CH * kt + CH],
                             start=(kt == 0), stop=False),
                    e.matmul(P6[:], onesb_col, plb[:, CH * kt:CH * kt + CH],
                             start=False, stop=(kt == nkt - 1)))[-1], "pe")
            E.wait("vector", "pe", pvmm)
            rec_v = E.op("vector", lambda e: e.reciprocal(rrow, P6[:]), "ve")
            E.wait("tensor", "ve", rec_v)
            E.wait("tensor", "ac", prev_rbc_cp)
            bc2 = E.op("tensor", lambda e: e.matmul(
                P[3][:], ones_row, rrow, start=True, stop=True), "pe")
            E.wait("scalar", "pe", bc2)
            rbc_cp = E.op("scalar", lambda e: e.copy(rbc[:], P[3][:]), "ac")
            prev_rbc_cp = rbc_cp
            opar = (h * NCH + j) % 2
            obkey = f"ob{opar}"
            E.wait("vector", "ac", rbc_cp)
            if opar in out_store:
                E.wait("vector", obkey, out_store[opar])
            onorm = E.op("vector", lambda e, opar=opar: e.tensor_mul(
                outsb[opar][:], P[2][:], rbc[:]), "ve")
            prev_onorm = onorm
            E.wait("scalar", "ve", onorm)
            ohdst = oh0_s if h < 8 else oh1_s
            out_store[opar] = E.dma(
                "scalar", ohdst.ap()[128 * (h % 8):128 * (h % 8) + 128,
                                     CH * j:CH * j + CH],
                outsb[opar][:], obkey)
        head_pe_done[h] = E.val("pe")
        if h == 7:
            E.wait("gpsimd", "ob0", E.val("ob0"))
            E.wait("gpsimd", "ob1", E.val("ob1"))
            ag0_done = E.op("gpsimd", lambda e: e.collective_compute(
                "AllGather", ALU.bypass, replica_groups=[list(range(8))],
                ins=[oh0_s.ap().opt()], outs=[ag0_out.ap().opt()]), "cc")

    b_pe = E.val("pe")

    xnhh = AB[:, 0:16384]
    xnlh = AB[:, 16384:32768]
    wdh = [AB[:, 32768:34816], AB[:, 34816:36864]]
    wdl = [AB[:, 36864:38912], AB[:, 38912:40960]]
    fw = [AI[:, 0:2048], AI[:, 2048:4096]]
    sxp = [FA[:, 0:2048], FA[:, 2048:4096]]
    wdf = [FA[:, 4096:6144], FA[:, 6144:8192]]
    fout = [FA[:, 8192:8704], FA[:, 8704:9216]]
    fpart = [FA[:, 9216:9728], FA[:, 9728:10240]]

    E.wait("vector", "ob0", E.val("ob0"))
    E.wait("vector", "ob1", E.val("ob1"))
    E.wait("gpsimd", "ob0", E.val("ob0"))
    E.wait("gpsimd", "ob1", E.val("ob1"))
    E.wait("gpsimd", "pe", b_pe)
    E.wait("sync", "ac", E.val("ac"))
    E.wait("sync", "ve", E.val("ve"))
    E.wait("sync", "gc", E.val("gc"))
    xn_ld0a = E.dma(
        "gpsimd", xnhh.rearrange("p (i t) -> p i t", t=1024),
        lambda: xnhT_s.ap().rearrange("(i p) t -> p i t", p=128)[
            :, 0:16, DS(pidbox["goff"], 1024)], "gp")
    xn_ld0 = E.dma(
        "gpsimd", xnlh.rearrange("p (i t) -> p i t", t=1024),
        lambda: xnlT_s.ap().rearrange("(i p) t -> p i t", p=128)[
            :, 0:16, DS(pidbox["goff"], 1024)], "gp")
    ag_done = E.op("gpsimd", lambda e: e.collective_compute(
        "AllGather", ALU.bypass, replica_groups=[list(range(8))],
        ins=[oh1_s.ap().opt()], outs=[ag1_out.ap().opt()]), "cc")

    wd_mm = {}
    dq_done = {}
    fout_store = {}
    fp_ev = {}
    fpart_used = {}
    hC_prev = {}
    di = 0
    for half in range(2):
        if half == 1:
            E.wait("gpsimd", "pe", wd_mm[di - 1])
            E.dma("gpsimd", xnhh.rearrange("p (i t) -> p i t", t=1024),
                  lambda: xnhT_s.ap().rearrange("(i p) t -> p i t", p=128)[
                      :, 16:32, DS(pidbox["goff"], 1024)], "gp")
            xn_ld1 = E.dma(
                "gpsimd", xnlh.rearrange("p (i t) -> p i t", t=1024),
                lambda: xnlT_s.ap().rearrange("(i p) t -> p i t", p=128)[
                    :, 16:32, DS(pidbox["goff"], 1024)], "gp")
            E.wait("sync", "fs0", E.val("fs0"))
            E.wait("sync", "fs1", E.val("fs1"))
        for ot in range(32):
            par = di % 2
            fkey = f"f{par}"
            if di >= 2:
                E.wait("sync", "ve", dq_done[di - 2])
            E.dma("sync", fw[par].rearrange("p (i o) -> p i o", o=128),
                  fwT.rearrange("(i p) o -> p i o", p=128)[
                      :, 16 * half:16 * half + 16, 128 * ot:128 * ot + 128], fkey)
            sxld = E.dma("sync", sxp[par].rearrange("p (i o) -> p i o", o=128),
                         sexpT.rearrange("(i p) o -> p i o", p=128)[
                             :, 16 * half:16 * half + 16, 128 * ot:128 * ot + 128], fkey)
            E.wait("vector", fkey, sxld)
            if di >= 2:
                E.wait("vector", "pe", wd_mm[di - 2])
                E.wait("vector", "ac", hC_prev.get(par, 0))
            E.op("vector", lambda e, par=par: e.tensor_copy(wdf[par][:], fw[par][:]))
            mul_v = E.op("vector", lambda e, par=par: e.tensor_mul(
                wdf[par][:], wdf[par][:], sxp[par][:]), "ve")
            E.wait("scalar", "ve", mul_v)
            E.wait("scalar", "pe", wd_mm[di - 2] if di >= 2 else b_pe)
            hC = E.op("scalar", lambda e, par=par: e.copy(wdh[par][:], wdf[par][:]), "ac")
            hC_prev[par] = hC
            E.wait("vector", "ac", hC)
            dq_v = E.op("vector", lambda e, par=par: e.tensor_sub(
                wdl[par][:], wdf[par][:], wdh[par][:]), "ve")
            dq_done[di] = dq_v
            E.wait("tensor", "ve", dq_v)
            if di == 0:
                E.wait("tensor", "gp", xn_ld0)
            if half == 1 and ot == 0:
                E.wait("tensor", "gp", xn_ld1)
            bb = 2 * (di % 2)
            for tc in range(2):
                E.wait("tensor", "ve", fp_ev.get(bb + tc, 0))
            mmv = None
            for i in range(16):
                for tc in range(2):
                    ik = "pe" if (i == 15 and tc == 1) else None
                    mmv = E.op("tensor", lambda e, i=i, tc=tc, par=par, bb=bb: (
                        e.matmul(P[bb + tc][:], wdh[par][:, 128 * i:128 * i + 128],
                                 xnhh[:, 1024 * i + CH * tc:1024 * i + CH * tc + CH],
                                 start=(i == 0), stop=False),
                        e.matmul(P[bb + tc][:], wdh[par][:, 128 * i:128 * i + 128],
                                 xnlh[:, 1024 * i + CH * tc:1024 * i + CH * tc + CH],
                                 start=False, stop=False),
                        e.matmul(P[bb + tc][:], wdl[par][:, 128 * i:128 * i + 128],
                                 xnhh[:, 1024 * i + CH * tc:1024 * i + CH * tc + CH],
                                 start=False, stop=(i == 15)))[-1], ik)
            wd_mm[di] = mmv
            for tc in range(2):
                skey = f"fs{tc}" if half == 0 else f"fs{2 + tc}"
                E.wait("vector", "pe", mmv)
                if half == 0:
                    prev = fout_store.get(tc)
                    if prev is not None:
                        E.wait("vector", prev[0], prev[1])
                    ev = E.op("vector", lambda e, tc=tc, bb=bb: e.tensor_copy(
                        fout[tc][:], P[bb + tc][:]), "ve")
                else:
                    E.wait("sync", "ve", fpart_used.get(tc, 0))
                    pld = E.dma("sync", fpart[tc][:],
                                ffnT_s[128 * ot:128 * ot + 128, CH * tc:CH * tc + CH],
                                f"p{tc}")
                    E.wait("vector", f"p{tc}", pld)
                    prev = fout_store.get(tc)
                    if prev is not None:
                        E.wait("vector", prev[0], prev[1])
                    E.op("vector", lambda e, tc=tc, ot=ot, bb=bb: e.tensor_scalar(
                        fout[tc][:], P[bb + tc][:], fbias_c[:, ot:ot + 1], None, ALU.add))
                    ev = E.op("vector", lambda e, tc=tc: e.tensor_add(
                        fout[tc][:], fout[tc][:], fpart[tc][:]), "ve")
                    fpart_used[tc] = ev
                fp_ev[bb + tc] = ev
                E.wait("scalar", "ve", ev)
                fout_store[tc] = (skey, E.dma(
                    "scalar", ffnT_s[128 * ot:128 * ot + 128, CH * tc:CH * tc + CH],
                    fout[tc][:], skey))
            di += 1
    d1_pe = E.val("pe")

    ohh = AB[:, 0:16384]
    ohl = AB[:, 16384:32768]
    owh = [AB[:, 32768:34816], AB[:, 34816:36864]]
    owl = [AB[:, 36864:38912], AB[:, 38912:40960]]
    ohstrip = FA[:, 0:4096]
    owf = [FA[:, 4096:6144], FA[:, 6144:8192]]
    extr = FA[:, 8192:11264]
    oout = [FA[:, 11264:11776], FA[:, 11776:12288]]

    def extile(kind, par):
        base = 8192 + {"x": 0, "ffn": 1024, "ap": 2048}[kind] + 512 * par
        return FA[:, base:base + 512]

    oslab_done = {}
    osub_done = {}
    oo_store = {}
    oev = {}
    ext_used = {}
    oi = 0
    E.wait("gpsimd", "cc", ag_done)
    E.wait("sync", "ve", E.val("ve"))
    E.wait("sync", "ac", E.val("ac"))
    E.wait("gpsimd", "ve", E.val("ve"))
    E.wait("gpsimd", "ac", E.val("ac"))
    for half in range(2):
        E.wait("gpsimd", "pe", d1_pe if half == 0 else oslab_done[oi - 1])
        strip_done = None
        for srci in range(2):
            agsrc = ag0_out if srci == 0 else ag1_out
            for kq in range(4):
                sld = E.dma("gpsimd",
                            ohstrip.rearrange("p (f t) -> p f t", t=256)[:, 0:8, :],
                            lambda half=half, kq=kq, agsrc=agsrc: agsrc.ap()[
                                DS(pidbox[f"rbA{half}"], 1024),
                                DS(pidbox[f"go{half}_{kq}"], 256)].rearrange(
                                "(f p) t -> p f t", p=128), "gp")
                E.wait("scalar", "gp", sld)
                if strip_done is not None:
                    E.wait("scalar", "ve", strip_done)
                shv = E.op("scalar", lambda e, kq=kq, srci=srci: e.copy(
                    ohh.rearrange("p (f t) -> p f t", t=1024)[
                        :, 8 * srci:8 * srci + 8, 256 * kq:256 * kq + 256],
                    ohstrip.rearrange("p (f t) -> p f t", t=256)[:, 0:8, :]), "ac")
                E.wait("vector", "ac", shv)
                strip_done = E.op("vector", lambda e, kq=kq, srci=srci: e.tensor_sub(
                    ohl.rearrange("p (f t) -> p f t", t=1024)[
                        :, 8 * srci:8 * srci + 8, 256 * kq:256 * kq + 256],
                    ohstrip.rearrange("p (f t) -> p f t", t=256)[:, 0:8, :],
                    ohh.rearrange("p (f t) -> p f t", t=1024)[
                        :, 8 * srci:8 * srci + 8, 256 * kq:256 * kq + 256]), "ve")
                E.wait("gpsimd", "ve", strip_done)
        oh_ready = strip_done
        if half == 1:
            E.wait("sync", "ap0", E.val("ap0"))
            E.wait("sync", "ap1", E.val("ap1"))
            E.wait("sync", "fs2", E.val("fs2"))
            E.wait("sync", "fs3", E.val("fs3"))
        for ot in range(32):
            par = oi % 2
            okey, gok = f"o{par}", f"go{par}"
            owT_ap = owT.rearrange("(f p) o -> p f o", p=128)[
                :, 16 * half:16 * half + 16, 128 * ot:128 * ot + 128]
            if oi >= 2:
                E.wait("sync", "ve", osub_done[oi - 2])
                E.wait("gpsimd", "pe", oslab_done[oi - 2])
            owld = E.dma("sync", owf[par].rearrange("p (f o) -> p f o", o=128),
                         owT_ap, okey)
            gld = E.dma("gpsimd", owh[par].rearrange("p (f o) -> p f o", o=128),
                        owT_ap, gok)
            E.wait("vector", okey, owld)
            E.wait("vector", gok, gld)
            if oi >= 2:
                E.wait("vector", "pe", oslab_done[oi - 2])
            osub = E.op("vector", lambda e, par=par: e.tensor_sub(
                owl[par][:], owf[par][:], owh[par][:]), "ve")
            osub_done[oi] = osub
            E.wait("tensor", "ve", osub)
            E.wait("tensor", "ve", oh_ready)
            ob_ = 2 * (oi % 2)
            for tc in range(2):
                E.wait("tensor", "ve", oev.get(ob_ + tc, 0))
            ommv = None
            for f in range(16):
                for tc in range(2):
                    ik = "pe" if (f == 15 and tc == 1) else None
                    ommv = E.op("tensor", lambda e, f=f, tc=tc, par=par, ob_=ob_: (
                        e.matmul(P[ob_ + tc][:], owh[par][:, 128 * f:128 * f + 128],
                                 ohh[:, 1024 * f + CH * tc:1024 * f + CH * tc + CH],
                                 start=(f == 0), stop=False),
                        e.matmul(P[ob_ + tc][:], owh[par][:, 128 * f:128 * f + 128],
                                 ohl[:, 1024 * f + CH * tc:1024 * f + CH * tc + CH],
                                 start=False, stop=False),
                        e.matmul(P[ob_ + tc][:], owl[par][:, 128 * f:128 * f + 128],
                                 ohh[:, 1024 * f + CH * tc:1024 * f + CH * tc + CH],
                                 start=False, stop=(f == 15)))[-1], ik)
            oslab_done[oi] = ommv
            for tc in range(2):
                E.wait("vector", "pe", ommv)
                prev = oo_store.get(tc)
                if half == 0:
                    if prev is not None:
                        E.wait("vector", prev[0], prev[1])
                    ev = E.op("vector", lambda e, tc=tc, ob_=ob_: e.tensor_copy(
                        oout[tc][:], P[ob_ + tc][:]), "ve")
                    oev[ob_ + tc] = ev
                    E.wait("scalar", "ve", ev)
                    oo_store[tc] = (f"ap{tc}", E.dma(
                        "scalar", apT_s[128 * ot:128 * ot + 128, CH * tc:CH * tc + CH],
                        oout[tc][:], f"ap{tc}"))
                else:
                    ekey = f"e{tc}"
                    E.wait("sync", "ve", ext_used.get(tc, 0))
                    E.dma("sync", extile("ap", tc)[:],
                          apT_s[128 * ot:128 * ot + 128, CH * tc:CH * tc + CH], ekey)
                    E.dma("sync", extile("ffn", tc)[:],
                          ffnT_s[128 * ot:128 * ot + 128, CH * tc:CH * tc + CH], ekey)
                    xld2 = E.dma("sync", extile("x", tc)[:],
                                 xTo[128 * ot:128 * ot + 128, CH * tc:CH * tc + CH],
                                 ekey)
                    E.wait("vector", ekey, xld2)
                    if prev is not None:
                        E.wait("vector", prev[0], prev[1])
                    E.op("vector", lambda e, tc=tc, ob_=ob_: e.tensor_add(
                        oout[tc][:], P[ob_ + tc][:], extile("ap", tc)[:]))
                    E.op("vector", lambda e, tc=tc: e.tensor_add(
                        oout[tc][:], oout[tc][:], extile("x", tc)[:]))
                    ev = E.op("vector", lambda e, tc=tc: e.tensor_add(
                        oout[tc][:], oout[tc][:], extile("ffn", tc)[:]), "ve")
                    ext_used[tc] = ev
                    oev[ob_ + tc] = ev
                    E.wait("scalar", "ve", ev)
                    oo_store[tc] = (f"ou{tc}", E.dma(
                        "scalar", outT[128 * ot:128 * ot + 128, CH * tc:CH * tc + CH],
                        oout[tc][:], f"ou{tc}"))
            oi += 1

    E.wait("scalar", "ou0", E.val("ou0"))
    E.wait("scalar", "ou1", E.val("ou1"))
    E.wait("gpsimd", "gp")

    with nc.Block() as block:
        block.sync(lambda e: [f(e) for f in E.ops["sync"]])
        block.scalar(lambda e: [f(e) for f in E.ops["scalar"]])
        block.vector(lambda e: [f(e) for f in E.ops["vector"]])
        block.gpsimd(lambda e: [f(e) for f in E.ops["gpsimd"]])
        block.tensor(lambda e: [f(e) for f in E.ops["tensor"]])

    for k in sem_ctx:
        sem_ctx[k].__exit__(None, None, None)
    return nc


_NC_CACHE = {}


def kernel(L0_attn_hidden_states, L0_attn_q_w, L0_attn_k_w, L0_attn_v_w,
           L0_attn_o_w, L0_ffn_qweight, L0_ffn_scales, L0_ffn_bias):
    import ml_dtypes
    x = np.asarray(L0_attn_hidden_states, dtype=np.float32)
    q_w = np.asarray(L0_attn_q_w, dtype=np.float32)
    k_w = np.asarray(L0_attn_k_w, dtype=np.float32)
    v_w = np.asarray(L0_attn_v_w, dtype=np.float32)
    o_w = np.asarray(L0_attn_o_w, dtype=np.float32)
    qw_i = np.asarray(L0_ffn_qweight, dtype=np.int32)
    scales = np.asarray(L0_ffn_scales, dtype=np.float32)
    fb = np.asarray(L0_ffn_bias, dtype=np.float32)

    owT = np.ascontiguousarray(o_w.T)
    fwT = np.ascontiguousarray(qw_i.T)
    s2 = scales.reshape(HID, 64)
    sexpT = np.ascontiguousarray(np.repeat(s2.T, 64, axis=0))
    fbias = np.ascontiguousarray(fb.reshape(HID, 1))

    inv = 1.0 / (10000.0 ** (np.arange(0, DH, 2, dtype=np.float64) / DH))
    freqs = np.arange(S, dtype=np.float64)[:, None] * inv[None, :]
    emb = np.concatenate([freqs, freqs], axis=-1)
    cosT = np.ascontiguousarray(np.cos(emb).T.astype(np.float32))
    sinT = np.ascontiguousarray(np.sin(emb).T.astype(np.float32))

    ptT = np.zeros((128, 128), dtype=np.float32)
    ptT[np.arange(64), np.arange(64) + 64] = 1.0
    ptT[np.arange(64) + 64, np.arange(64)] = -1.0

    ident = np.eye(128, dtype=np.float32)
    onesm = np.ones((128, 128), dtype=np.float32)
    onesb = np.ones((128, 128), dtype=ml_dtypes.bfloat16)
    identb = np.eye(128, dtype=ml_dtypes.bfloat16)

    in_maps = []
    for c in range(NCORES):
        b, g = c // 2, c % 2
        in_maps.append({
            "xT": np.ascontiguousarray(x[b].T),
            "xTo": np.ascontiguousarray(x[b].T[:, 1024 * g:1024 * g + 1024]),
            "qwT": np.ascontiguousarray(q_w.T[:, 2048 * g:2048 * g + 2048]),
            "kwT": np.ascontiguousarray(k_w.T[:, 2048 * g:2048 * g + 2048]),
            "vwT": np.ascontiguousarray(v_w.T[:, 2048 * g:2048 * g + 2048]),
            "owT": owT, "fwT": fwT, "sexpT": sexpT, "fbias": fbias,
            "cosT": cosT, "sinT": sinT, "ptT": ptT, "ident": ident,
            "onesm": onesm, "onesb": onesb, "identb": identb,
            "impsc": np.full((1, 1), 1.0 if b == 0 else 0.0, dtype=np.float32),
        })

    if "prog" not in _NC_CACHE:
        _NC_CACHE["prog"] = build_program()
    nc = _NC_CACHE["prog"]
    res = run_bass_kernel_spmd(nc, in_maps, core_ids=list(range(NCORES)))

    out = np.empty((B, S, HID), dtype=np.float32)
    for c in range(NCORES):
        b, g = c // 2, c % 2
        out[b, 1024 * g:1024 * g + 1024, :] = res.results[c]["outT"].T
    return out


# version 05
# speedup vs baseline: 1.3071x; 1.1857x over previous
import numpy as np
import concourse.bass as bass
import concourse.mybir as mybir
from concourse.bass_utils import run_bass_kernel_spmd

F32 = mybir.dt.float32
BF16 = mybir.dt.bfloat16
I32 = mybir.dt.int32
AF = mybir.ActivationFunctionType
ALU = mybir.AluOpType
AXX = mybir.AxisListType.X
DS = bass.ds

B, S, H, DH, HID = 4, 2048, 32, 128, 4096
NCORES = 8
SCALE = 1.0 / float(np.sqrt(128.0))
SHIFT = -30.0
EPS = 1e-5
VIS_LO, VIS_HI = 1, 513
NCH = 4
CH = 512
NFT = 16
NCT = 32

DMA_KEYS = ("c", "x0", "x1", "w0", "w1", "gw0", "gw1", "kv0", "kv1",
            "h0", "h1", "f0", "f1", "p0", "p1", "o0", "o1", "go0", "go1",
            "e0", "e1", "xs", "qk0", "qk1", "v0", "v1", "ob0", "ob1",
            "fs0", "fs1", "fs2", "fs3", "ap0", "ap1", "ou0", "ou1", "gp")
ENG_KEYS = ("pe", "ac", "ve", "gc", "cc")


class Emitter:
    def __init__(self, sems, nc=None):
        self.ops = {k: [] for k in ("sync", "scalar", "vector", "gpsimd", "tensor")}
        self.cnt = {}
        self.last_wait = {k: {} for k in self.ops}
        self.sems = sems
        self.nc = nc

    def mark(self, name):
        if self.nc is None:
            return
        for q in self.ops:
            self.ops[q].append(lambda e, n=name, nc=self.nc: (
                nc.leave_named_scope(), nc.enter_named_scope(n))[-1]
                if getattr(nc, "_scope_open", False) else
                (nc.enter_named_scope(n), setattr(nc, "_scope_open", True))[-1])

    def inc(self, key, n):
        self.cnt[key] = self.cnt.get(key, 0) + n
        return self.cnt[key]

    def val(self, key):
        return self.cnt.get(key, 0)

    def wait(self, eng, key, v=None):
        v = self.val(key) if v is None else v
        if not v or v <= 0:
            return
        lw = self.last_wait[eng]
        if lw.get(key, -1) >= v:
            return
        lw[key] = v
        s = self.sems[key]
        self.ops[eng].append(lambda e, s=s, vv=v: e.wait_ge(s, vv))

    def dma(self, q, out_ap, in_ap, key, slow=False):
        prev = self.val(key)
        if prev:
            self.wait(q, key, prev)
        v = self.inc(key, 16)
        s = self.sems[key]

        def run(e, o=out_ap, i=in_ap, s=s, slow=slow):
            if callable(o):
                o = o()
            if callable(i):
                i = i()
            if slow:
                e.dma_start(out=o, in_=i, allow_slow_non_contiguous=True).then_inc(s, 16)
            else:
                e.dma_start(out=o, in_=i).then_inc(s, 16)
        self.ops[q].append(run)
        return v

    def op(self, eng, fn, inc_key=None, inc_n=1):
        if inc_key is None:
            self.ops[eng].append(fn)
            return None
        v = self.inc(inc_key, inc_n)
        s = self.sems[inc_key]
        self.ops[eng].append(lambda e, f=fn, s=s, n=inc_n: f(e).then_inc(s, n))
        return v


def build_program():
    nc = bass.Bass("TRN2", target_bir_lowering=False, detect_race_conditions=False)

    xT = nc.dram_tensor("xT", [HID, S], F32, kind="ExternalInput").ap()
    qwT = nc.dram_tensor("qwT", [HID, 2048], F32, kind="ExternalInput").ap()
    kwT = nc.dram_tensor("kwT", [HID, 2048], F32, kind="ExternalInput").ap()
    vwT = nc.dram_tensor("vwT", [HID, 2048], F32, kind="ExternalInput").ap()
    owT = nc.dram_tensor("owT", [HID, HID], F32, kind="ExternalInput").ap()
    fwT = nc.dram_tensor("fwT", [HID, HID], I32, kind="ExternalInput").ap()
    sexpT = nc.dram_tensor("sexpT", [HID, HID], F32, kind="ExternalInput").ap()
    fbias = nc.dram_tensor("fbias", [HID, 1], F32, kind="ExternalInput").ap()
    cosT_d = nc.dram_tensor("cosT", [128, S], F32, kind="ExternalInput").ap()
    sinT_d = nc.dram_tensor("sinT", [128, S], F32, kind="ExternalInput").ap()
    ptT_d = nc.dram_tensor("ptT", [128, 128], F32, kind="ExternalInput").ap()
    ident_d = nc.dram_tensor("ident", [128, 128], F32, kind="ExternalInput").ap()
    ones_d = nc.dram_tensor("onesm", [128, 128], F32, kind="ExternalInput").ap()
    onesb_d = nc.dram_tensor("onesb", [128, 128], BF16, kind="ExternalInput").ap()
    identb_d = nc.dram_tensor("identb", [128, 128], BF16, kind="ExternalInput").ap()
    impsc_d = nc.dram_tensor("impsc", [1, 1], F32, kind="ExternalInput").ap()
    xTo = nc.dram_tensor("xTo", [HID, 1024], F32, kind="ExternalInput").ap()
    outT = nc.dram_tensor("outT", [HID, 1024], F32, kind="ExternalOutput").ap()

    qhT_s = nc.dram_tensor("qhT_s", [2048, S], BF16).ap()
    qlT_s = nc.dram_tensor("qlT_s", [2048, S], BF16).ap()
    khT_s = nc.dram_tensor("khT_s", [2048, S], BF16).ap()
    klT_s = nc.dram_tensor("klT_s", [2048, S], BF16).ap()
    vT_s = nc.dram_tensor("vT_s", [2048, S], F32).ap()
    xnhT_s = nc.dram_tensor("xnhT_s", [HID, S], BF16)
    xnlT_s = nc.dram_tensor("xnlT_s", [HID, S], BF16)
    oh0_s = nc.dram_tensor("oh0_s", [1024, S], F32)
    oh1_s = nc.dram_tensor("oh1_s", [1024, S], F32)
    ag0_out = nc.dram_tensor("ag0_out", [8 * 1024, S], F32)
    ag1_out = nc.dram_tensor("ag1_out", [8 * 1024, S], F32)
    ffnT_s = nc.dram_tensor("ffnT_s", [HID, 1024], F32).ap()
    apT_s = nc.dram_tensor("apT_s", [HID, 1024], F32).ap()
    ar_in = nc.dram_tensor("ar_in", [1, 512], F32)
    ar_out = nc.dram_tensor("ar_out", [1, 512], F32)
    kbias_s = nc.dram_tensor("kbias_s", [640, 1], F32).ap()

    FA = nc.alloc_sbuf_tensor("FA", [128, 13824], F32).ap()
    AB = nc.alloc_sbuf_tensor("AB", [128, 53248], BF16).ap()
    AI = nc.alloc_sbuf_tensor("AI", [128, 4096], I32).ap()
    ROWS = nc.alloc_sbuf_tensor("ROWS", [1, 3072], F32).ap()
    cosT = nc.alloc_sbuf_tensor("cosT_sb", [128, S], F32).ap()
    sinT = nc.alloc_sbuf_tensor("sinT_sb", [128, S], F32).ap()
    ptT = nc.alloc_sbuf_tensor("ptT_sb", [128, 128], F32).ap()
    ident = nc.alloc_sbuf_tensor("ident_sb", [128, 128], F32).ap()
    ones = nc.alloc_sbuf_tensor("ones_sb", [128, 128], F32).ap()
    onesb = nc.alloc_sbuf_tensor("onesb_sb", [128, 128], BF16).ap()
    identb = nc.alloc_sbuf_tensor("identb_sb", [128, 128], BF16).ap()
    sumq = nc.alloc_sbuf_tensor("sumq_sb", [128, 16], F32).ap()
    sumqh = nc.alloc_sbuf_tensor("sumqh_sb", [128, 16], BF16).ap()
    sumql = nc.alloc_sbuf_tensor("sumql_sb", [128, 16], BF16).ap()
    sumq_t = nc.alloc_sbuf_tensor("sumq_t", [128, 1], F32).ap()
    kbias = nc.alloc_sbuf_tensor("kbias_sb", [128, 8], F32).ap()
    m30 = nc.alloc_sbuf_tensor("m30_sb", [128, 1], F32).ap()
    impsc = nc.alloc_sbuf_tensor("impsc_sb", [1, 1], F32).ap()
    impcols = nc.alloc_sbuf_tensor("impcols_sb", [128, 4], F32).ap()
    fbias_c = nc.alloc_sbuf_tensor("fbias_sb", [128, 32], F32).ap()
    rankw = nc.alloc_sbuf_tensor("rankw_sb", [128, 8], F32).ap()

    P = [nc.alloc_psum_tensor(f"P{i}", [128, 512], F32).ap() for i in range(8)]
    P6 = P[6][0:1, :]
    P7 = P[7][0:1, :]

    sems, sem_ctx = {}, {}
    for k in DMA_KEYS + ENG_KEYS:
        g = nc.semaphore(k)
        sems[k] = g.__enter__()
        sem_ctx[k] = g
    E = Emitter(sems)

    R = {name: ROWS[0:1, 512 * i:512 * i + 512]
         for i, name in enumerate(("mean", "vpe", "b", "ss", "rstd", "negmr"))}

    ones_col = ones[:, 0:1]
    ones_row = ones[0:1, :]
    onesb_col = onesb[:, 0:1]

    E.dma("sync", cosT[:], cosT_d, "c")
    E.dma("sync", sinT[:], sinT_d, "c")
    E.dma("sync", ptT[:], ptT_d, "c")
    E.dma("sync", ident[:], ident_d, "c")
    E.dma("sync", ones[:], ones_d, "c")
    E.dma("sync", onesb[:], onesb_d, "c")
    E.dma("sync", identb[:], identb_d, "c")
    E.dma("sync", impsc[:], impsc_d, "c")
    consts_ld = E.dma("sync", fbias_c[:],
                      fbias[:, 0:1].rearrange("(o p) one -> p (o one)", p=128), "c",
                      slow=True)
    E.wait("vector", "c", consts_ld)
    E.op("vector", lambda e: e.memset(sumq[:], 0.0))
    E.op("vector", lambda e: e.memset(m30[:], SHIFT), "ve")

    pidbox = {}

    def gp_init(e):
        pid = nc.partition_id([mybir.EngineType.Pool])
        g = pid % 2
        pidbox["goff"] = g * 1024
        for half in range(2):
            pidbox[f"rbA{half}"] = (pid - g + half) * 1024
            for kq in range(4):
                pidbox[f"go{half}_{kq}"] = g * 1024 + 256 * kq
        return e.memset(rankw[:, 7:8], 0.0)
    E.op("gpsimd", gp_init, "gc")

    xtile = [FA[:, 0:512], FA[:, 512:1024]]
    wslab = [FA[:, 1024:5120], FA[:, 1024:5120]]
    vout = [FA[:, 5120:5632], FA[:, 5632:6144]]
    traw = [FA[:, 6144:6656], FA[:, 6656:7168]]
    t1b = FA[:, 7168:7680]
    t2b = FA[:, 7680:8192]
    qrout = [FA[:, 8192:8704], FA[:, 8704:9216]]
    xsq = [FA[:, 9216:9728], FA[:, 9728:10240]]
    bc_rstd = FA[:, 10240:10752]
    bc_negmr = FA[:, 10752:11264]

    xnh = AB[:, 0:16384]
    xnl = AB[:, 16384:32768]
    wsh = [AB[:, 32768:36864], AB[:, 36864:40960]]
    wsl = [AB[:, 40960:45056], AB[:, 45056:49152]]
    qrh = [AB[:, 49152:49664], AB[:, 49664:50176]]
    qrl = [AB[:, 50176:50688], AB[:, 50688:51200]]

    slab_mm = {}
    slab_sub = {}
    qr_store = {}
    v_store = {}
    evac_ac = {}
    evac_ve = {}
    traw_free = {}
    rot_free = {}
    pending_rot = []
    prev_nm_v = 0
    prev_bc_cp = 0
    gi = 0
    xn_split_done = 0

    def flush_rot(item):
        (rgi, rj, rft, rqp, rev, rdsth, rdstl) = item
        rp = rgi % 2
        qkey = f"qk{rqp}"
        E.wait("tensor", "ac", rev)
        E.wait("tensor", "ve", rot_free.get(rp, 0))
        rot = E.op("tensor", lambda e, rqp=rqp, rp=rp: e.matmul(
            P[2 + rp][:], ptT[:], traw[rqp][:], start=True, stop=True), "pe")
        E.wait("vector", "pe", rot)
        if rqp in qr_store:
            E.wait("vector", qr_store[rqp][0], qr_store[rqp][1])
        tv = E.op("vector", lambda e, rqp=rqp, rj=rj: e.tensor_mul(
            t1b[:], traw[rqp][:], cosT[:, CH * rj:CH * rj + CH]), "ve")
        traw_free[rqp] = tv
        rused = E.op("vector", lambda e, rp=rp, rj=rj: e.tensor_mul(
            t2b[:], P[2 + rp][:], sinT[:, CH * rj:CH * rj + CH]), "ve")
        rot_free[rp] = rused
        rope_v = E.op("vector", lambda e, rqp=rqp: e.tensor_add(
            qrout[rqp][:], t1b[:], t2b[:]), "ve")
        if rdstl is qlT_s and rj >= 1:
            lo = {1: 1, 2: 0, 3: 0}[rj]
            hi = {1: 512, 2: 512, 3: 456}[rj]
            rs_v = E.op("vector", lambda e, rqp=rqp, lo=lo, hi=hi: e.reduce_sum(
                sumq_t[:], qrout[rqp][:, lo:hi], axis=AXX), "ve")
            E.wait("vector", "ve", rs_v)
            acc_v = E.op("vector", lambda e, rft=rft: e.tensor_add(
                sumq[:, rft:rft + 1], sumq[:, rft:rft + 1], sumq_t[:]), "ve")
            E.wait("vector", "ve", acc_v)
        E.wait("scalar", "ve", rope_v)
        qh_v = E.op("scalar", lambda e, rqp=rqp: e.copy(
            qrh[rqp][:], qrout[rqp][:]), "ac")
        E.wait("vector", "ac", qh_v)
        ql_v = E.op("vector", lambda e, rqp=rqp: e.tensor_sub(
            qrl[rqp][:], qrout[rqp][:], qrh[rqp][:]), "ve")
        E.wait("scalar", "ve", ql_v)
        E.dma("scalar", rdsth[128 * rft:128 * rft + 128, CH * rj:CH * rj + CH],
              qrh[rqp][:], qkey)
        st2 = E.dma("scalar", rdstl[128 * rft:128 * rft + 128, CH * rj:CH * rj + CH],
                    qrl[rqp][:], qkey)
        qr_store[rqp] = (qkey, st2)

    for j in range(NCH):
        E.wait("tensor", "ve", prev_nm_v)
        sq_vals, ssq_vals = [], []
        for c in range(NCT):
            par = c % 2
            xk = f"x{par}"
            if c >= 2:
                E.wait("sync", "pe", ssq_vals[c - 2])
            xld = E.dma("sync", xtile[par][:],
                        xT.rearrange("(c p) t -> p c t", p=128)[
                            :, c, CH * j:CH * j + CH], xk)
            E.wait("scalar", xk, xld)
            if c >= 2:
                E.wait("scalar", "pe", ssq_vals[c - 2])
            sq_vals.append(E.op("scalar", lambda e, par=par: e.activation(
                xsq[par][:], xtile[par][:], AF.Square), "ac"))
            E.wait("tensor", xk, xld)
            E.op("tensor", lambda e, par=par, c=c: e.matmul(
                P6[:], ones_col, xtile[par][:],
                start=(c == 0), stop=(c == NCT - 1)))
            E.wait("tensor", "ac", sq_vals[c])
            ssq_vals.append(E.op("tensor", lambda e, par=par, c=c: e.matmul(
                P7[:], ones_col, xsq[par][:],
                start=(c == 0), stop=(c == NCT - 1)), "pe"))
        E.wait("vector", "pe", ssq_vals[-1])
        E.op("vector", lambda e: e.tensor_scalar(R["mean"], P6[:], 1.0 / HID, None, ALU.mult))
        E.op("vector", lambda e: e.tensor_scalar(R["vpe"], P7[:], 1.0 / HID, None, ALU.mult))
        E.op("vector", lambda e: e.tensor_mul(R["b"], R["mean"], R["mean"]))
        E.op("vector", lambda e: e.scalar_tensor_tensor(
            R["vpe"], R["vpe"], EPS, R["b"], ALU.add, ALU.subtract))
        rr_v = E.op("vector", lambda e: e.reciprocal(R["b"], R["vpe"]), "ve")
        E.wait("scalar", "ve", rr_v)
        ss_v = E.op("scalar", lambda e: e.activation(R["ss"], R["b"], AF.Sqrt), "ac")
        E.wait("vector", "ac", ss_v)
        E.op("vector", lambda e: e.tensor_mul(R["b"], R["ss"], R["ss"]))
        E.op("vector", lambda e: e.tensor_mul(R["b"], R["b"], R["vpe"]))
        E.op("vector", lambda e: e.tensor_scalar(R["b"], R["b"], -0.5, 1.5, ALU.mult, ALU.add))
        E.op("vector", lambda e: e.tensor_mul(R["rstd"], R["ss"], R["b"]))
        E.op("vector", lambda e: e.tensor_mul(R["negmr"], R["mean"], R["rstd"]))
        nm_v = E.op("vector", lambda e: e.tensor_scalar(
            R["negmr"], R["negmr"], -1.0, None, ALU.mult), "ve")
        prev_nm_v = nm_v
        E.wait("tensor", "ve", nm_v)
        E.wait("tensor", "ac", prev_bc_cp)
        E.wait("tensor", "ve", max(rot_free.get(0, 0), rot_free.get(1, 0)))
        bc_mm = E.op("tensor", lambda e: (
            e.matmul(P[2][:], ones_row, R["rstd"], start=True, stop=True),
            e.matmul(P[3][:], ones_row, R["negmr"], start=True, stop=True))[-1], "pe")
        E.wait("scalar", "pe", bc_mm)
        E.op("scalar", lambda e: e.copy(bc_rstd[:], P[2][:]))
        bc_cp = E.op("scalar", lambda e: e.copy(bc_negmr[:], P[3][:]), "ac")
        prev_bc_cp = bc_cp
        E.wait("vector", "ac", bc_cp)

        E.wait("sync", "pe", ssq_vals[-1])
        pe_cum = E.val("pe")
        E.wait("scalar", "pe", pe_cum)
        E.wait("vector", "pe", pe_cum)
        E.wait("scalar", "xs", E.val("xs"))
        E.wait("vector", "xs", E.val("xs"))
        xn_sub = None
        xn_subs = []
        for c in range(NCT):
            par = c % 2
            xk = f"x{par}"
            if c >= 2:
                E.wait("sync", "ve", xn_subs[c - 2])
            xld = E.dma("sync", xtile[par][:],
                        xT.rearrange("(c p) t -> p c t", p=128)[
                            :, c, CH * j:CH * j + CH], xk)
            E.wait("vector", xk, xld)
            E.op("vector", lambda e, par=par: e.tensor_mul(
                xtile[par][:], xtile[par][:], bc_rstd[:]))
            aff_v = E.op("vector", lambda e, par=par: e.tensor_add(
                xtile[par][:], xtile[par][:], bc_negmr[:]), "ve")
            E.wait("scalar", "ve", aff_v)
            hc_v = E.op("scalar", lambda e, par=par, c=c: e.copy(
                xnh[:, CH * c:CH * c + CH], xtile[par][:]), "ac")
            E.wait("vector", "ac", hc_v)
            xn_sub = E.op("vector", lambda e, par=par, c=c: e.tensor_sub(
                xnl[:, CH * c:CH * c + CH], xtile[par][:],
                xnh[:, CH * c:CH * c + CH]), "ve")
            xn_subs.append(xn_sub)
        xn_split_done = xn_sub

        E.wait("scalar", "ve", xn_split_done)
        E.dma("scalar",
              xnhT_s.ap().rearrange("(c p) t -> p c t", p=128)[:, :, CH * j:CH * j + CH],
              xnh.rearrange("p (c t) -> p c t", t=CH), "xs")
        xs_st = E.dma("scalar",
                      xnlT_s.ap().rearrange("(c p) t -> p c t", p=128)[:, :, CH * j:CH * j + CH],
                      xnl.rearrange("p (c t) -> p c t", t=CH), "xs")

        groups = [(proj, wsrc, dsth, dstl)
                  for proj, wsrc, dsth, dstl in (("q", qwT, qhT_s, qlT_s),
                                                 ("k", kwT, khT_s, klT_s),
                                                 ("v", vwT, vT_s, None))
                  for _ft in range(NFT)]

        def prepA(gidx, ft):
            par = gidx % 2
            wk, gk = f"w{par}", f"gw{par}"
            wsrc_ap = groups[ft[0]][1].rearrange("(c p) f -> p c f", p=128)[
                :, :, 128 * ft[1]:128 * ft[1] + 128]
            if gidx >= 1:
                E.wait("sync", "ve", slab_sub.get(gidx - 1, 0))
            if gidx >= 2:
                E.wait("gpsimd", "pe", slab_mm.get(gidx - 2, 0))
            wld = E.dma("sync", wslab[par].rearrange("p (c f) -> p c f", f=128),
                        wsrc_ap, wk)
            gld = E.dma("gpsimd", wsh[par].rearrange("p (c f) -> p c f", f=128),
                        wsrc_ap, gk)
            E.wait("vector", wk, wld)
            E.wait("vector", gk, gld)
            E.wait("vector", "pe", slab_mm.get(gidx - 2, 0))
            dq_v = E.op("vector", lambda e, par=par: e.tensor_sub(
                wsl[par][:], wslab[par][:], wsh[par][:]), "ve")
            slab_sub[gidx] = dq_v

        base_gi = gi
        prepA(base_gi, (0, 0))
        for idx in range(len(groups)):
            proj, wsrc, dsth, dstl = groups[idx]
            ft = idx % NFT
            if idx + 1 < len(groups):
                prepA(base_gi + idx + 1, (idx + 1, (idx + 1) % NFT))
            par = gi % 2
            E.wait("tensor", "ve", slab_sub[gi])
            E.wait("tensor", "ve", xn_split_done)
            E.wait("tensor", "ac", evac_ac.get(gi - 2, 0))
            pv = None
            for c in range(NCT):
                ik = "pe" if c == NCT - 1 else None
                pv = E.op("tensor", lambda e, c=c, par=par: (
                    e.matmul(P[par][:], wsh[par][:, 128 * c:128 * c + 128],
                             xnh[:, CH * c:CH * c + CH],
                             start=(c == 0), stop=False),
                    e.matmul(P[par][:], wsh[par][:, 128 * c:128 * c + 128],
                             xnl[:, CH * c:CH * c + CH],
                             start=False, stop=False),
                    e.matmul(P[par][:], wsl[par][:, 128 * c:128 * c + 128],
                             xnh[:, CH * c:CH * c + CH],
                             start=False, stop=(c == NCT - 1)))[-1], ik)
            slab_mm[gi] = pv
            if proj == "v":
                vp = gi % 2
                vkey = f"v{vp}"
                E.wait("scalar", "pe", pv)
                if vp in v_store:
                    E.wait("scalar", vkey, v_store[vp])
                hv = E.op("scalar", lambda e, vp=vp, par=par: e.copy(
                    vout[vp][:], P[par][:]), "ac")
                evac_ac[gi] = hv
                v_store[vp] = E.dma(
                    "scalar", dsth[128 * ft:128 * ft + 128, CH * j:CH * j + CH],
                    vout[vp][:], vkey)
            else:
                qp = gi % 2
                E.wait("scalar", "pe", pv)
                E.wait("scalar", "ve", traw_free.get(qp, 0))
                ev = E.op("scalar", lambda e, qp=qp, par=par: e.copy(
                    traw[qp][:], P[par][:]), "ac")
                evac_ac[gi] = ev
                pending_rot.append((gi, j, ft, qp, ev, dsth, dstl))
            if len(pending_rot) > 1:
                flush_rot(pending_rot.pop(0))
            gi += 1
        while pending_rot:
            flush_rot(pending_rot.pop(0))

    kvh = [AB[:, 32768 + i * 512:32768 + i * 512 + 512] for i in range(2)]
    kvl = [AB[:, 33792 + i * 512:33792 + i * 512 + 512] for i in range(2)]
    for kk in ("qk0", "qk1", "v0", "v1", "xs"):
        E.wait("sync", kk, E.val(kk))
    E.wait("sync", "pe", E.val("pe"))
    sq_all = E.val("ve")
    E.wait("scalar", "ve", sq_all)
    sqh_v = E.op("scalar", lambda e: e.copy(sumqh[:], sumq[:]), "ac")
    E.wait("vector", "ac", sqh_v)
    sql_v = E.op("vector", lambda e: e.tensor_sub(sumql[:], sumq[:], sumqh[:]), "ve")
    E.wait("vector", "ve", sql_v)
    imp_vals = []
    for h in range(NFT):
        kvk = f"kv{h % 2}"
        if h >= 2:
            E.wait("sync", "pe", imp_vals[h - 2])
        E.dma("sync", kvh[h % 2][:], khT_s[128 * h:128 * h + 128, VIS_LO:VIS_HI], kvk)
        kvld = E.dma("sync", kvl[h % 2][:], klT_s[128 * h:128 * h + 128, VIS_LO:VIS_HI], kvk)
        E.wait("tensor", kvk, kvld)
        E.wait("tensor", "ve", sql_v)
        imp_vals.append(E.op("tensor", lambda e, h=h: (
            e.matmul(P6[:], sumqh[:, h:h + 1], kvh[h % 2][:],
                     start=(h == 0), stop=False),
            e.matmul(P6[:], sumqh[:, h:h + 1], kvl[h % 2][:],
                     start=False, stop=False),
            e.matmul(P6[:], sumql[:, h:h + 1], kvh[h % 2][:],
                     start=False, stop=(h == NFT - 1)))[-1], "pe"))
    E.wait("vector", "pe", imp_vals[-1])
    imp_v = E.op("vector", lambda e: e.tensor_scalar(
        R["mean"], P6[:], impsc[0:1, 0:1], None, ALU.mult), "ve")
    E.wait("gpsimd", "ve", imp_v)
    E.dma("gpsimd", ar_in.ap(), R["mean"], "gp")
    E.wait("gpsimd", "gp")
    ar_done = E.op("gpsimd", lambda e: e.collective_compute(
        "AllReduce", ALU.add, replica_groups=[list(range(8))],
        ins=[ar_in.ap().opt()], outs=[ar_out.ap().opt()]), "cc")
    E.wait("gpsimd", "cc", ar_done)
    impr_ld = E.dma("gpsimd", R["mean"], ar_out.ap(), "gp")
    impc_ld = E.dma("gpsimd", impcols[:],
                    ar_out.ap().rearrange("one (v p) -> (one p) v", p=128), "gp",
                    slow=True)
    E.wait("tensor", "gp", impc_ld)
    bcimp = E.op("tensor", lambda e: e.matmul(
        P[2][:], ones_row, R["mean"], start=True, stop=True), "pe")
    E.wait("scalar", "pe", bcimp)
    impbc = FA[:, 1024:1536]
    onestile = FA[:, 1536:2048]
    cgt = FA[:, 2048:2560]
    ceq = FA[:, 2560:3072]
    ultv = FA[:, 3072:3584]
    impbc_cp = E.op("scalar", lambda e: e.copy(impbc[:], P[2][:]), "ac")
    E.wait("vector", "ac", impbc_cp)
    ones512_v = E.op("vector", lambda e: e.memset(onestile[:], 1.0), "ve")
    E.wait("gpsimd", "ve", ones512_v)
    E.wait("vector", "gp", impc_ld)
    for vt in range(4):
        asel_v = E.op("gpsimd", lambda e, vt=vt: e.affine_select(
            ultv[:], onestile[:], pattern=[[-1, 512]], compare_op=ALU.is_gt,
            fill=0.0, base=128 * vt, channel_multiplier=1), "gc")
        E.op("vector", lambda e, vt=vt: e.tensor_scalar(
            cgt[:], impbc[:], impcols[:, vt:vt + 1], None, ALU.is_gt))
        E.op("vector", lambda e, vt=vt: e.tensor_scalar(
            ceq[:], impbc[:], impcols[:, vt:vt + 1], None, ALU.is_equal))
        E.wait("vector", "gc", asel_v)
        E.op("vector", lambda e: e.tensor_mul(ceq[:], ceq[:], ultv[:]))
        E.op("vector", lambda e: e.tensor_add(cgt[:], cgt[:], ceq[:]))
        v1 = E.op("vector", lambda e: e.reduce_sum(rankw[:, 0:1], cgt[:], axis=AXX), "ve")
        E.wait("vector", "ve", v1)
        v2 = E.op("vector", lambda e: e.tensor_scalar(
            rankw[:, 1:2], rankw[:, 0:1], 255.5, None, ALU.is_lt), "ve")
        E.wait("vector", "ve", v2)
        v3 = E.op("vector", lambda e: e.tensor_scalar(
            rankw[:, 2:3], rankw[:, 1:2], 1e30, -1e30, ALU.mult, ALU.add), "ve")
        E.wait("vector", "ve", v3)
        kb_v = E.op("vector", lambda e, vt=vt: e.tensor_scalar(
            kbias[:, vt:vt + 1], rankw[:, 2:3], SHIFT, None, ALU.add), "ve")
        E.wait("vector", "ve", kb_v)
        E.wait("gpsimd", "ve", kb_v)
    E.dma("gpsimd", kbias_s[0:1, 0:1], m30[0:1, 0:1], "gp")
    E.dma("gpsimd",
          kbias_s[1:513, 0:1].rearrange("(v p) one -> p (v one)", p=128),
          kbias[:, 0:4], "gp", slow=True)
    E.dma("gpsimd", kbias_s[513:640, 0:1], m30[0:127, 0:1], "gp")
    kb_ld = None
    for kt in range(5):
        kb_ld = E.dma("gpsimd", kbias[:, kt:kt + 1],
                      kbias_s[128 * kt:128 * kt + 128, 0:1], "gp")

    def hb(p, i):
        return AB[:, 12288 * p + 2048 * i:12288 * p + 2048 * i + 2048]
    kqv = {nm: [hb(0, i), hb(1, i)] for i, nm in
           enumerate(("kh", "kl", "qh", "ql", "vth", "vtl"))}
    vfT = [FA[:, 9728:11776], FA[:, 11776:13824]]
    phb = AB[:, 32768:40960]
    plb = AB[:, 40960:49152]
    pbuf = FA[:, 0:8192]
    outsb = [FA[:, 8192:8704], FA[:, 8704:9216]]
    rbc = FA[:, 9216:9728]
    rrow = R["b"]

    head_pe_done = {}
    out_store = {}
    prev_onorm = 0
    prev_rbc_cp = 0
    E.wait("scalar", "gp", kb_ld)
    E.wait("sync", "pe", E.val("pe"))
    for h in range(NFT):
        par = h % 2
        hkey = f"h{par}"
        if h >= 2:
            E.wait("sync", "pe", head_pe_done[h - 2])
        for nm, src in (("kh", khT_s), ("kl", klT_s), ("qh", qhT_s), ("ql", qlT_s)):
            E.dma("sync", kqv[nm][par][:], src[128 * h:128 * h + 128, :], hkey)
        ldall = E.dma("sync", vfT[par][:], vT_s[128 * h:128 * h + 128, :], hkey)
        E.wait("tensor", hkey, ldall)
        cp_vals = []
        sub_vals = []
        for kt in range(16):
            pb = P[4 + (kt % 2)]
            if kt >= 2:
                E.wait("tensor", "ac", cp_vals[kt - 2])
                E.wait("tensor", "ve", sub_vals[kt - 2])
            tr = E.op("tensor", lambda e, kt=kt, par=par, pb=pb: e.transpose(
                pb[:, 0:128], vfT[par][:, 128 * kt:128 * kt + 128], ident[:]), "pe")
            E.wait("scalar", "pe", tr)
            cp_vals.append(E.op("scalar", lambda e, kt=kt, par=par, pb=pb: e.copy(
                kqv["vth"][par][:, 128 * kt:128 * kt + 128], pb[:, 0:128]), "ac"))
            E.wait("vector", "ac", cp_vals[-1])
            sub_vals.append(E.op("vector", lambda e, kt=kt, par=par, pb=pb: e.tensor_sub(
                kqv["vtl"][par][:, 128 * kt:128 * kt + 128], pb[:, 0:128],
                kqv["vth"][par][:, 128 * kt:128 * kt + 128]), "ve"))
        vt_ready = cp_vals[-1]
        vt_readyv = sub_vals[-1]
        for j in range(NCH):
            nkt = 4 * (j + 1)
            E.wait("scalar", "gc", E.val("gc"))
            exp_done = {}
            psplit = {}
            for kt in range(nkt):
                sb = [P[0], P[1], P[7]][kt % 3]
                if kt >= 3:
                    E.wait("tensor", "ac", exp_done[kt - 3])
                smm = E.op("tensor", lambda e, kt=kt, j=j, par=par, sb=sb: (
                    e.matmul(sb[:], kqv["kh"][par][:, 128 * kt:128 * kt + 128],
                             kqv["qh"][par][:, CH * j:CH * j + CH],
                             start=True, stop=False),
                    e.matmul(sb[:], kqv["kh"][par][:, 128 * kt:128 * kt + 128],
                             kqv["ql"][par][:, CH * j:CH * j + CH],
                             start=False, stop=False),
                    e.matmul(sb[:], kqv["kl"][par][:, 128 * kt:128 * kt + 128],
                             kqv["qh"][par][:, CH * j:CH * j + CH],
                             start=False, stop=True))[-1], "pe")
                E.wait("scalar", "pe", smm)
                bias = kbias[:, kt:kt + 1] if kt <= 4 else m30[:, 0:1]
                exp_done[kt] = E.op("scalar", lambda e, kt=kt, sb=sb, bias=bias:
                                    e.activation(pbuf[:, CH * kt:CH * kt + CH], sb[:],
                                                 AF.Exp, bias=bias, scale=SCALE), "ac")
            asel = {}
            for kt in range(4 * j, nkt):
                E.wait("gpsimd", "ac", exp_done[kt])
                asel[kt] = E.op("gpsimd", lambda e, kt=kt, j=j: e.affine_select(
                    pbuf[:, CH * kt:CH * kt + CH], pbuf[:, CH * kt:CH * kt + CH],
                    pattern=[[1, 512]], compare_op=ALU.is_ge, fill=0.0,
                    base=CH * j - 128 * kt, channel_multiplier=-1), "gc")
            for kt in range(nkt):
                E.wait("scalar", "ac", exp_done[kt])
                if kt in asel:
                    E.wait("scalar", "gc", asel[kt])
                hv = E.op("scalar", lambda e, kt=kt: e.copy(
                    phb[:, CH * kt:CH * kt + CH], pbuf[:, CH * kt:CH * kt + CH]), "ac")
                E.wait("vector", "ac", hv)
                psplit[kt] = E.op("vector", lambda e, kt=kt: e.tensor_sub(
                    plb[:, CH * kt:CH * kt + CH], pbuf[:, CH * kt:CH * kt + CH],
                    phb[:, CH * kt:CH * kt + CH]), "ve")
            E.wait("tensor", "ac", vt_ready)
            E.wait("tensor", "ve", vt_readyv)
            E.wait("tensor", "ve", prev_onorm)
            pvmm = None
            for kt in range(nkt):
                E.wait("tensor", "ve", psplit[kt])
                E.op("tensor", lambda e, kt=kt, par=par, nkt=nkt: (
                    e.matmul(P[2][:], kqv["vth"][par][:, 128 * kt:128 * kt + 128],
                             phb[:, CH * kt:CH * kt + CH],
                             start=(kt == 0), stop=False),
                    e.matmul(P[2][:], kqv["vth"][par][:, 128 * kt:128 * kt + 128],
                             plb[:, CH * kt:CH * kt + CH],
                             start=False, stop=False),
                    e.matmul(P[2][:], kqv["vtl"][par][:, 128 * kt:128 * kt + 128],
                             phb[:, CH * kt:CH * kt + CH],
                             start=False, stop=(kt == nkt - 1)))[-1])
                pvmm = E.op("tensor", lambda e, kt=kt, nkt=nkt: (
                    e.matmul(P6[:], onesb_col, phb[:, CH * kt:CH * kt + CH],
                             start=(kt == 0), stop=False),
                    e.matmul(P6[:], onesb_col, plb[:, CH * kt:CH * kt + CH],
                             start=False, stop=(kt == nkt - 1)))[-1], "pe")
            E.wait("vector", "pe", pvmm)
            rec_v = E.op("vector", lambda e: e.reciprocal(rrow, P6[:]), "ve")
            E.wait("tensor", "ve", rec_v)
            E.wait("tensor", "ac", prev_rbc_cp)
            bc2 = E.op("tensor", lambda e: e.matmul(
                P[3][:], ones_row, rrow, start=True, stop=True), "pe")
            E.wait("scalar", "pe", bc2)
            rbc_cp = E.op("scalar", lambda e: e.copy(rbc[:], P[3][:]), "ac")
            prev_rbc_cp = rbc_cp
            opar = (h * NCH + j) % 2
            obkey = f"ob{opar}"
            E.wait("vector", "ac", rbc_cp)
            if opar in out_store:
                E.wait("vector", obkey, out_store[opar])
            onorm = E.op("vector", lambda e, opar=opar: e.tensor_mul(
                outsb[opar][:], P[2][:], rbc[:]), "ve")
            prev_onorm = onorm
            E.wait("scalar", "ve", onorm)
            ohdst = oh0_s if h < 8 else oh1_s
            out_store[opar] = E.dma(
                "scalar", ohdst.ap()[128 * (h % 8):128 * (h % 8) + 128,
                                     CH * j:CH * j + CH],
                outsb[opar][:], obkey)
        head_pe_done[h] = E.val("pe")
        if h == 7:
            E.wait("gpsimd", "ob0", E.val("ob0"))
            E.wait("gpsimd", "ob1", E.val("ob1"))
            ag0_done = E.op("gpsimd", lambda e: e.collective_compute(
                "AllGather", ALU.bypass, replica_groups=[list(range(8))],
                ins=[oh0_s.ap().opt()], outs=[ag0_out.ap().opt()]), "cc")

    b_pe = E.val("pe")

    xnhh = AB[:, 0:16384]
    xnlh = AB[:, 16384:32768]
    wdh = [AB[:, 32768:34816], AB[:, 34816:36864]]
    wdl = [AB[:, 36864:38912], AB[:, 38912:40960]]
    fw = [AI[:, 0:2048], AI[:, 2048:4096]]
    sxp = [FA[:, 0:2048], FA[:, 2048:4096]]
    wdf = [FA[:, 4096:6144], FA[:, 6144:8192]]
    fout = [FA[:, 8192:8704], FA[:, 8704:9216]]
    fpart = [FA[:, 9216:9728], FA[:, 9728:10240]]

    E.wait("vector", "ob0", E.val("ob0"))
    E.wait("vector", "ob1", E.val("ob1"))
    E.wait("gpsimd", "ob0", E.val("ob0"))
    E.wait("gpsimd", "ob1", E.val("ob1"))
    E.wait("gpsimd", "pe", b_pe)
    E.wait("sync", "ac", E.val("ac"))
    E.wait("sync", "ve", E.val("ve"))
    E.wait("sync", "gc", E.val("gc"))
    xn_ld0a = E.dma(
        "gpsimd", xnhh.rearrange("p (i t) -> p i t", t=1024),
        lambda: xnhT_s.ap().rearrange("(i p) t -> p i t", p=128)[
            :, 0:16, DS(pidbox["goff"], 1024)], "gp")
    xn_ld0 = E.dma(
        "gpsimd", xnlh.rearrange("p (i t) -> p i t", t=1024),
        lambda: xnlT_s.ap().rearrange("(i p) t -> p i t", p=128)[
            :, 0:16, DS(pidbox["goff"], 1024)], "gp")
    ag_done = E.op("gpsimd", lambda e: e.collective_compute(
        "AllGather", ALU.bypass, replica_groups=[list(range(8))],
        ins=[oh1_s.ap().opt()], outs=[ag1_out.ap().opt()]), "cc")

    wd_mm = {}
    dq_done = {}
    fout_store = {}
    fp_ev = {}
    fpart_used = {}
    hC_prev = {}

    def prepD(d):
        par = d % 2
        fkey = f"f{par}"
        dh, dot = d // 32, d % 32
        if d >= 2:
            E.wait("sync", "ve", dq_done[d - 2])
        E.dma("sync", fw[par].rearrange("p (i o) -> p i o", o=128),
              fwT.rearrange("(i p) o -> p i o", p=128)[
                  :, 16 * dh:16 * dh + 16, 128 * dot:128 * dot + 128], fkey)
        sxld = E.dma("sync", sxp[par].rearrange("p (i o) -> p i o", o=128),
                     sexpT.rearrange("(i p) o -> p i o", p=128)[
                         :, 16 * dh:16 * dh + 16, 128 * dot:128 * dot + 128], fkey)
        E.wait("vector", fkey, sxld)
        if d >= 2:
            E.wait("vector", "pe", wd_mm[d - 2])
            E.wait("vector", "ac", hC_prev.get(par, 0))
        E.op("vector", lambda e, par=par: e.tensor_copy(wdf[par][:], fw[par][:]))
        mul_v = E.op("vector", lambda e, par=par: e.tensor_mul(
            wdf[par][:], wdf[par][:], sxp[par][:]), "ve")
        E.wait("scalar", "ve", mul_v)
        E.wait("scalar", "pe", wd_mm[d - 2] if d >= 2 else b_pe)
        hC = E.op("scalar", lambda e, par=par: e.copy(wdh[par][:], wdf[par][:]), "ac")
        hC_prev[par] = hC
        E.wait("vector", "ac", hC)
        dq_done[d] = E.op("vector", lambda e, par=par: e.tensor_sub(
            wdl[par][:], wdf[par][:], wdh[par][:]), "ve")

    di = 0
    prepD(0)
    for half in range(2):
        if half == 1:
            E.wait("gpsimd", "pe", wd_mm[di - 1])
            E.dma("gpsimd", xnhh.rearrange("p (i t) -> p i t", t=1024),
                  lambda: xnhT_s.ap().rearrange("(i p) t -> p i t", p=128)[
                      :, 16:32, DS(pidbox["goff"], 1024)], "gp")
            xn_ld1 = E.dma(
                "gpsimd", xnlh.rearrange("p (i t) -> p i t", t=1024),
                lambda: xnlT_s.ap().rearrange("(i p) t -> p i t", p=128)[
                    :, 16:32, DS(pidbox["goff"], 1024)], "gp")
            E.wait("sync", "fs0", E.val("fs0"))
            E.wait("sync", "fs1", E.val("fs1"))
        for ot in range(32):
            par = di % 2
            if di + 1 < 64:
                prepD(di + 1)
            E.wait("tensor", "ve", dq_done[di])
            if di == 0:
                E.wait("tensor", "gp", xn_ld0)
            if half == 1 and ot == 0:
                E.wait("tensor", "gp", xn_ld1)
            bb = 2 * (di % 2)
            for tc in range(2):
                E.wait("tensor", "ve", fp_ev.get(bb + tc, 0))
            mmv = None
            for i in range(16):
                for tc in range(2):
                    ik = "pe" if (i == 15 and tc == 1) else None
                    mmv = E.op("tensor", lambda e, i=i, tc=tc, par=par, bb=bb: (
                        e.matmul(P[bb + tc][:], wdh[par][:, 128 * i:128 * i + 128],
                                 xnhh[:, 1024 * i + CH * tc:1024 * i + CH * tc + CH],
                                 start=(i == 0), stop=False),
                        e.matmul(P[bb + tc][:], wdh[par][:, 128 * i:128 * i + 128],
                                 xnlh[:, 1024 * i + CH * tc:1024 * i + CH * tc + CH],
                                 start=False, stop=False),
                        e.matmul(P[bb + tc][:], wdl[par][:, 128 * i:128 * i + 128],
                                 xnhh[:, 1024 * i + CH * tc:1024 * i + CH * tc + CH],
                                 start=False, stop=(i == 15)))[-1], ik)
            wd_mm[di] = mmv
            for tc in range(2):
                skey = f"fs{tc}" if half == 0 else f"fs{2 + tc}"
                E.wait("vector", "pe", mmv)
                if half == 0:
                    prev = fout_store.get(tc)
                    if prev is not None:
                        E.wait("vector", prev[0], prev[1])
                    ev = E.op("vector", lambda e, tc=tc, bb=bb: e.tensor_copy(
                        fout[tc][:], P[bb + tc][:]), "ve")
                else:
                    E.wait("sync", "ve", fpart_used.get(tc, 0))
                    pld = E.dma("sync", fpart[tc][:],
                                ffnT_s[128 * ot:128 * ot + 128, CH * tc:CH * tc + CH],
                                f"p{tc}")
                    E.wait("vector", f"p{tc}", pld)
                    prev = fout_store.get(tc)
                    if prev is not None:
                        E.wait("vector", prev[0], prev[1])
                    E.op("vector", lambda e, tc=tc, ot=ot, bb=bb: e.tensor_scalar(
                        fout[tc][:], P[bb + tc][:], fbias_c[:, ot:ot + 1], None, ALU.add))
                    ev = E.op("vector", lambda e, tc=tc: e.tensor_add(
                        fout[tc][:], fout[tc][:], fpart[tc][:]), "ve")
                    fpart_used[tc] = ev
                fp_ev[bb + tc] = ev
                E.wait("scalar", "ve", ev)
                fout_store[tc] = (skey, E.dma(
                    "scalar", ffnT_s[128 * ot:128 * ot + 128, CH * tc:CH * tc + CH],
                    fout[tc][:], skey))
            di += 1
    d1_pe = E.val("pe")

    ohh = AB[:, 0:16384]
    ohl = AB[:, 16384:32768]
    owh = [AB[:, 32768:34816], AB[:, 34816:36864]]
    owl = [AB[:, 36864:38912], AB[:, 38912:40960]]
    ohstrip = FA[:, 0:4096]
    owf = [FA[:, 4096:6144], FA[:, 6144:8192]]
    extr = FA[:, 8192:11264]
    oout = [FA[:, 11264:11776], FA[:, 11776:12288]]

    def extile(kind, par):
        base = 8192 + {"x": 0, "ffn": 1024, "ap": 2048}[kind] + 512 * par
        return FA[:, base:base + 512]

    oslab_done = {}
    osub_done = {}
    oo_store = {}
    oev = {}
    ext_used = {}

    def prepO(o):
        par = o % 2
        okey, gok = f"o{par}", f"go{par}"
        oh_, oot = o // 32, o % 32
        owT_ap = owT.rearrange("(f p) o -> p f o", p=128)[
            :, 16 * oh_:16 * oh_ + 16, 128 * oot:128 * oot + 128]
        if o >= 2:
            E.wait("sync", "ve", osub_done[o - 2])
            E.wait("gpsimd", "pe", oslab_done[o - 2])
        owld = E.dma("sync", owf[par].rearrange("p (f o) -> p f o", o=128),
                     owT_ap, okey)
        gld = E.dma("gpsimd", owh[par].rearrange("p (f o) -> p f o", o=128),
                    owT_ap, gok)
        E.wait("vector", okey, owld)
        E.wait("vector", gok, gld)
        if o >= 2:
            E.wait("vector", "pe", oslab_done[o - 2])
        osub_done[o] = E.op("vector", lambda e, par=par: e.tensor_sub(
            owl[par][:], owf[par][:], owh[par][:]), "ve")

    oi = 0
    E.wait("gpsimd", "cc", ag_done)
    E.wait("sync", "ve", E.val("ve"))
    E.wait("sync", "ac", E.val("ac"))
    E.wait("gpsimd", "ve", E.val("ve"))
    E.wait("gpsimd", "ac", E.val("ac"))
    E.wait("gpsimd", "pe", d1_pe)
    prepO(0)
    for half in range(2):
        E.wait("gpsimd", "pe", d1_pe if half == 0 else oslab_done[oi - 1])
        strip_done = None
        for srci in range(2):
            agsrc = ag0_out if srci == 0 else ag1_out
            for kq in range(4):
                sld = E.dma("gpsimd",
                            ohstrip.rearrange("p (f t) -> p f t", t=256)[:, 0:8, :],
                            lambda half=half, kq=kq, agsrc=agsrc: agsrc.ap()[
                                DS(pidbox[f"rbA{half}"], 1024),
                                DS(pidbox[f"go{half}_{kq}"], 256)].rearrange(
                                "(f p) t -> p f t", p=128), "gp")
                E.wait("scalar", "gp", sld)
                if strip_done is not None:
                    E.wait("scalar", "ve", strip_done)
                shv = E.op("scalar", lambda e, kq=kq, srci=srci: e.copy(
                    ohh.rearrange("p (f t) -> p f t", t=1024)[
                        :, 8 * srci:8 * srci + 8, 256 * kq:256 * kq + 256],
                    ohstrip.rearrange("p (f t) -> p f t", t=256)[:, 0:8, :]), "ac")
                E.wait("vector", "ac", shv)
                strip_done = E.op("vector", lambda e, kq=kq, srci=srci: e.tensor_sub(
                    ohl.rearrange("p (f t) -> p f t", t=1024)[
                        :, 8 * srci:8 * srci + 8, 256 * kq:256 * kq + 256],
                    ohstrip.rearrange("p (f t) -> p f t", t=256)[:, 0:8, :],
                    ohh.rearrange("p (f t) -> p f t", t=1024)[
                        :, 8 * srci:8 * srci + 8, 256 * kq:256 * kq + 256]), "ve")
                E.wait("gpsimd", "ve", strip_done)
        oh_ready = strip_done
        if half == 1:
            E.wait("sync", "ap0", E.val("ap0"))
            E.wait("sync", "ap1", E.val("ap1"))
            E.wait("sync", "fs2", E.val("fs2"))
            E.wait("sync", "fs3", E.val("fs3"))
        for ot in range(32):
            par = oi % 2
            if oi + 1 < 64:
                prepO(oi + 1)
            E.wait("tensor", "ve", osub_done[oi])
            E.wait("tensor", "ve", oh_ready)
            ob_ = 2 * (oi % 2)
            for tc in range(2):
                E.wait("tensor", "ve", oev.get(ob_ + tc, 0))
            ommv = None
            for f in range(16):
                for tc in range(2):
                    ik = "pe" if (f == 15 and tc == 1) else None
                    ommv = E.op("tensor", lambda e, f=f, tc=tc, par=par, ob_=ob_: (
                        e.matmul(P[ob_ + tc][:], owh[par][:, 128 * f:128 * f + 128],
                                 ohh[:, 1024 * f + CH * tc:1024 * f + CH * tc + CH],
                                 start=(f == 0), stop=False),
                        e.matmul(P[ob_ + tc][:], owh[par][:, 128 * f:128 * f + 128],
                                 ohl[:, 1024 * f + CH * tc:1024 * f + CH * tc + CH],
                                 start=False, stop=False),
                        e.matmul(P[ob_ + tc][:], owl[par][:, 128 * f:128 * f + 128],
                                 ohh[:, 1024 * f + CH * tc:1024 * f + CH * tc + CH],
                                 start=False, stop=(f == 15)))[-1], ik)
            oslab_done[oi] = ommv
            for tc in range(2):
                E.wait("vector", "pe", ommv)
                prev = oo_store.get(tc)
                if half == 0:
                    if prev is not None:
                        E.wait("vector", prev[0], prev[1])
                    ev = E.op("vector", lambda e, tc=tc, ob_=ob_: e.tensor_copy(
                        oout[tc][:], P[ob_ + tc][:]), "ve")
                    oev[ob_ + tc] = ev
                    E.wait("scalar", "ve", ev)
                    oo_store[tc] = (f"ap{tc}", E.dma(
                        "scalar", apT_s[128 * ot:128 * ot + 128, CH * tc:CH * tc + CH],
                        oout[tc][:], f"ap{tc}"))
                else:
                    ekey = f"e{tc}"
                    E.wait("sync", "ve", ext_used.get(tc, 0))
                    E.dma("sync", extile("ap", tc)[:],
                          apT_s[128 * ot:128 * ot + 128, CH * tc:CH * tc + CH], ekey)
                    E.dma("sync", extile("ffn", tc)[:],
                          ffnT_s[128 * ot:128 * ot + 128, CH * tc:CH * tc + CH], ekey)
                    xld2 = E.dma("sync", extile("x", tc)[:],
                                 xTo[128 * ot:128 * ot + 128, CH * tc:CH * tc + CH],
                                 ekey)
                    E.wait("vector", ekey, xld2)
                    if prev is not None:
                        E.wait("vector", prev[0], prev[1])
                    E.op("vector", lambda e, tc=tc, ob_=ob_: e.tensor_add(
                        oout[tc][:], P[ob_ + tc][:], extile("ap", tc)[:]))
                    E.op("vector", lambda e, tc=tc: e.tensor_add(
                        oout[tc][:], oout[tc][:], extile("x", tc)[:]))
                    ev = E.op("vector", lambda e, tc=tc: e.tensor_add(
                        oout[tc][:], oout[tc][:], extile("ffn", tc)[:]), "ve")
                    ext_used[tc] = ev
                    oev[ob_ + tc] = ev
                    E.wait("scalar", "ve", ev)
                    oo_store[tc] = (f"ou{tc}", E.dma(
                        "scalar", outT[128 * ot:128 * ot + 128, CH * tc:CH * tc + CH],
                        oout[tc][:], f"ou{tc}"))
            oi += 1

    E.wait("scalar", "ou0", E.val("ou0"))
    E.wait("scalar", "ou1", E.val("ou1"))
    E.wait("gpsimd", "gp")

    with nc.Block() as block:
        block.sync(lambda e: [f(e) for f in E.ops["sync"]])
        block.scalar(lambda e: [f(e) for f in E.ops["scalar"]])
        block.vector(lambda e: [f(e) for f in E.ops["vector"]])
        block.gpsimd(lambda e: [f(e) for f in E.ops["gpsimd"]])
        block.tensor(lambda e: [f(e) for f in E.ops["tensor"]])

    for k in sem_ctx:
        sem_ctx[k].__exit__(None, None, None)
    return nc


_NC_CACHE = {}


def kernel(L0_attn_hidden_states, L0_attn_q_w, L0_attn_k_w, L0_attn_v_w,
           L0_attn_o_w, L0_ffn_qweight, L0_ffn_scales, L0_ffn_bias):
    import ml_dtypes
    x = np.asarray(L0_attn_hidden_states, dtype=np.float32)
    q_w = np.asarray(L0_attn_q_w, dtype=np.float32)
    k_w = np.asarray(L0_attn_k_w, dtype=np.float32)
    v_w = np.asarray(L0_attn_v_w, dtype=np.float32)
    o_w = np.asarray(L0_attn_o_w, dtype=np.float32)
    qw_i = np.asarray(L0_ffn_qweight, dtype=np.int32)
    scales = np.asarray(L0_ffn_scales, dtype=np.float32)
    fb = np.asarray(L0_ffn_bias, dtype=np.float32)

    owT = np.ascontiguousarray(o_w.T)
    fwT = np.ascontiguousarray(qw_i.T)
    s2 = scales.reshape(HID, 64)
    sexpT = np.ascontiguousarray(np.repeat(s2.T, 64, axis=0))
    fbias = np.ascontiguousarray(fb.reshape(HID, 1))

    inv = 1.0 / (10000.0 ** (np.arange(0, DH, 2, dtype=np.float64) / DH))
    freqs = np.arange(S, dtype=np.float64)[:, None] * inv[None, :]
    emb = np.concatenate([freqs, freqs], axis=-1)
    cosT = np.ascontiguousarray(np.cos(emb).T.astype(np.float32))
    sinT = np.ascontiguousarray(np.sin(emb).T.astype(np.float32))

    ptT = np.zeros((128, 128), dtype=np.float32)
    ptT[np.arange(64), np.arange(64) + 64] = 1.0
    ptT[np.arange(64) + 64, np.arange(64)] = -1.0

    ident = np.eye(128, dtype=np.float32)
    onesm = np.ones((128, 128), dtype=np.float32)
    onesb = np.ones((128, 128), dtype=ml_dtypes.bfloat16)
    identb = np.eye(128, dtype=ml_dtypes.bfloat16)

    in_maps = []
    for c in range(NCORES):
        b, g = c // 2, c % 2
        in_maps.append({
            "xT": np.ascontiguousarray(x[b].T),
            "xTo": np.ascontiguousarray(x[b].T[:, 1024 * g:1024 * g + 1024]),
            "qwT": np.ascontiguousarray(q_w.T[:, 2048 * g:2048 * g + 2048]),
            "kwT": np.ascontiguousarray(k_w.T[:, 2048 * g:2048 * g + 2048]),
            "vwT": np.ascontiguousarray(v_w.T[:, 2048 * g:2048 * g + 2048]),
            "owT": owT, "fwT": fwT, "sexpT": sexpT, "fbias": fbias,
            "cosT": cosT, "sinT": sinT, "ptT": ptT, "ident": ident,
            "onesm": onesm, "onesb": onesb, "identb": identb,
            "impsc": np.full((1, 1), 1.0 if b == 0 else 0.0, dtype=np.float32),
        })

    if "prog" not in _NC_CACHE:
        _NC_CACHE["prog"] = build_program()
    nc = _NC_CACHE["prog"]
    res = run_bass_kernel_spmd(nc, in_maps, core_ids=list(range(NCORES)))

    out = np.empty((B, S, HID), dtype=np.float32)
    for c in range(NCORES):
        b, g = c // 2, c % 2
        out[b, 1024 * g:1024 * g + 1024, :] = res.results[c]["outT"].T
    return out


# version 06
# speedup vs baseline: 1.3088x; 1.0014x over previous
import numpy as np
import concourse.bass as bass
import concourse.mybir as mybir
from concourse.bass_utils import run_bass_kernel_spmd

F32 = mybir.dt.float32
BF16 = mybir.dt.bfloat16
I32 = mybir.dt.int32
AF = mybir.ActivationFunctionType
ALU = mybir.AluOpType
AXX = mybir.AxisListType.X
DS = bass.ds

B, S, H, DH, HID = 4, 2048, 32, 128, 4096
NCORES = 8
SCALE = 1.0 / float(np.sqrt(128.0))
SHIFT = -30.0
EPS = 1e-5
VIS_LO, VIS_HI = 1, 513
NCH = 4
CH = 512
NFT = 16
NCT = 32

DMA_KEYS = ("c", "x0", "x1", "w0", "w1", "gw0", "gw1", "kv0", "kv1",
            "h0", "h1", "f0", "f1", "p0", "p1", "o0", "o1", "go0", "go1",
            "e0", "e1", "xs", "qk0", "qk1", "v0", "v1", "ob0", "ob1",
            "fs0", "fs1", "fs2", "fs3", "ap0", "ap1", "ou0", "ou1", "gp")
ENG_KEYS = ("pe", "ac", "ve", "gc", "cc")


class Emitter:
    def __init__(self, sems, nc=None):
        self.ops = {k: [] for k in ("sync", "scalar", "vector", "gpsimd", "tensor")}
        self.cnt = {}
        self.last_wait = {k: {} for k in self.ops}
        self.sems = sems
        self.nc = nc

    def mark(self, name):
        if self.nc is None:
            return
        for q in self.ops:
            self.ops[q].append(lambda e, n=name, nc=self.nc: (
                nc.leave_named_scope(), nc.enter_named_scope(n))[-1]
                if getattr(nc, "_scope_open", False) else
                (nc.enter_named_scope(n), setattr(nc, "_scope_open", True))[-1])

    def inc(self, key, n):
        self.cnt[key] = self.cnt.get(key, 0) + n
        return self.cnt[key]

    def val(self, key):
        return self.cnt.get(key, 0)

    def wait(self, eng, key, v=None):
        v = self.val(key) if v is None else v
        if not v or v <= 0:
            return
        lw = self.last_wait[eng]
        if lw.get(key, -1) >= v:
            return
        lw[key] = v
        s = self.sems[key]
        self.ops[eng].append(lambda e, s=s, vv=v: e.wait_ge(s, vv))

    def dma(self, q, out_ap, in_ap, key, slow=False):
        prev = self.val(key)
        if prev:
            self.wait(q, key, prev)
        v = self.inc(key, 16)
        s = self.sems[key]

        def run(e, o=out_ap, i=in_ap, s=s, slow=slow):
            if callable(o):
                o = o()
            if callable(i):
                i = i()
            if slow:
                e.dma_start(out=o, in_=i, allow_slow_non_contiguous=True).then_inc(s, 16)
            else:
                e.dma_start(out=o, in_=i).then_inc(s, 16)
        self.ops[q].append(run)
        return v

    def op(self, eng, fn, inc_key=None, inc_n=1):
        if inc_key is None:
            self.ops[eng].append(fn)
            return None
        v = self.inc(inc_key, inc_n)
        s = self.sems[inc_key]
        self.ops[eng].append(lambda e, f=fn, s=s, n=inc_n: f(e).then_inc(s, n))
        return v


def build_program():
    nc = bass.Bass("TRN2", target_bir_lowering=False, detect_race_conditions=False)

    xT = nc.dram_tensor("xT", [HID, S], F32, kind="ExternalInput").ap()
    qwT = nc.dram_tensor("qwT", [HID, 2048], F32, kind="ExternalInput").ap()
    kwT = nc.dram_tensor("kwT", [HID, 2048], F32, kind="ExternalInput").ap()
    vwT = nc.dram_tensor("vwT", [HID, 2048], F32, kind="ExternalInput").ap()
    owT = nc.dram_tensor("owT", [HID, HID], F32, kind="ExternalInput").ap()
    fwT = nc.dram_tensor("fwT", [HID, HID], I32, kind="ExternalInput").ap()
    sexpT = nc.dram_tensor("sexpT", [HID, HID], F32, kind="ExternalInput").ap()
    fbias = nc.dram_tensor("fbias", [HID, 1], F32, kind="ExternalInput").ap()
    cosT_d = nc.dram_tensor("cosT", [128, S], F32, kind="ExternalInput").ap()
    sinT_d = nc.dram_tensor("sinT", [128, S], F32, kind="ExternalInput").ap()
    ptT_d = nc.dram_tensor("ptT", [128, 128], F32, kind="ExternalInput").ap()
    ident_d = nc.dram_tensor("ident", [128, 128], F32, kind="ExternalInput").ap()
    ones_d = nc.dram_tensor("onesm", [128, 128], F32, kind="ExternalInput").ap()
    onesb_d = nc.dram_tensor("onesb", [128, 128], BF16, kind="ExternalInput").ap()
    identb_d = nc.dram_tensor("identb", [128, 128], BF16, kind="ExternalInput").ap()
    impsc_d = nc.dram_tensor("impsc", [1, 1], F32, kind="ExternalInput").ap()
    xTo = nc.dram_tensor("xTo", [HID, 1024], F32, kind="ExternalInput").ap()
    outT = nc.dram_tensor("outT", [HID, 1024], F32, kind="ExternalOutput").ap()

    qhT_s = nc.dram_tensor("qhT_s", [2048, S], BF16).ap()
    qlT_s = nc.dram_tensor("qlT_s", [2048, S], BF16).ap()
    khT_s = nc.dram_tensor("khT_s", [2048, S], BF16).ap()
    klT_s = nc.dram_tensor("klT_s", [2048, S], BF16).ap()
    vT_s = nc.dram_tensor("vT_s", [2048, S], F32).ap()
    xnhT_s = nc.dram_tensor("xnhT_s", [HID, S], BF16)
    xnlT_s = nc.dram_tensor("xnlT_s", [HID, S], BF16)
    oh0_s = nc.dram_tensor("oh0_s", [1024, S], F32)
    oh1_s = nc.dram_tensor("oh1_s", [1024, S], F32)
    ag0_out = nc.dram_tensor("ag0_out", [8 * 1024, S], F32)
    ag1_out = nc.dram_tensor("ag1_out", [8 * 1024, S], F32)
    ffnT_s = nc.dram_tensor("ffnT_s", [HID, 1024], F32).ap()
    apT_s = nc.dram_tensor("apT_s", [HID, 1024], F32).ap()
    ar_in = nc.dram_tensor("ar_in", [1, 512], F32)
    ar_out = nc.dram_tensor("ar_out", [1, 512], F32)
    kbias_s = nc.dram_tensor("kbias_s", [640, 1], F32).ap()

    FA = nc.alloc_sbuf_tensor("FA", [128, 13824], F32).ap()
    AB = nc.alloc_sbuf_tensor("AB", [128, 53248], BF16).ap()
    AI = nc.alloc_sbuf_tensor("AI", [128, 4096], I32).ap()
    ROWS = nc.alloc_sbuf_tensor("ROWS", [1, 3072], F32).ap()
    cosT = nc.alloc_sbuf_tensor("cosT_sb", [128, S], F32).ap()
    sinT = nc.alloc_sbuf_tensor("sinT_sb", [128, S], F32).ap()
    ptT = nc.alloc_sbuf_tensor("ptT_sb", [128, 128], F32).ap()
    ident = nc.alloc_sbuf_tensor("ident_sb", [128, 128], F32).ap()
    ones = nc.alloc_sbuf_tensor("ones_sb", [128, 128], F32).ap()
    onesb = nc.alloc_sbuf_tensor("onesb_sb", [128, 128], BF16).ap()
    identb = nc.alloc_sbuf_tensor("identb_sb", [128, 128], BF16).ap()
    sumq = nc.alloc_sbuf_tensor("sumq_sb", [128, 16], F32).ap()
    sumqh = nc.alloc_sbuf_tensor("sumqh_sb", [128, 16], BF16).ap()
    sumql = nc.alloc_sbuf_tensor("sumql_sb", [128, 16], BF16).ap()
    sumq_t = nc.alloc_sbuf_tensor("sumq_t", [128, 1], F32).ap()
    kbias = nc.alloc_sbuf_tensor("kbias_sb", [128, 8], F32).ap()
    m30 = nc.alloc_sbuf_tensor("m30_sb", [128, 1], F32).ap()
    impsc = nc.alloc_sbuf_tensor("impsc_sb", [1, 1], F32).ap()
    impcols = nc.alloc_sbuf_tensor("impcols_sb", [128, 4], F32).ap()
    fbias_c = nc.alloc_sbuf_tensor("fbias_sb", [128, 32], F32).ap()
    rankw = nc.alloc_sbuf_tensor("rankw_sb", [128, 8], F32).ap()

    P = [nc.alloc_psum_tensor(f"P{i}", [128, 512], F32).ap() for i in range(8)]
    P6 = P[6][0:1, :]
    P7 = P[7][0:1, :]

    sems, sem_ctx = {}, {}
    for k in DMA_KEYS + ENG_KEYS:
        g = nc.semaphore(k)
        sems[k] = g.__enter__()
        sem_ctx[k] = g
    E = Emitter(sems)

    R = {name: ROWS[0:1, 512 * i:512 * i + 512]
         for i, name in enumerate(("mean", "vpe", "b", "ss", "rstd", "negmr"))}

    ones_col = ones[:, 0:1]
    ones_row = ones[0:1, :]
    onesb_col = onesb[:, 0:1]

    E.dma("sync", cosT[:], cosT_d, "c")
    E.dma("sync", sinT[:], sinT_d, "c")
    E.dma("sync", ptT[:], ptT_d, "c")
    E.dma("sync", ident[:], ident_d, "c")
    E.dma("sync", ones[:], ones_d, "c")
    E.dma("sync", onesb[:], onesb_d, "c")
    E.dma("sync", identb[:], identb_d, "c")
    E.dma("sync", impsc[:], impsc_d, "c")
    consts_ld = E.dma("sync", fbias_c[:],
                      fbias[:, 0:1].rearrange("(o p) one -> p (o one)", p=128), "c",
                      slow=True)
    E.wait("vector", "c", consts_ld)
    E.op("vector", lambda e: e.memset(sumq[:], 0.0))
    E.op("vector", lambda e: e.memset(m30[:], SHIFT), "ve")

    pidbox = {}

    def gp_init(e):
        pid = nc.partition_id([mybir.EngineType.Pool])
        g = pid % 2
        pidbox["goff"] = g * 1024
        for half in range(2):
            pidbox[f"rbA{half}"] = (pid - g + half) * 1024
            for kq in range(4):
                pidbox[f"go{half}_{kq}"] = g * 1024 + 256 * kq
        return e.memset(rankw[:, 7:8], 0.0)
    E.op("gpsimd", gp_init, "gc")

    xtile = [FA[:, 0:512], FA[:, 512:1024]]
    wslab = [FA[:, 1024:5120], FA[:, 1024:5120]]
    vout = [FA[:, 5120:5632], FA[:, 5632:6144]]
    traw = [FA[:, 6144:6656], FA[:, 6656:7168]]
    t1b = FA[:, 7168:7680]
    t2b = FA[:, 7680:8192]
    qrout = [FA[:, 8192:8704], FA[:, 8704:9216]]
    xsq = [FA[:, 9216:9728], FA[:, 9728:10240]]
    bc_rstd = FA[:, 10240:10752]
    bc_negmr = FA[:, 10752:11264]

    xnh = AB[:, 0:16384]
    xnl = AB[:, 16384:32768]
    wsh = [AB[:, 32768:36864], AB[:, 36864:40960]]
    wsl = [AB[:, 40960:45056], AB[:, 45056:49152]]
    qrh = [AB[:, 49152:49664], AB[:, 49664:50176]]
    qrl = [AB[:, 50176:50688], AB[:, 50688:51200]]

    slab_mm = {}
    slab_sub = {}
    qr_store = {}
    v_store = {}
    evac_ac = {}
    evac_ve = {}
    traw_free = {}
    rot_free = {}
    pending_rot = []
    prev_nm_v = 0
    prev_bc_cp = 0
    gi = 0
    xn_split_done = 0

    def flush_rot(item):
        (rgi, rj, rft, rqp, rev, rdsth, rdstl) = item
        rp = rgi % 2
        qkey = f"qk{rqp}"
        E.wait("tensor", "ac", rev)
        E.wait("tensor", "ve", rot_free.get(rp, 0))
        rot = E.op("tensor", lambda e, rqp=rqp, rp=rp: e.matmul(
            P[2 + rp][:], ptT[:], traw[rqp][:], start=True, stop=True), "pe")
        E.wait("vector", "pe", rot)
        if rqp in qr_store:
            E.wait("vector", qr_store[rqp][0], qr_store[rqp][1])
        tv = E.op("vector", lambda e, rqp=rqp, rj=rj: e.tensor_mul(
            t1b[:], traw[rqp][:], cosT[:, CH * rj:CH * rj + CH]), "ve")
        traw_free[rqp] = tv
        rused = E.op("vector", lambda e, rp=rp, rj=rj: e.tensor_mul(
            t2b[:], P[2 + rp][:], sinT[:, CH * rj:CH * rj + CH]), "ve")
        rot_free[rp] = rused
        rope_v = E.op("vector", lambda e, rqp=rqp: e.tensor_add(
            qrout[rqp][:], t1b[:], t2b[:]), "ve")
        if rdstl is qlT_s and rj >= 1:
            lo = {1: 1, 2: 0, 3: 0}[rj]
            hi = {1: 512, 2: 512, 3: 456}[rj]
            rs_v = E.op("vector", lambda e, rqp=rqp, lo=lo, hi=hi: e.reduce_sum(
                sumq_t[:], qrout[rqp][:, lo:hi], axis=AXX), "ve")
            E.wait("vector", "ve", rs_v)
            acc_v = E.op("vector", lambda e, rft=rft: e.tensor_add(
                sumq[:, rft:rft + 1], sumq[:, rft:rft + 1], sumq_t[:]), "ve")
            E.wait("vector", "ve", acc_v)
        E.wait("scalar", "ve", rope_v)
        qh_v = E.op("scalar", lambda e, rqp=rqp: e.copy(
            qrh[rqp][:], qrout[rqp][:]), "ac")
        E.wait("vector", "ac", qh_v)
        ql_v = E.op("vector", lambda e, rqp=rqp: e.tensor_sub(
            qrl[rqp][:], qrout[rqp][:], qrh[rqp][:]), "ve")
        E.wait("scalar", "ve", ql_v)
        E.dma("scalar", rdsth[128 * rft:128 * rft + 128, CH * rj:CH * rj + CH],
              qrh[rqp][:], qkey)
        st2 = E.dma("scalar", rdstl[128 * rft:128 * rft + 128, CH * rj:CH * rj + CH],
                    qrl[rqp][:], qkey)
        qr_store[rqp] = (qkey, st2)

    for j in range(NCH):
        E.wait("tensor", "ve", prev_nm_v)
        E.wait("sync", "gc", E.val("gc"))
        sq_vals, ssq_vals = [], []
        for c in range(NCT):
            par = c % 2
            xk = f"x{par}"
            if c >= 2:
                E.wait("sync", "pe", ssq_vals[c - 2])
            xld = E.dma("sync", xtile[par][:],
                        xT.rearrange("(c p) t -> p c t", p=128)[
                            :, c, CH * j:CH * j + CH], xk)
            E.wait("scalar", xk, xld)
            if c >= 2:
                E.wait("scalar", "pe", ssq_vals[c - 2])
            sq_vals.append(E.op("scalar", lambda e, par=par: e.activation(
                xsq[par][:], xtile[par][:], AF.Square), "ac"))
            E.wait("tensor", xk, xld)
            E.op("tensor", lambda e, par=par, c=c: e.matmul(
                P6[:], ones_col, xtile[par][:],
                start=(c == 0), stop=(c == NCT - 1)))
            E.wait("tensor", "ac", sq_vals[c])
            ssq_vals.append(E.op("tensor", lambda e, par=par, c=c: e.matmul(
                P7[:], ones_col, xsq[par][:],
                start=(c == 0), stop=(c == NCT - 1)), "pe"))
        E.wait("vector", "pe", ssq_vals[-1])
        E.op("vector", lambda e: e.tensor_scalar(R["mean"], P6[:], 1.0 / HID, None, ALU.mult))
        E.op("vector", lambda e: e.tensor_scalar(R["vpe"], P7[:], 1.0 / HID, None, ALU.mult))
        E.op("vector", lambda e: e.tensor_mul(R["b"], R["mean"], R["mean"]))
        E.op("vector", lambda e: e.scalar_tensor_tensor(
            R["vpe"], R["vpe"], EPS, R["b"], ALU.add, ALU.subtract))
        rr_v = E.op("vector", lambda e: e.reciprocal(R["b"], R["vpe"]), "ve")
        E.wait("scalar", "ve", rr_v)
        ss_v = E.op("scalar", lambda e: e.activation(R["ss"], R["b"], AF.Sqrt), "ac")
        E.wait("vector", "ac", ss_v)
        E.op("vector", lambda e: e.tensor_mul(R["b"], R["ss"], R["ss"]))
        E.op("vector", lambda e: e.tensor_mul(R["b"], R["b"], R["vpe"]))
        E.op("vector", lambda e: e.tensor_scalar(R["b"], R["b"], -0.5, 1.5, ALU.mult, ALU.add))
        E.op("vector", lambda e: e.tensor_mul(R["rstd"], R["ss"], R["b"]))
        E.op("vector", lambda e: e.tensor_mul(R["negmr"], R["mean"], R["rstd"]))
        nm_v = E.op("vector", lambda e: e.tensor_scalar(
            R["negmr"], R["negmr"], -1.0, None, ALU.mult), "ve")
        prev_nm_v = nm_v
        E.wait("tensor", "ve", nm_v)
        E.wait("tensor", "ac", prev_bc_cp)
        E.wait("tensor", "ve", max(rot_free.get(0, 0), rot_free.get(1, 0)))
        bc_mm = E.op("tensor", lambda e: (
            e.matmul(P[2][:], ones_row, R["rstd"], start=True, stop=True),
            e.matmul(P[3][:], ones_row, R["negmr"], start=True, stop=True))[-1], "pe")
        E.wait("scalar", "pe", bc_mm)
        E.op("scalar", lambda e: e.copy(bc_rstd[:], P[2][:]))
        bc_cp = E.op("scalar", lambda e: e.copy(bc_negmr[:], P[3][:]), "ac")
        prev_bc_cp = bc_cp
        E.wait("vector", "ac", bc_cp)

        E.wait("sync", "pe", ssq_vals[-1])
        pe_cum = E.val("pe")
        E.wait("scalar", "pe", pe_cum)
        E.wait("vector", "pe", pe_cum)
        E.wait("gpsimd", "pe", pe_cum)
        E.wait("scalar", "xs", E.val("xs"))
        E.wait("vector", "xs", E.val("xs"))
        E.wait("gpsimd", "xs", E.val("xs"))
        E.wait("gpsimd", "ac", E.val("ac"))
        xn_subs = []
        for c in range(NCT):
            par = c % 2
            eng = "vector" if par == 0 else "gpsimd"
            ekey = "ve" if par == 0 else "gc"
            xk = f"x{par}"
            if c >= 2:
                E.wait("sync", ekey, xn_subs[c - 2])
            xld = E.dma("sync", xtile[par][:],
                        xT.rearrange("(c p) t -> p c t", p=128)[
                            :, c, CH * j:CH * j + CH], xk)
            E.wait(eng, xk, xld)
            E.op(eng, lambda e, par=par: e.tensor_mul(
                xtile[par][:], xtile[par][:], bc_rstd[:]))
            aff_v = E.op(eng, lambda e, par=par: e.tensor_add(
                xtile[par][:], xtile[par][:], bc_negmr[:]), ekey)
            E.wait("scalar", ekey, aff_v)
            hc_v = E.op("scalar", lambda e, par=par, c=c: e.copy(
                xnh[:, CH * c:CH * c + CH], xtile[par][:]), "ac")
            E.wait(eng, "ac", hc_v)
            xn_subs.append(E.op(eng, lambda e, par=par, c=c: e.tensor_sub(
                xnl[:, CH * c:CH * c + CH], xtile[par][:],
                xnh[:, CH * c:CH * c + CH]), ekey))
        xn_split_done = xn_subs[-2]
        xn_split_gc = xn_subs[-1]

        E.wait("scalar", "ve", xn_split_done)
        E.wait("scalar", "gc", xn_split_gc)
        E.dma("scalar",
              xnhT_s.ap().rearrange("(c p) t -> p c t", p=128)[:, :, CH * j:CH * j + CH],
              xnh.rearrange("p (c t) -> p c t", t=CH), "xs")
        xs_st = E.dma("scalar",
                      xnlT_s.ap().rearrange("(c p) t -> p c t", p=128)[:, :, CH * j:CH * j + CH],
                      xnl.rearrange("p (c t) -> p c t", t=CH), "xs")

        groups = [(proj, wsrc, dsth, dstl)
                  for proj, wsrc, dsth, dstl in (("q", qwT, qhT_s, qlT_s),
                                                 ("k", kwT, khT_s, klT_s),
                                                 ("v", vwT, vT_s, None))
                  for _ft in range(NFT)]

        def prepA(gidx, ft):
            par = gidx % 2
            wk, gk = f"w{par}", f"gw{par}"
            wsrc_ap = groups[ft[0]][1].rearrange("(c p) f -> p c f", p=128)[
                :, :, 128 * ft[1]:128 * ft[1] + 128]
            if gidx >= 1:
                E.wait("sync", "ve", slab_sub.get(gidx - 1, 0))
            if gidx >= 2:
                E.wait("gpsimd", "pe", slab_mm.get(gidx - 2, 0))
            wld = E.dma("sync", wslab[par].rearrange("p (c f) -> p c f", f=128),
                        wsrc_ap, wk)
            gld = E.dma("gpsimd", wsh[par].rearrange("p (c f) -> p c f", f=128),
                        wsrc_ap, gk)
            E.wait("vector", wk, wld)
            E.wait("vector", gk, gld)
            E.wait("vector", "pe", slab_mm.get(gidx - 2, 0))
            dq_v = E.op("vector", lambda e, par=par: e.tensor_sub(
                wsl[par][:], wslab[par][:], wsh[par][:]), "ve")
            slab_sub[gidx] = dq_v

        base_gi = gi
        prepA(base_gi, (0, 0))
        for idx in range(len(groups)):
            proj, wsrc, dsth, dstl = groups[idx]
            ft = idx % NFT
            if idx + 1 < len(groups):
                prepA(base_gi + idx + 1, (idx + 1, (idx + 1) % NFT))
            par = gi % 2
            E.wait("tensor", "ve", slab_sub[gi])
            E.wait("tensor", "ve", xn_split_done)
            E.wait("tensor", "gc", xn_split_gc)
            E.wait("tensor", "ac", evac_ac.get(gi - 2, 0))
            pv = None
            for c in range(NCT):
                ik = "pe" if c == NCT - 1 else None
                pv = E.op("tensor", lambda e, c=c, par=par: (
                    e.matmul(P[par][:], wsh[par][:, 128 * c:128 * c + 128],
                             xnh[:, CH * c:CH * c + CH],
                             start=(c == 0), stop=False),
                    e.matmul(P[par][:], wsh[par][:, 128 * c:128 * c + 128],
                             xnl[:, CH * c:CH * c + CH],
                             start=False, stop=False),
                    e.matmul(P[par][:], wsl[par][:, 128 * c:128 * c + 128],
                             xnh[:, CH * c:CH * c + CH],
                             start=False, stop=(c == NCT - 1)))[-1], ik)
            slab_mm[gi] = pv
            if proj == "v":
                vp = gi % 2
                vkey = f"v{vp}"
                E.wait("scalar", "pe", pv)
                if vp in v_store:
                    E.wait("scalar", vkey, v_store[vp])
                hv = E.op("scalar", lambda e, vp=vp, par=par: e.copy(
                    vout[vp][:], P[par][:]), "ac")
                evac_ac[gi] = hv
                v_store[vp] = E.dma(
                    "scalar", dsth[128 * ft:128 * ft + 128, CH * j:CH * j + CH],
                    vout[vp][:], vkey)
            else:
                qp = gi % 2
                E.wait("scalar", "pe", pv)
                E.wait("scalar", "ve", traw_free.get(qp, 0))
                ev = E.op("scalar", lambda e, qp=qp, par=par: e.copy(
                    traw[qp][:], P[par][:]), "ac")
                evac_ac[gi] = ev
                pending_rot.append((gi, j, ft, qp, ev, dsth, dstl))
            if len(pending_rot) > 1:
                flush_rot(pending_rot.pop(0))
            gi += 1
        while pending_rot:
            flush_rot(pending_rot.pop(0))

    kvh = [AB[:, 32768 + i * 512:32768 + i * 512 + 512] for i in range(2)]
    kvl = [AB[:, 33792 + i * 512:33792 + i * 512 + 512] for i in range(2)]
    for kk in ("qk0", "qk1", "v0", "v1", "xs"):
        E.wait("sync", kk, E.val(kk))
    E.wait("sync", "pe", E.val("pe"))
    sq_all = E.val("ve")
    E.wait("scalar", "ve", sq_all)
    sqh_v = E.op("scalar", lambda e: e.copy(sumqh[:], sumq[:]), "ac")
    E.wait("vector", "ac", sqh_v)
    sql_v = E.op("vector", lambda e: e.tensor_sub(sumql[:], sumq[:], sumqh[:]), "ve")
    E.wait("vector", "ve", sql_v)
    imp_vals = []
    for h in range(NFT):
        kvk = f"kv{h % 2}"
        if h >= 2:
            E.wait("sync", "pe", imp_vals[h - 2])
        E.dma("sync", kvh[h % 2][:], khT_s[128 * h:128 * h + 128, VIS_LO:VIS_HI], kvk)
        kvld = E.dma("sync", kvl[h % 2][:], klT_s[128 * h:128 * h + 128, VIS_LO:VIS_HI], kvk)
        E.wait("tensor", kvk, kvld)
        E.wait("tensor", "ve", sql_v)
        imp_vals.append(E.op("tensor", lambda e, h=h: (
            e.matmul(P6[:], sumqh[:, h:h + 1], kvh[h % 2][:],
                     start=(h == 0), stop=False),
            e.matmul(P6[:], sumqh[:, h:h + 1], kvl[h % 2][:],
                     start=False, stop=False),
            e.matmul(P6[:], sumql[:, h:h + 1], kvh[h % 2][:],
                     start=False, stop=(h == NFT - 1)))[-1], "pe"))
    E.wait("vector", "pe", imp_vals[-1])
    imp_v = E.op("vector", lambda e: e.tensor_scalar(
        R["mean"], P6[:], impsc[0:1, 0:1], None, ALU.mult), "ve")
    E.wait("gpsimd", "ve", imp_v)
    E.dma("gpsimd", ar_in.ap(), R["mean"], "gp")
    E.wait("gpsimd", "gp")
    ar_done = E.op("gpsimd", lambda e: e.collective_compute(
        "AllReduce", ALU.add, replica_groups=[list(range(8))],
        ins=[ar_in.ap().opt()], outs=[ar_out.ap().opt()]), "cc")
    E.wait("gpsimd", "cc", ar_done)
    impr_ld = E.dma("gpsimd", R["mean"], ar_out.ap(), "gp")
    impc_ld = E.dma("gpsimd", impcols[:],
                    ar_out.ap().rearrange("one (v p) -> (one p) v", p=128), "gp",
                    slow=True)
    E.wait("tensor", "gp", impc_ld)
    bcimp = E.op("tensor", lambda e: e.matmul(
        P[2][:], ones_row, R["mean"], start=True, stop=True), "pe")
    E.wait("scalar", "pe", bcimp)
    impbc = FA[:, 1024:1536]
    onestile = FA[:, 1536:2048]
    cgt = FA[:, 2048:2560]
    ceq = FA[:, 2560:3072]
    ultv = FA[:, 3072:3584]
    impbc_cp = E.op("scalar", lambda e: e.copy(impbc[:], P[2][:]), "ac")
    E.wait("vector", "ac", impbc_cp)
    ones512_v = E.op("vector", lambda e: e.memset(onestile[:], 1.0), "ve")
    E.wait("gpsimd", "ve", ones512_v)
    E.wait("vector", "gp", impc_ld)
    for vt in range(4):
        asel_v = E.op("gpsimd", lambda e, vt=vt: e.affine_select(
            ultv[:], onestile[:], pattern=[[-1, 512]], compare_op=ALU.is_gt,
            fill=0.0, base=128 * vt, channel_multiplier=1), "gc")
        E.op("vector", lambda e, vt=vt: e.tensor_scalar(
            cgt[:], impbc[:], impcols[:, vt:vt + 1], None, ALU.is_gt))
        E.op("vector", lambda e, vt=vt: e.tensor_scalar(
            ceq[:], impbc[:], impcols[:, vt:vt + 1], None, ALU.is_equal))
        E.wait("vector", "gc", asel_v)
        E.op("vector", lambda e: e.tensor_mul(ceq[:], ceq[:], ultv[:]))
        E.op("vector", lambda e: e.tensor_add(cgt[:], cgt[:], ceq[:]))
        v1 = E.op("vector", lambda e: e.reduce_sum(rankw[:, 0:1], cgt[:], axis=AXX), "ve")
        E.wait("vector", "ve", v1)
        v2 = E.op("vector", lambda e: e.tensor_scalar(
            rankw[:, 1:2], rankw[:, 0:1], 255.5, None, ALU.is_lt), "ve")
        E.wait("vector", "ve", v2)
        v3 = E.op("vector", lambda e: e.tensor_scalar(
            rankw[:, 2:3], rankw[:, 1:2], 1e30, -1e30, ALU.mult, ALU.add), "ve")
        E.wait("vector", "ve", v3)
        kb_v = E.op("vector", lambda e, vt=vt: e.tensor_scalar(
            kbias[:, vt:vt + 1], rankw[:, 2:3], SHIFT, None, ALU.add), "ve")
        E.wait("vector", "ve", kb_v)
        E.wait("gpsimd", "ve", kb_v)
    E.dma("gpsimd", kbias_s[0:1, 0:1], m30[0:1, 0:1], "gp")
    E.dma("gpsimd",
          kbias_s[1:513, 0:1].rearrange("(v p) one -> p (v one)", p=128),
          kbias[:, 0:4], "gp", slow=True)
    E.dma("gpsimd", kbias_s[513:640, 0:1], m30[0:127, 0:1], "gp")
    kb_ld = None
    for kt in range(5):
        kb_ld = E.dma("gpsimd", kbias[:, kt:kt + 1],
                      kbias_s[128 * kt:128 * kt + 128, 0:1], "gp")

    def hb(p, i):
        return AB[:, 12288 * p + 2048 * i:12288 * p + 2048 * i + 2048]
    kqv = {nm: [hb(0, i), hb(1, i)] for i, nm in
           enumerate(("kh", "kl", "qh", "ql", "vth", "vtl"))}
    vfT = [FA[:, 9728:11776], FA[:, 11776:13824]]
    phb = AB[:, 32768:40960]
    plb = AB[:, 40960:49152]
    pbuf = FA[:, 0:8192]
    outsb = [FA[:, 8192:8704], FA[:, 8704:9216]]
    rbc = FA[:, 9216:9728]
    rrow = R["b"]

    head_pe_done = {}
    out_store = {}
    prev_onorm = 0
    prev_rbc_cp = 0
    E.wait("scalar", "gp", kb_ld)
    E.wait("sync", "pe", E.val("pe"))
    for h in range(NFT):
        par = h % 2
        hkey = f"h{par}"
        if h >= 2:
            E.wait("sync", "pe", head_pe_done[h - 2])
        for nm, src in (("kh", khT_s), ("kl", klT_s), ("qh", qhT_s), ("ql", qlT_s)):
            E.dma("sync", kqv[nm][par][:], src[128 * h:128 * h + 128, :], hkey)
        ldall = E.dma("sync", vfT[par][:], vT_s[128 * h:128 * h + 128, :], hkey)
        E.wait("tensor", hkey, ldall)
        cp_vals = []
        sub_vals = []
        for kt in range(16):
            pb = P[4 + (kt % 2)]
            if kt >= 2:
                E.wait("tensor", "ac", cp_vals[kt - 2])
                E.wait("tensor", "ve", sub_vals[kt - 2])
            tr = E.op("tensor", lambda e, kt=kt, par=par, pb=pb: e.transpose(
                pb[:, 0:128], vfT[par][:, 128 * kt:128 * kt + 128], ident[:]), "pe")
            E.wait("scalar", "pe", tr)
            cp_vals.append(E.op("scalar", lambda e, kt=kt, par=par, pb=pb: e.copy(
                kqv["vth"][par][:, 128 * kt:128 * kt + 128], pb[:, 0:128]), "ac"))
            E.wait("vector", "ac", cp_vals[-1])
            sub_vals.append(E.op("vector", lambda e, kt=kt, par=par, pb=pb: e.tensor_sub(
                kqv["vtl"][par][:, 128 * kt:128 * kt + 128], pb[:, 0:128],
                kqv["vth"][par][:, 128 * kt:128 * kt + 128]), "ve"))
        vt_ready = cp_vals[-1]
        vt_readyv = sub_vals[-1]
        for j in range(NCH):
            nkt = 4 * (j + 1)
            E.wait("scalar", "gc", E.val("gc"))
            exp_done = {}
            psplit = {}
            for kt in range(nkt):
                sb = [P[0], P[1], P[7]][kt % 3]
                if kt >= 3:
                    E.wait("tensor", "ac", exp_done[kt - 3])
                smm = E.op("tensor", lambda e, kt=kt, j=j, par=par, sb=sb: (
                    e.matmul(sb[:], kqv["kh"][par][:, 128 * kt:128 * kt + 128],
                             kqv["qh"][par][:, CH * j:CH * j + CH],
                             start=True, stop=False),
                    e.matmul(sb[:], kqv["kh"][par][:, 128 * kt:128 * kt + 128],
                             kqv["ql"][par][:, CH * j:CH * j + CH],
                             start=False, stop=False),
                    e.matmul(sb[:], kqv["kl"][par][:, 128 * kt:128 * kt + 128],
                             kqv["qh"][par][:, CH * j:CH * j + CH],
                             start=False, stop=True))[-1], "pe")
                E.wait("scalar", "pe", smm)
                bias = kbias[:, kt:kt + 1] if kt <= 4 else m30[:, 0:1]
                exp_done[kt] = E.op("scalar", lambda e, kt=kt, sb=sb, bias=bias:
                                    e.activation(pbuf[:, CH * kt:CH * kt + CH], sb[:],
                                                 AF.Exp, bias=bias, scale=SCALE), "ac")
            asel = {}
            for kt in range(4 * j, nkt):
                E.wait("gpsimd", "ac", exp_done[kt])
                asel[kt] = E.op("gpsimd", lambda e, kt=kt, j=j: e.affine_select(
                    pbuf[:, CH * kt:CH * kt + CH], pbuf[:, CH * kt:CH * kt + CH],
                    pattern=[[1, 512]], compare_op=ALU.is_ge, fill=0.0,
                    base=CH * j - 128 * kt, channel_multiplier=-1), "gc")
            for kt in range(nkt):
                E.wait("scalar", "ac", exp_done[kt])
                if kt in asel:
                    E.wait("scalar", "gc", asel[kt])
                hv = E.op("scalar", lambda e, kt=kt: e.copy(
                    phb[:, CH * kt:CH * kt + CH], pbuf[:, CH * kt:CH * kt + CH]), "ac")
                E.wait("vector", "ac", hv)
                psplit[kt] = E.op("vector", lambda e, kt=kt: e.tensor_sub(
                    plb[:, CH * kt:CH * kt + CH], pbuf[:, CH * kt:CH * kt + CH],
                    phb[:, CH * kt:CH * kt + CH]), "ve")
            E.wait("tensor", "ac", vt_ready)
            E.wait("tensor", "ve", vt_readyv)
            E.wait("tensor", "ve", prev_onorm)
            pvmm = None
            for kt in range(nkt):
                E.wait("tensor", "ve", psplit[kt])
                E.op("tensor", lambda e, kt=kt, par=par, nkt=nkt: (
                    e.matmul(P[2][:], kqv["vth"][par][:, 128 * kt:128 * kt + 128],
                             phb[:, CH * kt:CH * kt + CH],
                             start=(kt == 0), stop=False),
                    e.matmul(P[2][:], kqv["vth"][par][:, 128 * kt:128 * kt + 128],
                             plb[:, CH * kt:CH * kt + CH],
                             start=False, stop=False),
                    e.matmul(P[2][:], kqv["vtl"][par][:, 128 * kt:128 * kt + 128],
                             phb[:, CH * kt:CH * kt + CH],
                             start=False, stop=(kt == nkt - 1)))[-1])
                pvmm = E.op("tensor", lambda e, kt=kt, nkt=nkt: (
                    e.matmul(P6[:], onesb_col, phb[:, CH * kt:CH * kt + CH],
                             start=(kt == 0), stop=False),
                    e.matmul(P6[:], onesb_col, plb[:, CH * kt:CH * kt + CH],
                             start=False, stop=(kt == nkt - 1)))[-1], "pe")
            E.wait("vector", "pe", pvmm)
            rec_v = E.op("vector", lambda e: e.reciprocal(rrow, P6[:]), "ve")
            E.wait("tensor", "ve", rec_v)
            E.wait("tensor", "ac", prev_rbc_cp)
            bc2 = E.op("tensor", lambda e: e.matmul(
                P[3][:], ones_row, rrow, start=True, stop=True), "pe")
            E.wait("scalar", "pe", bc2)
            rbc_cp = E.op("scalar", lambda e: e.copy(rbc[:], P[3][:]), "ac")
            prev_rbc_cp = rbc_cp
            opar = (h * NCH + j) % 2
            obkey = f"ob{opar}"
            E.wait("vector", "ac", rbc_cp)
            if opar in out_store:
                E.wait("vector", obkey, out_store[opar])
            onorm = E.op("vector", lambda e, opar=opar: e.tensor_mul(
                outsb[opar][:], P[2][:], rbc[:]), "ve")
            prev_onorm = onorm
            E.wait("scalar", "ve", onorm)
            ohdst = oh0_s if h < 8 else oh1_s
            out_store[opar] = E.dma(
                "scalar", ohdst.ap()[128 * (h % 8):128 * (h % 8) + 128,
                                     CH * j:CH * j + CH],
                outsb[opar][:], obkey)
        head_pe_done[h] = E.val("pe")
        if h == 7:
            E.wait("gpsimd", "ob0", E.val("ob0"))
            E.wait("gpsimd", "ob1", E.val("ob1"))
            ag0_done = E.op("gpsimd", lambda e: e.collective_compute(
                "AllGather", ALU.bypass, replica_groups=[list(range(8))],
                ins=[oh0_s.ap().opt()], outs=[ag0_out.ap().opt()]), "cc")

    b_pe = E.val("pe")

    xnhh = AB[:, 0:16384]
    xnlh = AB[:, 16384:32768]
    wdh = [AB[:, 32768:34816], AB[:, 34816:36864]]
    wdl = [AB[:, 36864:38912], AB[:, 38912:40960]]
    fw = [AI[:, 0:2048], AI[:, 2048:4096]]
    sxp = [FA[:, 0:2048], FA[:, 2048:4096]]
    wdf = [FA[:, 4096:6144], FA[:, 6144:8192]]
    fout = [FA[:, 8192:8704], FA[:, 8704:9216]]
    fpart = [FA[:, 9216:9728], FA[:, 9728:10240]]

    E.wait("vector", "ob0", E.val("ob0"))
    E.wait("vector", "ob1", E.val("ob1"))
    E.wait("gpsimd", "ob0", E.val("ob0"))
    E.wait("gpsimd", "ob1", E.val("ob1"))
    E.wait("gpsimd", "pe", b_pe)
    E.wait("sync", "ac", E.val("ac"))
    E.wait("sync", "ve", E.val("ve"))
    E.wait("sync", "gc", E.val("gc"))
    xn_ld0a = E.dma(
        "gpsimd", xnhh.rearrange("p (i t) -> p i t", t=1024),
        lambda: xnhT_s.ap().rearrange("(i p) t -> p i t", p=128)[
            :, 0:16, DS(pidbox["goff"], 1024)], "gp")
    xn_ld0 = E.dma(
        "gpsimd", xnlh.rearrange("p (i t) -> p i t", t=1024),
        lambda: xnlT_s.ap().rearrange("(i p) t -> p i t", p=128)[
            :, 0:16, DS(pidbox["goff"], 1024)], "gp")
    ag_done = E.op("gpsimd", lambda e: e.collective_compute(
        "AllGather", ALU.bypass, replica_groups=[list(range(8))],
        ins=[oh1_s.ap().opt()], outs=[ag1_out.ap().opt()]), "cc")

    wd_mm = {}
    dq_done = {}
    fout_store = {}
    fp_ev = {}
    fpart_used = {}
    hC_prev = {}

    def prepD(d):
        par = d % 2
        fkey = f"f{par}"
        dh, dot = d // 32, d % 32
        if d >= 2:
            E.wait("sync", "ve", dq_done[d - 2])
        E.dma("sync", fw[par].rearrange("p (i o) -> p i o", o=128),
              fwT.rearrange("(i p) o -> p i o", p=128)[
                  :, 16 * dh:16 * dh + 16, 128 * dot:128 * dot + 128], fkey)
        sxld = E.dma("sync", sxp[par].rearrange("p (i o) -> p i o", o=128),
                     sexpT.rearrange("(i p) o -> p i o", p=128)[
                         :, 16 * dh:16 * dh + 16, 128 * dot:128 * dot + 128], fkey)
        E.wait("vector", fkey, sxld)
        if d >= 2:
            E.wait("vector", "pe", wd_mm[d - 2])
            E.wait("vector", "ac", hC_prev.get(par, 0))
        E.op("vector", lambda e, par=par: e.tensor_copy(wdf[par][:], fw[par][:]))
        mul_v = E.op("vector", lambda e, par=par: e.tensor_mul(
            wdf[par][:], wdf[par][:], sxp[par][:]), "ve")
        E.wait("scalar", "ve", mul_v)
        E.wait("scalar", "pe", wd_mm[d - 2] if d >= 2 else b_pe)
        hC = E.op("scalar", lambda e, par=par: e.copy(wdh[par][:], wdf[par][:]), "ac")
        hC_prev[par] = hC
        E.wait("vector", "ac", hC)
        dq_done[d] = E.op("vector", lambda e, par=par: e.tensor_sub(
            wdl[par][:], wdf[par][:], wdh[par][:]), "ve")

    di = 0
    prepD(0)
    for half in range(2):
        if half == 1:
            E.wait("gpsimd", "pe", wd_mm[di - 1])
            E.dma("gpsimd", xnhh.rearrange("p (i t) -> p i t", t=1024),
                  lambda: xnhT_s.ap().rearrange("(i p) t -> p i t", p=128)[
                      :, 16:32, DS(pidbox["goff"], 1024)], "gp")
            xn_ld1 = E.dma(
                "gpsimd", xnlh.rearrange("p (i t) -> p i t", t=1024),
                lambda: xnlT_s.ap().rearrange("(i p) t -> p i t", p=128)[
                    :, 16:32, DS(pidbox["goff"], 1024)], "gp")
            E.wait("sync", "fs0", E.val("fs0"))
            E.wait("sync", "fs1", E.val("fs1"))
        for ot in range(32):
            par = di % 2
            if di + 1 < 64:
                prepD(di + 1)
            E.wait("tensor", "ve", dq_done[di])
            if di == 0:
                E.wait("tensor", "gp", xn_ld0)
            if half == 1 and ot == 0:
                E.wait("tensor", "gp", xn_ld1)
            bb = 2 * (di % 2)
            for tc in range(2):
                E.wait("tensor", "ve", fp_ev.get(bb + tc, 0))
            mmv = None
            for i in range(16):
                for tc in range(2):
                    ik = "pe" if (i == 15 and tc == 1) else None
                    mmv = E.op("tensor", lambda e, i=i, tc=tc, par=par, bb=bb: (
                        e.matmul(P[bb + tc][:], wdh[par][:, 128 * i:128 * i + 128],
                                 xnhh[:, 1024 * i + CH * tc:1024 * i + CH * tc + CH],
                                 start=(i == 0), stop=False),
                        e.matmul(P[bb + tc][:], wdh[par][:, 128 * i:128 * i + 128],
                                 xnlh[:, 1024 * i + CH * tc:1024 * i + CH * tc + CH],
                                 start=False, stop=False),
                        e.matmul(P[bb + tc][:], wdl[par][:, 128 * i:128 * i + 128],
                                 xnhh[:, 1024 * i + CH * tc:1024 * i + CH * tc + CH],
                                 start=False, stop=(i == 15)))[-1], ik)
            wd_mm[di] = mmv
            for tc in range(2):
                skey = f"fs{tc}" if half == 0 else f"fs{2 + tc}"
                E.wait("vector", "pe", mmv)
                if half == 0:
                    prev = fout_store.get(tc)
                    if prev is not None:
                        E.wait("vector", prev[0], prev[1])
                    ev = E.op("vector", lambda e, tc=tc, bb=bb: e.tensor_copy(
                        fout[tc][:], P[bb + tc][:]), "ve")
                else:
                    E.wait("sync", "ve", fpart_used.get(tc, 0))
                    pld = E.dma("sync", fpart[tc][:],
                                ffnT_s[128 * ot:128 * ot + 128, CH * tc:CH * tc + CH],
                                f"p{tc}")
                    E.wait("vector", f"p{tc}", pld)
                    prev = fout_store.get(tc)
                    if prev is not None:
                        E.wait("vector", prev[0], prev[1])
                    E.op("vector", lambda e, tc=tc, ot=ot, bb=bb: e.tensor_scalar(
                        fout[tc][:], P[bb + tc][:], fbias_c[:, ot:ot + 1], None, ALU.add))
                    ev = E.op("vector", lambda e, tc=tc: e.tensor_add(
                        fout[tc][:], fout[tc][:], fpart[tc][:]), "ve")
                    fpart_used[tc] = ev
                fp_ev[bb + tc] = ev
                E.wait("scalar", "ve", ev)
                fout_store[tc] = (skey, E.dma(
                    "scalar", ffnT_s[128 * ot:128 * ot + 128, CH * tc:CH * tc + CH],
                    fout[tc][:], skey))
            di += 1
    d1_pe = E.val("pe")

    ohh = AB[:, 0:16384]
    ohl = AB[:, 16384:32768]
    owh = [AB[:, 32768:34816], AB[:, 34816:36864]]
    owl = [AB[:, 36864:38912], AB[:, 38912:40960]]
    ohstrip = FA[:, 0:4096]
    owf = [FA[:, 4096:6144], FA[:, 6144:8192]]
    extr = FA[:, 8192:11264]
    oout = [FA[:, 11264:11776], FA[:, 11776:12288]]

    def extile(kind, par):
        base = 8192 + {"x": 0, "ffn": 1024, "ap": 2048}[kind] + 512 * par
        return FA[:, base:base + 512]

    oslab_done = {}
    osub_done = {}
    oo_store = {}
    oev = {}
    ext_used = {}

    def prepO(o):
        par = o % 2
        okey, gok = f"o{par}", f"go{par}"
        oh_, oot = o // 32, o % 32
        owT_ap = owT.rearrange("(f p) o -> p f o", p=128)[
            :, 16 * oh_:16 * oh_ + 16, 128 * oot:128 * oot + 128]
        if o >= 2:
            E.wait("sync", "ve", osub_done[o - 2])
            E.wait("gpsimd", "pe", oslab_done[o - 2])
        owld = E.dma("sync", owf[par].rearrange("p (f o) -> p f o", o=128),
                     owT_ap, okey)
        gld = E.dma("gpsimd", owh[par].rearrange("p (f o) -> p f o", o=128),
                    owT_ap, gok)
        E.wait("vector", okey, owld)
        E.wait("vector", gok, gld)
        if o >= 2:
            E.wait("vector", "pe", oslab_done[o - 2])
        osub_done[o] = E.op("vector", lambda e, par=par: e.tensor_sub(
            owl[par][:], owf[par][:], owh[par][:]), "ve")

    oi = 0
    E.wait("gpsimd", "cc", ag_done)
    E.wait("sync", "ve", E.val("ve"))
    E.wait("sync", "ac", E.val("ac"))
    E.wait("gpsimd", "ve", E.val("ve"))
    E.wait("gpsimd", "ac", E.val("ac"))
    E.wait("gpsimd", "pe", d1_pe)
    prepO(0)
    for half in range(2):
        E.wait("gpsimd", "pe", d1_pe if half == 0 else oslab_done[oi - 1])
        strip_done = None
        for srci in range(2):
            agsrc = ag0_out if srci == 0 else ag1_out
            for kq in range(4):
                sld = E.dma("gpsimd",
                            ohstrip.rearrange("p (f t) -> p f t", t=256)[:, 0:8, :],
                            lambda half=half, kq=kq, agsrc=agsrc: agsrc.ap()[
                                DS(pidbox[f"rbA{half}"], 1024),
                                DS(pidbox[f"go{half}_{kq}"], 256)].rearrange(
                                "(f p) t -> p f t", p=128), "gp")
                E.wait("scalar", "gp", sld)
                if strip_done is not None:
                    E.wait("scalar", "ve", strip_done)
                shv = E.op("scalar", lambda e, kq=kq, srci=srci: e.copy(
                    ohh.rearrange("p (f t) -> p f t", t=1024)[
                        :, 8 * srci:8 * srci + 8, 256 * kq:256 * kq + 256],
                    ohstrip.rearrange("p (f t) -> p f t", t=256)[:, 0:8, :]), "ac")
                E.wait("vector", "ac", shv)
                strip_done = E.op("vector", lambda e, kq=kq, srci=srci: e.tensor_sub(
                    ohl.rearrange("p (f t) -> p f t", t=1024)[
                        :, 8 * srci:8 * srci + 8, 256 * kq:256 * kq + 256],
                    ohstrip.rearrange("p (f t) -> p f t", t=256)[:, 0:8, :],
                    ohh.rearrange("p (f t) -> p f t", t=1024)[
                        :, 8 * srci:8 * srci + 8, 256 * kq:256 * kq + 256]), "ve")
                E.wait("gpsimd", "ve", strip_done)
        oh_ready = strip_done
        if half == 1:
            E.wait("sync", "ap0", E.val("ap0"))
            E.wait("sync", "ap1", E.val("ap1"))
            E.wait("sync", "fs2", E.val("fs2"))
            E.wait("sync", "fs3", E.val("fs3"))
        for ot in range(32):
            par = oi % 2
            if oi + 1 < 64:
                prepO(oi + 1)
            E.wait("tensor", "ve", osub_done[oi])
            E.wait("tensor", "ve", oh_ready)
            ob_ = 2 * (oi % 2)
            for tc in range(2):
                E.wait("tensor", "ve", oev.get(ob_ + tc, 0))
            ommv = None
            for f in range(16):
                for tc in range(2):
                    ik = "pe" if (f == 15 and tc == 1) else None
                    ommv = E.op("tensor", lambda e, f=f, tc=tc, par=par, ob_=ob_: (
                        e.matmul(P[ob_ + tc][:], owh[par][:, 128 * f:128 * f + 128],
                                 ohh[:, 1024 * f + CH * tc:1024 * f + CH * tc + CH],
                                 start=(f == 0), stop=False),
                        e.matmul(P[ob_ + tc][:], owh[par][:, 128 * f:128 * f + 128],
                                 ohl[:, 1024 * f + CH * tc:1024 * f + CH * tc + CH],
                                 start=False, stop=False),
                        e.matmul(P[ob_ + tc][:], owl[par][:, 128 * f:128 * f + 128],
                                 ohh[:, 1024 * f + CH * tc:1024 * f + CH * tc + CH],
                                 start=False, stop=(f == 15)))[-1], ik)
            oslab_done[oi] = ommv
            for tc in range(2):
                E.wait("vector", "pe", ommv)
                prev = oo_store.get(tc)
                if half == 0:
                    if prev is not None:
                        E.wait("vector", prev[0], prev[1])
                    ev = E.op("vector", lambda e, tc=tc, ob_=ob_: e.tensor_copy(
                        oout[tc][:], P[ob_ + tc][:]), "ve")
                    oev[ob_ + tc] = ev
                    E.wait("scalar", "ve", ev)
                    oo_store[tc] = (f"ap{tc}", E.dma(
                        "scalar", apT_s[128 * ot:128 * ot + 128, CH * tc:CH * tc + CH],
                        oout[tc][:], f"ap{tc}"))
                else:
                    ekey = f"e{tc}"
                    E.wait("sync", "ve", ext_used.get(tc, 0))
                    E.dma("sync", extile("ap", tc)[:],
                          apT_s[128 * ot:128 * ot + 128, CH * tc:CH * tc + CH], ekey)
                    E.dma("sync", extile("ffn", tc)[:],
                          ffnT_s[128 * ot:128 * ot + 128, CH * tc:CH * tc + CH], ekey)
                    xld2 = E.dma("sync", extile("x", tc)[:],
                                 xTo[128 * ot:128 * ot + 128, CH * tc:CH * tc + CH],
                                 ekey)
                    E.wait("vector", ekey, xld2)
                    if prev is not None:
                        E.wait("vector", prev[0], prev[1])
                    E.op("vector", lambda e, tc=tc, ob_=ob_: e.tensor_add(
                        oout[tc][:], P[ob_ + tc][:], extile("ap", tc)[:]))
                    E.op("vector", lambda e, tc=tc: e.tensor_add(
                        oout[tc][:], oout[tc][:], extile("x", tc)[:]))
                    ev = E.op("vector", lambda e, tc=tc: e.tensor_add(
                        oout[tc][:], oout[tc][:], extile("ffn", tc)[:]), "ve")
                    ext_used[tc] = ev
                    oev[ob_ + tc] = ev
                    E.wait("scalar", "ve", ev)
                    oo_store[tc] = (f"ou{tc}", E.dma(
                        "scalar", outT[128 * ot:128 * ot + 128, CH * tc:CH * tc + CH],
                        oout[tc][:], f"ou{tc}"))
            oi += 1

    E.wait("scalar", "ou0", E.val("ou0"))
    E.wait("scalar", "ou1", E.val("ou1"))
    E.wait("gpsimd", "gp")

    with nc.Block() as block:
        block.sync(lambda e: [f(e) for f in E.ops["sync"]])
        block.scalar(lambda e: [f(e) for f in E.ops["scalar"]])
        block.vector(lambda e: [f(e) for f in E.ops["vector"]])
        block.gpsimd(lambda e: [f(e) for f in E.ops["gpsimd"]])
        block.tensor(lambda e: [f(e) for f in E.ops["tensor"]])

    for k in sem_ctx:
        sem_ctx[k].__exit__(None, None, None)
    return nc


_NC_CACHE = {}


def kernel(L0_attn_hidden_states, L0_attn_q_w, L0_attn_k_w, L0_attn_v_w,
           L0_attn_o_w, L0_ffn_qweight, L0_ffn_scales, L0_ffn_bias):
    import ml_dtypes
    x = np.asarray(L0_attn_hidden_states, dtype=np.float32)
    q_w = np.asarray(L0_attn_q_w, dtype=np.float32)
    k_w = np.asarray(L0_attn_k_w, dtype=np.float32)
    v_w = np.asarray(L0_attn_v_w, dtype=np.float32)
    o_w = np.asarray(L0_attn_o_w, dtype=np.float32)
    qw_i = np.asarray(L0_ffn_qweight, dtype=np.int32)
    scales = np.asarray(L0_ffn_scales, dtype=np.float32)
    fb = np.asarray(L0_ffn_bias, dtype=np.float32)

    owT = np.ascontiguousarray(o_w.T)
    fwT = np.ascontiguousarray(qw_i.T)
    s2 = scales.reshape(HID, 64)
    sexpT = np.ascontiguousarray(np.repeat(s2.T, 64, axis=0))
    fbias = np.ascontiguousarray(fb.reshape(HID, 1))

    inv = 1.0 / (10000.0 ** (np.arange(0, DH, 2, dtype=np.float64) / DH))
    freqs = np.arange(S, dtype=np.float64)[:, None] * inv[None, :]
    emb = np.concatenate([freqs, freqs], axis=-1)
    cosT = np.ascontiguousarray(np.cos(emb).T.astype(np.float32))
    sinT = np.ascontiguousarray(np.sin(emb).T.astype(np.float32))

    ptT = np.zeros((128, 128), dtype=np.float32)
    ptT[np.arange(64), np.arange(64) + 64] = 1.0
    ptT[np.arange(64) + 64, np.arange(64)] = -1.0

    ident = np.eye(128, dtype=np.float32)
    onesm = np.ones((128, 128), dtype=np.float32)
    onesb = np.ones((128, 128), dtype=ml_dtypes.bfloat16)
    identb = np.eye(128, dtype=ml_dtypes.bfloat16)

    in_maps = []
    for c in range(NCORES):
        b, g = c // 2, c % 2
        in_maps.append({
            "xT": np.ascontiguousarray(x[b].T),
            "xTo": np.ascontiguousarray(x[b].T[:, 1024 * g:1024 * g + 1024]),
            "qwT": np.ascontiguousarray(q_w.T[:, 2048 * g:2048 * g + 2048]),
            "kwT": np.ascontiguousarray(k_w.T[:, 2048 * g:2048 * g + 2048]),
            "vwT": np.ascontiguousarray(v_w.T[:, 2048 * g:2048 * g + 2048]),
            "owT": owT, "fwT": fwT, "sexpT": sexpT, "fbias": fbias,
            "cosT": cosT, "sinT": sinT, "ptT": ptT, "ident": ident,
            "onesm": onesm, "onesb": onesb, "identb": identb,
            "impsc": np.full((1, 1), 1.0 if b == 0 else 0.0, dtype=np.float32),
        })

    if "prog" not in _NC_CACHE:
        _NC_CACHE["prog"] = build_program()
    nc = _NC_CACHE["prog"]
    res = run_bass_kernel_spmd(nc, in_maps, core_ids=list(range(NCORES)))

    out = np.empty((B, S, HID), dtype=np.float32)
    for c in range(NCORES):
        b, g = c // 2, c % 2
        out[b, 1024 * g:1024 * g + 1024, :] = res.results[c]["outT"].T
    return out


# version 07 (final)
# speedup vs baseline: 1.3400x; 1.0238x over previous
import numpy as np
import concourse.bass as bass
import concourse.mybir as mybir
from concourse.bass_utils import run_bass_kernel_spmd

F32 = mybir.dt.float32
BF16 = mybir.dt.bfloat16
I32 = mybir.dt.int32
AF = mybir.ActivationFunctionType
ALU = mybir.AluOpType
AXX = mybir.AxisListType.X
DS = bass.ds

B, S, H, DH, HID = 4, 2048, 32, 128, 4096
NCORES = 8
SCALE = 1.0 / float(np.sqrt(128.0))
SHIFT = -30.0
EPS = 1e-5
VIS_LO, VIS_HI = 1, 513
NCH = 4
CH = 512
NFT = 16
NCT = 32

DMA_KEYS = ("c", "x0", "x1", "w0", "w1", "gw0", "gw1", "kv0", "kv1",
            "h0", "h1", "f0", "f1", "p0", "p1", "o0", "o1", "go0", "go1",
            "e0", "e1", "xs", "qk0", "qk1", "v0", "v1", "ob0", "ob1",
            "fs0", "fs1", "fs2", "fs3", "ap0", "ap1", "ou0", "ou1", "gp")
ENG_KEYS = ("pe", "ac", "ve", "gc", "cc")


class Emitter:
    def __init__(self, sems, nc=None):
        self.ops = {k: [] for k in ("sync", "scalar", "vector", "gpsimd", "tensor")}
        self.cnt = {}
        self.last_wait = {k: {} for k in self.ops}
        self.sems = sems
        self.nc = nc

    def mark(self, name):
        if self.nc is None:
            return
        for q in self.ops:
            self.ops[q].append(lambda e, n=name, nc=self.nc: (
                nc.leave_named_scope(), nc.enter_named_scope(n))[-1]
                if getattr(nc, "_scope_open", False) else
                (nc.enter_named_scope(n), setattr(nc, "_scope_open", True))[-1])

    def inc(self, key, n):
        self.cnt[key] = self.cnt.get(key, 0) + n
        return self.cnt[key]

    def val(self, key):
        return self.cnt.get(key, 0)

    def wait(self, eng, key, v=None):
        v = self.val(key) if v is None else v
        if not v or v <= 0:
            return
        lw = self.last_wait[eng]
        if lw.get(key, -1) >= v:
            return
        lw[key] = v
        s = self.sems[key]
        self.ops[eng].append(lambda e, s=s, vv=v: e.wait_ge(s, vv))

    def dma(self, q, out_ap, in_ap, key, slow=False):
        prev = self.val(key)
        if prev:
            self.wait(q, key, prev)
        v = self.inc(key, 16)
        s = self.sems[key]

        def run(e, o=out_ap, i=in_ap, s=s, slow=slow):
            if callable(o):
                o = o()
            if callable(i):
                i = i()
            if slow:
                e.dma_start(out=o, in_=i, allow_slow_non_contiguous=True).then_inc(s, 16)
            else:
                e.dma_start(out=o, in_=i).then_inc(s, 16)
        self.ops[q].append(run)
        return v

    def op(self, eng, fn, inc_key=None, inc_n=1):
        if inc_key is None:
            self.ops[eng].append(fn)
            return None
        v = self.inc(inc_key, inc_n)
        s = self.sems[inc_key]
        self.ops[eng].append(lambda e, f=fn, s=s, n=inc_n: f(e).then_inc(s, n))
        return v


def build_program():
    nc = bass.Bass("TRN2", target_bir_lowering=False, detect_race_conditions=False)

    xT = nc.dram_tensor("xT", [HID, S], F32, kind="ExternalInput").ap()
    qwT = nc.dram_tensor("qwT", [HID, 2048], F32, kind="ExternalInput").ap()
    kwT = nc.dram_tensor("kwT", [HID, 2048], F32, kind="ExternalInput").ap()
    vwT = nc.dram_tensor("vwT", [HID, 2048], F32, kind="ExternalInput").ap()
    owT = nc.dram_tensor("owT", [HID, HID], F32, kind="ExternalInput").ap()
    fwT = nc.dram_tensor("fwT", [HID, HID], I32, kind="ExternalInput").ap()
    sexpT = nc.dram_tensor("sexpT", [HID, HID], F32, kind="ExternalInput").ap()
    fbias = nc.dram_tensor("fbias", [HID, 1], F32, kind="ExternalInput").ap()
    cosT_d = nc.dram_tensor("cosT", [128, S], F32, kind="ExternalInput").ap()
    sinT_d = nc.dram_tensor("sinT", [128, S], F32, kind="ExternalInput").ap()
    ptT_d = nc.dram_tensor("ptT", [128, 128], F32, kind="ExternalInput").ap()
    ident_d = nc.dram_tensor("ident", [128, 128], F32, kind="ExternalInput").ap()
    ones_d = nc.dram_tensor("onesm", [128, 128], F32, kind="ExternalInput").ap()
    onesb_d = nc.dram_tensor("onesb", [128, 128], BF16, kind="ExternalInput").ap()
    identb_d = nc.dram_tensor("identb", [128, 128], BF16, kind="ExternalInput").ap()
    impsc_d = nc.dram_tensor("impsc", [1, 1], F32, kind="ExternalInput").ap()
    xTo = nc.dram_tensor("xTo", [HID, 1024], F32, kind="ExternalInput").ap()
    outT = nc.dram_tensor("outT", [HID, 1024], F32, kind="ExternalOutput").ap()

    qhT_s = nc.dram_tensor("qhT_s", [2048, S], BF16).ap()
    qlT_s = nc.dram_tensor("qlT_s", [2048, S], BF16).ap()
    khT_s = nc.dram_tensor("khT_s", [2048, S], BF16).ap()
    klT_s = nc.dram_tensor("klT_s", [2048, S], BF16).ap()
    vT_s = nc.dram_tensor("vT_s", [2048, S], F32).ap()
    xnhT_s = nc.dram_tensor("xnhT_s", [HID, S], BF16)
    xnlT_s = nc.dram_tensor("xnlT_s", [HID, S], BF16)
    oh0_s = nc.dram_tensor("oh0_s", [1024, S], F32)
    oh1_s = nc.dram_tensor("oh1_s", [1024, S], F32)
    ag0_out = nc.dram_tensor("ag0_out", [8 * 1024, S], F32,
                             addr_space="Shared")
    ag1_out = nc.dram_tensor("ag1_out", [8 * 1024, S], F32,
                             addr_space="Shared")
    ffnT_s = nc.dram_tensor("ffnT_s", [HID, 1024], F32).ap()
    apT_s = nc.dram_tensor("apT_s", [HID, 1024], F32).ap()
    ar_in = nc.dram_tensor("ar_in", [1, 512], F32)
    ar_out = nc.dram_tensor("ar_out", [1, 512], F32)
    kbias_s = nc.dram_tensor("kbias_s", [640, 1], F32).ap()

    FA = nc.alloc_sbuf_tensor("FA", [128, 13824], F32).ap()
    AB = nc.alloc_sbuf_tensor("AB", [128, 53248], BF16).ap()
    AI = nc.alloc_sbuf_tensor("AI", [128, 4096], I32).ap()
    ROWS = nc.alloc_sbuf_tensor("ROWS", [1, 3072], F32).ap()
    cosT = nc.alloc_sbuf_tensor("cosT_sb", [128, S], F32).ap()
    sinT = nc.alloc_sbuf_tensor("sinT_sb", [128, S], F32).ap()
    ptT = nc.alloc_sbuf_tensor("ptT_sb", [128, 128], F32).ap()
    ident = nc.alloc_sbuf_tensor("ident_sb", [128, 128], F32).ap()
    ones = nc.alloc_sbuf_tensor("ones_sb", [128, 128], F32).ap()
    onesb = nc.alloc_sbuf_tensor("onesb_sb", [128, 128], BF16).ap()
    identb = nc.alloc_sbuf_tensor("identb_sb", [128, 128], BF16).ap()
    sumq = nc.alloc_sbuf_tensor("sumq_sb", [128, 16], F32).ap()
    sumqh = nc.alloc_sbuf_tensor("sumqh_sb", [128, 16], BF16).ap()
    sumql = nc.alloc_sbuf_tensor("sumql_sb", [128, 16], BF16).ap()
    sumq_t = nc.alloc_sbuf_tensor("sumq_t", [128, 1], F32).ap()
    kbias = nc.alloc_sbuf_tensor("kbias_sb", [128, 8], F32).ap()
    m30 = nc.alloc_sbuf_tensor("m30_sb", [128, 1], F32).ap()
    impsc = nc.alloc_sbuf_tensor("impsc_sb", [1, 1], F32).ap()
    impcols = nc.alloc_sbuf_tensor("impcols_sb", [128, 4], F32).ap()
    fbias_c = nc.alloc_sbuf_tensor("fbias_sb", [128, 32], F32).ap()
    rankw = nc.alloc_sbuf_tensor("rankw_sb", [128, 8], F32).ap()

    P = [nc.alloc_psum_tensor(f"P{i}", [128, 512], F32).ap() for i in range(8)]
    P6 = P[6][0:1, :]
    P7 = P[7][0:1, :]

    sems, sem_ctx = {}, {}
    for k in DMA_KEYS + ENG_KEYS:
        g = nc.semaphore(k)
        sems[k] = g.__enter__()
        sem_ctx[k] = g
    E = Emitter(sems)

    R = {name: ROWS[0:1, 512 * i:512 * i + 512]
         for i, name in enumerate(("mean", "vpe", "b", "ss", "rstd", "negmr"))}

    ones_col = ones[:, 0:1]
    ones_row = ones[0:1, :]
    onesb_col = onesb[:, 0:1]

    E.dma("sync", cosT[:], cosT_d, "c")
    E.dma("sync", sinT[:], sinT_d, "c")
    E.dma("sync", ptT[:], ptT_d, "c")
    E.dma("sync", ident[:], ident_d, "c")
    E.dma("sync", ones[:], ones_d, "c")
    E.dma("sync", onesb[:], onesb_d, "c")
    E.dma("sync", identb[:], identb_d, "c")
    E.dma("sync", impsc[:], impsc_d, "c")
    consts_ld = E.dma("sync", fbias_c[:],
                      fbias[:, 0:1].rearrange("(o p) one -> p (o one)", p=128), "c",
                      slow=True)
    E.wait("vector", "c", consts_ld)
    E.op("vector", lambda e: e.memset(sumq[:], 0.0))
    E.op("vector", lambda e: e.memset(m30[:], SHIFT), "ve")

    pidbox = {}

    def gp_init(e):
        pid = nc.partition_id([mybir.EngineType.Pool])
        g = pid % 2
        pidbox["goff"] = g * 1024
        for half in range(2):
            pidbox[f"rbA{half}"] = (pid - g + half) * 1024
            for kq in range(4):
                pidbox[f"go{half}_{kq}"] = g * 1024 + 256 * kq
        return e.memset(rankw[:, 7:8], 0.0)
    E.op("gpsimd", gp_init, "gc")

    xtile = [FA[:, 0:512], FA[:, 512:1024]]
    wslab = [FA[:, 1024:5120], FA[:, 1024:5120]]
    vout = [FA[:, 5120:5632], FA[:, 5632:6144]]
    traw = [FA[:, 6144:6656], FA[:, 6656:7168]]
    t1b = FA[:, 7168:7680]
    t2b = FA[:, 7680:8192]
    qrout = [FA[:, 8192:8704], FA[:, 8704:9216]]
    xsq = [FA[:, 9216:9728], FA[:, 9728:10240]]
    bc_rstd = FA[:, 10240:10752]
    bc_negmr = FA[:, 10752:11264]

    xnh = AB[:, 0:16384]
    xnl = AB[:, 16384:32768]
    wsh = [AB[:, 32768:36864], AB[:, 36864:40960]]
    wsl = [AB[:, 40960:45056], AB[:, 45056:49152]]
    qrh = [AB[:, 49152:49664], AB[:, 49664:50176]]
    qrl = [AB[:, 50176:50688], AB[:, 50688:51200]]

    slab_mm = {}
    slab_sub = {}
    qr_store = {}
    v_store = {}
    evac_ac = {}
    evac_ve = {}
    traw_free = {}
    rot_free = {}
    pending_rot = []
    prev_nm_v = 0
    prev_bc_cp = 0
    gi = 0
    xn_split_done = 0

    def flush_rot(item):
        (rgi, rj, rft, rqp, rev, rdsth, rdstl) = item
        rp = rgi % 2
        qkey = f"qk{rqp}"
        E.wait("tensor", "ac", rev)
        E.wait("tensor", "ve", rot_free.get(rp, 0))
        rot = E.op("tensor", lambda e, rqp=rqp, rp=rp: e.matmul(
            P[2 + rp][:], ptT[:], traw[rqp][:], start=True, stop=True), "pe")
        E.wait("vector", "pe", rot)
        if rqp in qr_store:
            E.wait("vector", qr_store[rqp][0], qr_store[rqp][1])
        tv = E.op("vector", lambda e, rqp=rqp, rj=rj: e.tensor_mul(
            t1b[:], traw[rqp][:], cosT[:, CH * rj:CH * rj + CH]), "ve")
        traw_free[rqp] = tv
        rused = E.op("vector", lambda e, rp=rp, rj=rj: e.tensor_mul(
            t2b[:], P[2 + rp][:], sinT[:, CH * rj:CH * rj + CH]), "ve")
        rot_free[rp] = rused
        rope_v = E.op("vector", lambda e, rqp=rqp: e.tensor_add(
            qrout[rqp][:], t1b[:], t2b[:]), "ve")
        if rdstl is qlT_s and rj >= 1:
            lo = {1: 1, 2: 0, 3: 0}[rj]
            hi = {1: 512, 2: 512, 3: 456}[rj]
            rs_v = E.op("vector", lambda e, rqp=rqp, lo=lo, hi=hi: e.reduce_sum(
                sumq_t[:], qrout[rqp][:, lo:hi], axis=AXX), "ve")
            E.wait("vector", "ve", rs_v)
            acc_v = E.op("vector", lambda e, rft=rft: e.tensor_add(
                sumq[:, rft:rft + 1], sumq[:, rft:rft + 1], sumq_t[:]), "ve")
            E.wait("vector", "ve", acc_v)
        E.wait("scalar", "ve", rope_v)
        qh_v = E.op("scalar", lambda e, rqp=rqp: e.copy(
            qrh[rqp][:], qrout[rqp][:]), "ac")
        E.wait("vector", "ac", qh_v)
        ql_v = E.op("vector", lambda e, rqp=rqp: e.tensor_sub(
            qrl[rqp][:], qrout[rqp][:], qrh[rqp][:]), "ve")
        E.wait("scalar", "ve", ql_v)
        E.dma("scalar", rdsth[128 * rft:128 * rft + 128, CH * rj:CH * rj + CH],
              qrh[rqp][:], qkey)
        st2 = E.dma("scalar", rdstl[128 * rft:128 * rft + 128, CH * rj:CH * rj + CH],
                    qrl[rqp][:], qkey)
        qr_store[rqp] = (qkey, st2)

    for j in range(NCH):
        E.wait("tensor", "ve", prev_nm_v)
        sq_vals, ssq_vals = [], []
        for c in range(NCT):
            par = c % 2
            xk = f"x{par}"
            if c >= 2:
                E.wait("sync", "pe", ssq_vals[c - 2])
            xld = E.dma("sync", xtile[par][:],
                        xT.rearrange("(c p) t -> p c t", p=128)[
                            :, c, CH * j:CH * j + CH], xk)
            E.wait("scalar", xk, xld)
            if c >= 2:
                E.wait("scalar", "pe", ssq_vals[c - 2])
            sq_vals.append(E.op("scalar", lambda e, par=par: e.activation(
                xsq[par][:], xtile[par][:], AF.Square), "ac"))
            E.wait("tensor", xk, xld)
            E.op("tensor", lambda e, par=par, c=c: e.matmul(
                P6[:], ones_col, xtile[par][:],
                start=(c == 0), stop=(c == NCT - 1)))
            E.wait("tensor", "ac", sq_vals[c])
            ssq_vals.append(E.op("tensor", lambda e, par=par, c=c: e.matmul(
                P7[:], ones_col, xsq[par][:],
                start=(c == 0), stop=(c == NCT - 1)), "pe"))
        E.wait("vector", "pe", ssq_vals[-1])
        E.op("vector", lambda e: e.tensor_scalar(R["mean"], P6[:], 1.0 / HID, None, ALU.mult))
        E.op("vector", lambda e: e.tensor_scalar(R["vpe"], P7[:], 1.0 / HID, None, ALU.mult))
        E.op("vector", lambda e: e.tensor_mul(R["b"], R["mean"], R["mean"]))
        E.op("vector", lambda e: e.scalar_tensor_tensor(
            R["vpe"], R["vpe"], EPS, R["b"], ALU.add, ALU.subtract))
        rr_v = E.op("vector", lambda e: e.reciprocal(R["b"], R["vpe"]), "ve")
        E.wait("scalar", "ve", rr_v)
        ss_v = E.op("scalar", lambda e: e.activation(R["ss"], R["b"], AF.Sqrt), "ac")
        E.wait("vector", "ac", ss_v)
        E.op("vector", lambda e: e.tensor_mul(R["b"], R["ss"], R["ss"]))
        E.op("vector", lambda e: e.tensor_mul(R["b"], R["b"], R["vpe"]))
        E.op("vector", lambda e: e.tensor_scalar(R["b"], R["b"], -0.5, 1.5, ALU.mult, ALU.add))
        E.op("vector", lambda e: e.tensor_mul(R["rstd"], R["ss"], R["b"]))
        E.op("vector", lambda e: e.tensor_mul(R["negmr"], R["mean"], R["rstd"]))
        nm_v = E.op("vector", lambda e: e.tensor_scalar(
            R["negmr"], R["negmr"], -1.0, None, ALU.mult), "ve")
        prev_nm_v = nm_v
        E.wait("tensor", "ve", nm_v)
        E.wait("tensor", "ac", prev_bc_cp)
        E.wait("tensor", "ve", max(rot_free.get(0, 0), rot_free.get(1, 0)))
        bc_mm = E.op("tensor", lambda e: (
            e.matmul(P[2][:], ones_row, R["rstd"], start=True, stop=True),
            e.matmul(P[3][:], ones_row, R["negmr"], start=True, stop=True))[-1], "pe")
        E.wait("scalar", "pe", bc_mm)
        E.op("scalar", lambda e: e.copy(bc_rstd[:], P[2][:]))
        bc_cp = E.op("scalar", lambda e: e.copy(bc_negmr[:], P[3][:]), "ac")
        prev_bc_cp = bc_cp
        E.wait("vector", "ac", bc_cp)

        E.wait("sync", "pe", ssq_vals[-1])
        pe_cum = E.val("pe")
        E.wait("scalar", "pe", pe_cum)
        E.wait("vector", "pe", pe_cum)
        E.wait("scalar", "xs", E.val("xs"))
        E.wait("vector", "xs", E.val("xs"))
        xn_sub = None
        xn_subs = []
        for c in range(NCT):
            par = c % 2
            xk = f"x{par}"
            if c >= 2:
                E.wait("sync", "ve", xn_subs[c - 2])
            xld = E.dma("sync", xtile[par][:],
                        xT.rearrange("(c p) t -> p c t", p=128)[
                            :, c, CH * j:CH * j + CH], xk)
            E.wait("vector", xk, xld)
            E.op("vector", lambda e, par=par: e.tensor_mul(
                xtile[par][:], xtile[par][:], bc_rstd[:]))
            aff_v = E.op("vector", lambda e, par=par: e.tensor_add(
                xtile[par][:], xtile[par][:], bc_negmr[:]), "ve")
            E.wait("scalar", "ve", aff_v)
            hc_v = E.op("scalar", lambda e, par=par, c=c: e.copy(
                xnh[:, CH * c:CH * c + CH], xtile[par][:]), "ac")
            E.wait("vector", "ac", hc_v)
            xn_sub = E.op("vector", lambda e, par=par, c=c: e.tensor_sub(
                xnl[:, CH * c:CH * c + CH], xtile[par][:],
                xnh[:, CH * c:CH * c + CH]), "ve")
            xn_subs.append(xn_sub)
        xn_split_done = xn_sub

        E.wait("scalar", "ve", xn_split_done)
        E.dma("scalar",
              xnhT_s.ap().rearrange("(c p) t -> p c t", p=128)[:, :, CH * j:CH * j + CH],
              xnh.rearrange("p (c t) -> p c t", t=CH), "xs")
        xs_st = E.dma("scalar",
                      xnlT_s.ap().rearrange("(c p) t -> p c t", p=128)[:, :, CH * j:CH * j + CH],
                      xnl.rearrange("p (c t) -> p c t", t=CH), "xs")

        groups = [(proj, wsrc, dsth, dstl)
                  for proj, wsrc, dsth, dstl in (("q", qwT, qhT_s, qlT_s),
                                                 ("k", kwT, khT_s, klT_s),
                                                 ("v", vwT, vT_s, None))
                  for _ft in range(NFT)]

        def prepA(gidx, ft):
            par = gidx % 2
            wk, gk = f"w{par}", f"gw{par}"
            wsrc_ap = groups[ft[0]][1].rearrange("(c p) f -> p c f", p=128)[
                :, :, 128 * ft[1]:128 * ft[1] + 128]
            if gidx >= 1:
                E.wait("sync", "ve", slab_sub.get(gidx - 1, 0))
            if gidx >= 2:
                E.wait("gpsimd", "pe", slab_mm.get(gidx - 2, 0))
            wld = E.dma("sync", wslab[par].rearrange("p (c f) -> p c f", f=128),
                        wsrc_ap, wk)
            gld = E.dma("gpsimd", wsh[par].rearrange("p (c f) -> p c f", f=128),
                        wsrc_ap, gk)
            E.wait("vector", wk, wld)
            E.wait("vector", gk, gld)
            E.wait("vector", "pe", slab_mm.get(gidx - 2, 0))
            dq_v = E.op("vector", lambda e, par=par: e.tensor_sub(
                wsl[par][:], wslab[par][:], wsh[par][:]), "ve")
            slab_sub[gidx] = dq_v

        base_gi = gi
        prepA(base_gi, (0, 0))
        for idx in range(len(groups)):
            proj, wsrc, dsth, dstl = groups[idx]
            ft = idx % NFT
            if idx + 1 < len(groups):
                prepA(base_gi + idx + 1, (idx + 1, (idx + 1) % NFT))
            par = gi % 2
            E.wait("tensor", "ve", slab_sub[gi])
            E.wait("tensor", "ve", xn_split_done)
            E.wait("tensor", "ac", evac_ac.get(gi - 2, 0))
            pv = None
            for c in range(NCT):
                ik = "pe" if c == NCT - 1 else None
                pv = E.op("tensor", lambda e, c=c, par=par: (
                    e.matmul(P[par][:], wsh[par][:, 128 * c:128 * c + 128],
                             xnh[:, CH * c:CH * c + CH],
                             start=(c == 0), stop=False),
                    e.matmul(P[par][:], wsh[par][:, 128 * c:128 * c + 128],
                             xnl[:, CH * c:CH * c + CH],
                             start=False, stop=False),
                    e.matmul(P[par][:], wsl[par][:, 128 * c:128 * c + 128],
                             xnh[:, CH * c:CH * c + CH],
                             start=False, stop=(c == NCT - 1)))[-1], ik)
            slab_mm[gi] = pv
            if proj == "v":
                vp = gi % 2
                vkey = f"v{vp}"
                E.wait("scalar", "pe", pv)
                if vp in v_store:
                    E.wait("scalar", vkey, v_store[vp])
                hv = E.op("scalar", lambda e, vp=vp, par=par: e.copy(
                    vout[vp][:], P[par][:]), "ac")
                evac_ac[gi] = hv
                v_store[vp] = E.dma(
                    "scalar", dsth[128 * ft:128 * ft + 128, CH * j:CH * j + CH],
                    vout[vp][:], vkey)
            else:
                qp = gi % 2
                E.wait("scalar", "pe", pv)
                E.wait("scalar", "ve", traw_free.get(qp, 0))
                ev = E.op("scalar", lambda e, qp=qp, par=par: e.copy(
                    traw[qp][:], P[par][:]), "ac")
                evac_ac[gi] = ev
                pending_rot.append((gi, j, ft, qp, ev, dsth, dstl))
            if len(pending_rot) > 1:
                flush_rot(pending_rot.pop(0))
            gi += 1
        while pending_rot:
            flush_rot(pending_rot.pop(0))

    kvh = [AB[:, 32768 + i * 512:32768 + i * 512 + 512] for i in range(2)]
    kvl = [AB[:, 33792 + i * 512:33792 + i * 512 + 512] for i in range(2)]
    for kk in ("qk0", "qk1", "v0", "v1", "xs"):
        E.wait("sync", kk, E.val(kk))
    E.wait("sync", "pe", E.val("pe"))
    sq_all = E.val("ve")
    E.wait("scalar", "ve", sq_all)
    sqh_v = E.op("scalar", lambda e: e.copy(sumqh[:], sumq[:]), "ac")
    E.wait("vector", "ac", sqh_v)
    sql_v = E.op("vector", lambda e: e.tensor_sub(sumql[:], sumq[:], sumqh[:]), "ve")
    E.wait("vector", "ve", sql_v)
    imp_vals = []
    for h in range(NFT):
        kvk = f"kv{h % 2}"
        if h >= 2:
            E.wait("sync", "pe", imp_vals[h - 2])
        E.dma("sync", kvh[h % 2][:], khT_s[128 * h:128 * h + 128, VIS_LO:VIS_HI], kvk)
        kvld = E.dma("sync", kvl[h % 2][:], klT_s[128 * h:128 * h + 128, VIS_LO:VIS_HI], kvk)
        E.wait("tensor", kvk, kvld)
        E.wait("tensor", "ve", sql_v)
        imp_vals.append(E.op("tensor", lambda e, h=h: (
            e.matmul(P6[:], sumqh[:, h:h + 1], kvh[h % 2][:],
                     start=(h == 0), stop=False),
            e.matmul(P6[:], sumqh[:, h:h + 1], kvl[h % 2][:],
                     start=False, stop=False),
            e.matmul(P6[:], sumql[:, h:h + 1], kvh[h % 2][:],
                     start=False, stop=(h == NFT - 1)))[-1], "pe"))
    E.wait("vector", "pe", imp_vals[-1])
    imp_v = E.op("vector", lambda e: e.tensor_scalar(
        R["mean"], P6[:], impsc[0:1, 0:1], None, ALU.mult), "ve")
    E.wait("gpsimd", "ve", imp_v)
    E.dma("gpsimd", ar_in.ap(), R["mean"], "gp")
    E.wait("gpsimd", "gp")
    ar_done = E.op("gpsimd", lambda e: e.collective_compute(
        "AllReduce", ALU.add, replica_groups=[list(range(8))],
        ins=[ar_in.ap().opt()], outs=[ar_out.ap().opt()]), "cc")
    E.wait("gpsimd", "cc", ar_done)
    impr_ld = E.dma("gpsimd", R["mean"], ar_out.ap(), "gp")
    impc_ld = E.dma("gpsimd", impcols[:],
                    ar_out.ap().rearrange("one (v p) -> (one p) v", p=128), "gp",
                    slow=True)
    E.wait("tensor", "gp", impc_ld)
    bcimp = E.op("tensor", lambda e: e.matmul(
        P[2][:], ones_row, R["mean"], start=True, stop=True), "pe")
    E.wait("scalar", "pe", bcimp)
    impbc = FA[:, 1024:1536]
    onestile = FA[:, 1536:2048]
    cgt = FA[:, 2048:2560]
    ceq = FA[:, 2560:3072]
    ultv = FA[:, 3072:3584]
    impbc_cp = E.op("scalar", lambda e: e.copy(impbc[:], P[2][:]), "ac")
    E.wait("vector", "ac", impbc_cp)
    ones512_v = E.op("vector", lambda e: e.memset(onestile[:], 1.0), "ve")
    E.wait("gpsimd", "ve", ones512_v)
    E.wait("vector", "gp", impc_ld)
    for vt in range(4):
        asel_v = E.op("gpsimd", lambda e, vt=vt: e.affine_select(
            ultv[:], onestile[:], pattern=[[-1, 512]], compare_op=ALU.is_gt,
            fill=0.0, base=128 * vt, channel_multiplier=1), "gc")
        E.op("vector", lambda e, vt=vt: e.tensor_scalar(
            cgt[:], impbc[:], impcols[:, vt:vt + 1], None, ALU.is_gt))
        E.op("vector", lambda e, vt=vt: e.tensor_scalar(
            ceq[:], impbc[:], impcols[:, vt:vt + 1], None, ALU.is_equal))
        E.wait("vector", "gc", asel_v)
        E.op("vector", lambda e: e.tensor_mul(ceq[:], ceq[:], ultv[:]))
        E.op("vector", lambda e: e.tensor_add(cgt[:], cgt[:], ceq[:]))
        v1 = E.op("vector", lambda e: e.reduce_sum(rankw[:, 0:1], cgt[:], axis=AXX), "ve")
        E.wait("vector", "ve", v1)
        v2 = E.op("vector", lambda e: e.tensor_scalar(
            rankw[:, 1:2], rankw[:, 0:1], 255.5, None, ALU.is_lt), "ve")
        E.wait("vector", "ve", v2)
        v3 = E.op("vector", lambda e: e.tensor_scalar(
            rankw[:, 2:3], rankw[:, 1:2], 1e30, -1e30, ALU.mult, ALU.add), "ve")
        E.wait("vector", "ve", v3)
        kb_v = E.op("vector", lambda e, vt=vt: e.tensor_scalar(
            kbias[:, vt:vt + 1], rankw[:, 2:3], SHIFT, None, ALU.add), "ve")
        E.wait("vector", "ve", kb_v)
        E.wait("gpsimd", "ve", kb_v)
    E.dma("gpsimd", kbias_s[0:1, 0:1], m30[0:1, 0:1], "gp")
    E.dma("gpsimd",
          kbias_s[1:513, 0:1].rearrange("(v p) one -> p (v one)", p=128),
          kbias[:, 0:4], "gp", slow=True)
    E.dma("gpsimd", kbias_s[513:640, 0:1], m30[0:127, 0:1], "gp")
    kb_ld = None
    for kt in range(5):
        kb_ld = E.dma("gpsimd", kbias[:, kt:kt + 1],
                      kbias_s[128 * kt:128 * kt + 128, 0:1], "gp")

    def hb(p, i):
        return AB[:, 12288 * p + 2048 * i:12288 * p + 2048 * i + 2048]
    kqv = {nm: [hb(0, i), hb(1, i)] for i, nm in
           enumerate(("kh", "kl", "qh", "ql", "vth", "vtl"))}
    vfT = [FA[:, 9728:11776], FA[:, 11776:13824]]
    phb = AB[:, 32768:40960]
    plb = AB[:, 40960:49152]
    pbuf = FA[:, 0:8192]
    outsb = [FA[:, 8192:8704], FA[:, 8704:9216]]
    rbc = FA[:, 9216:9728]
    rrow = R["b"]

    head_pe_done = {}
    out_store = {}
    prev_onorm = 0
    prev_rbc_cp = 0
    E.wait("scalar", "gp", kb_ld)
    E.wait("sync", "pe", E.val("pe"))
    for h in range(NFT):
        par = h % 2
        hkey = f"h{par}"
        if h >= 2:
            E.wait("sync", "pe", head_pe_done[h - 2])
        for nm, src in (("kh", khT_s), ("kl", klT_s), ("qh", qhT_s), ("ql", qlT_s)):
            E.dma("sync", kqv[nm][par][:], src[128 * h:128 * h + 128, :], hkey)
        ldall = E.dma("sync", vfT[par][:], vT_s[128 * h:128 * h + 128, :], hkey)
        E.wait("tensor", hkey, ldall)
        cp_vals = []
        sub_vals = []
        for kt in range(16):
            pb = P[4 + (kt % 2)]
            if kt >= 2:
                E.wait("tensor", "ac", cp_vals[kt - 2])
                E.wait("tensor", "ve", sub_vals[kt - 2])
            tr = E.op("tensor", lambda e, kt=kt, par=par, pb=pb: e.transpose(
                pb[:, 0:128], vfT[par][:, 128 * kt:128 * kt + 128], ident[:]), "pe")
            E.wait("scalar", "pe", tr)
            cp_vals.append(E.op("scalar", lambda e, kt=kt, par=par, pb=pb: e.copy(
                kqv["vth"][par][:, 128 * kt:128 * kt + 128], pb[:, 0:128]), "ac"))
            E.wait("vector", "ac", cp_vals[-1])
            sub_vals.append(E.op("vector", lambda e, kt=kt, par=par, pb=pb: e.tensor_sub(
                kqv["vtl"][par][:, 128 * kt:128 * kt + 128], pb[:, 0:128],
                kqv["vth"][par][:, 128 * kt:128 * kt + 128]), "ve"))
        vt_ready = cp_vals[-1]
        vt_readyv = sub_vals[-1]
        for j in range(NCH):
            nkt = 4 * (j + 1)
            E.wait("scalar", "gc", E.val("gc"))
            exp_done = {}
            psplit = {}
            for kt in range(nkt):
                sb = [P[0], P[1], P[7]][kt % 3]
                if kt >= 3:
                    E.wait("tensor", "ac", exp_done[kt - 3])
                smm = E.op("tensor", lambda e, kt=kt, j=j, par=par, sb=sb: (
                    e.matmul(sb[:], kqv["kh"][par][:, 128 * kt:128 * kt + 128],
                             kqv["qh"][par][:, CH * j:CH * j + CH],
                             start=True, stop=False),
                    e.matmul(sb[:], kqv["kh"][par][:, 128 * kt:128 * kt + 128],
                             kqv["ql"][par][:, CH * j:CH * j + CH],
                             start=False, stop=False),
                    e.matmul(sb[:], kqv["kl"][par][:, 128 * kt:128 * kt + 128],
                             kqv["qh"][par][:, CH * j:CH * j + CH],
                             start=False, stop=True))[-1], "pe")
                E.wait("scalar", "pe", smm)
                bias = kbias[:, kt:kt + 1] if kt <= 4 else m30[:, 0:1]
                exp_done[kt] = E.op("scalar", lambda e, kt=kt, sb=sb, bias=bias:
                                    e.activation(pbuf[:, CH * kt:CH * kt + CH], sb[:],
                                                 AF.Exp, bias=bias, scale=SCALE), "ac")
            asel = {}
            for kt in range(4 * j, nkt):
                E.wait("gpsimd", "ac", exp_done[kt])
                asel[kt] = E.op("gpsimd", lambda e, kt=kt, j=j: e.affine_select(
                    pbuf[:, CH * kt:CH * kt + CH], pbuf[:, CH * kt:CH * kt + CH],
                    pattern=[[1, 512]], compare_op=ALU.is_ge, fill=0.0,
                    base=CH * j - 128 * kt, channel_multiplier=-1), "gc")
            for kt in range(nkt):
                E.wait("scalar", "ac", exp_done[kt])
                if kt in asel:
                    E.wait("scalar", "gc", asel[kt])
                hv = E.op("scalar", lambda e, kt=kt: e.copy(
                    phb[:, CH * kt:CH * kt + CH], pbuf[:, CH * kt:CH * kt + CH]), "ac")
                E.wait("vector", "ac", hv)
                psplit[kt] = E.op("vector", lambda e, kt=kt: e.tensor_sub(
                    plb[:, CH * kt:CH * kt + CH], pbuf[:, CH * kt:CH * kt + CH],
                    phb[:, CH * kt:CH * kt + CH]), "ve")
            E.wait("tensor", "ac", vt_ready)
            E.wait("tensor", "ve", vt_readyv)
            E.wait("tensor", "ve", prev_onorm)
            pvmm = None
            for kt in range(nkt):
                E.wait("tensor", "ve", psplit[kt])
                E.op("tensor", lambda e, kt=kt, par=par, nkt=nkt: (
                    e.matmul(P[2][:], kqv["vth"][par][:, 128 * kt:128 * kt + 128],
                             phb[:, CH * kt:CH * kt + CH],
                             start=(kt == 0), stop=False),
                    e.matmul(P[2][:], kqv["vth"][par][:, 128 * kt:128 * kt + 128],
                             plb[:, CH * kt:CH * kt + CH],
                             start=False, stop=False),
                    e.matmul(P[2][:], kqv["vtl"][par][:, 128 * kt:128 * kt + 128],
                             phb[:, CH * kt:CH * kt + CH],
                             start=False, stop=(kt == nkt - 1)))[-1])
                pvmm = E.op("tensor", lambda e, kt=kt, nkt=nkt: (
                    e.matmul(P6[:], onesb_col, phb[:, CH * kt:CH * kt + CH],
                             start=(kt == 0), stop=False),
                    e.matmul(P6[:], onesb_col, plb[:, CH * kt:CH * kt + CH],
                             start=False, stop=(kt == nkt - 1)))[-1], "pe")
            E.wait("vector", "pe", pvmm)
            rec_v = E.op("vector", lambda e: e.reciprocal(rrow, P6[:]), "ve")
            E.wait("tensor", "ve", rec_v)
            E.wait("tensor", "ac", prev_rbc_cp)
            bc2 = E.op("tensor", lambda e: e.matmul(
                P[3][:], ones_row, rrow, start=True, stop=True), "pe")
            E.wait("scalar", "pe", bc2)
            rbc_cp = E.op("scalar", lambda e: e.copy(rbc[:], P[3][:]), "ac")
            prev_rbc_cp = rbc_cp
            opar = (h * NCH + j) % 2
            obkey = f"ob{opar}"
            E.wait("vector", "ac", rbc_cp)
            if opar in out_store:
                E.wait("vector", obkey, out_store[opar])
            onorm = E.op("vector", lambda e, opar=opar: e.tensor_mul(
                outsb[opar][:], P[2][:], rbc[:]), "ve")
            prev_onorm = onorm
            E.wait("scalar", "ve", onorm)
            ohdst = oh0_s if h < 8 else oh1_s
            out_store[opar] = E.dma(
                "scalar", ohdst.ap()[128 * (h % 8):128 * (h % 8) + 128,
                                     CH * j:CH * j + CH],
                outsb[opar][:], obkey)
        head_pe_done[h] = E.val("pe")
        if h == 7:
            E.wait("gpsimd", "ob0", E.val("ob0"))
            E.wait("gpsimd", "ob1", E.val("ob1"))
            ag0_done = E.op("gpsimd", lambda e: e.collective_compute(
                "AllGather", ALU.bypass, replica_groups=[list(range(8))],
                ins=[oh0_s.ap().opt()], outs=[ag0_out.ap().opt()]), "cc")

    b_pe = E.val("pe")

    xnhh = AB[:, 0:16384]
    xnlh = AB[:, 16384:32768]
    wdh = [AB[:, 32768:34816], AB[:, 34816:36864]]
    wdl = [AB[:, 36864:38912], AB[:, 38912:40960]]
    fw = [AI[:, 0:2048], AI[:, 2048:4096]]
    sxp = [FA[:, 0:2048], FA[:, 2048:4096]]
    wdf = [FA[:, 4096:6144], FA[:, 6144:8192]]
    fout = [FA[:, 8192:8704], FA[:, 8704:9216]]
    fpart = [FA[:, 9216:9728], FA[:, 9728:10240]]

    E.wait("vector", "ob0", E.val("ob0"))
    E.wait("vector", "ob1", E.val("ob1"))
    E.wait("gpsimd", "ob0", E.val("ob0"))
    E.wait("gpsimd", "ob1", E.val("ob1"))
    E.wait("gpsimd", "pe", b_pe)
    E.wait("sync", "ac", E.val("ac"))
    E.wait("sync", "ve", E.val("ve"))
    E.wait("sync", "gc", E.val("gc"))
    xn_ld0a = E.dma(
        "gpsimd", xnhh.rearrange("p (i t) -> p i t", t=1024),
        lambda: xnhT_s.ap().rearrange("(i p) t -> p i t", p=128)[
            :, 0:16, DS(pidbox["goff"], 1024)], "gp")
    xn_ld0 = E.dma(
        "gpsimd", xnlh.rearrange("p (i t) -> p i t", t=1024),
        lambda: xnlT_s.ap().rearrange("(i p) t -> p i t", p=128)[
            :, 0:16, DS(pidbox["goff"], 1024)], "gp")
    ag_done = E.op("gpsimd", lambda e: e.collective_compute(
        "AllGather", ALU.bypass, replica_groups=[list(range(8))],
        ins=[oh1_s.ap().opt()], outs=[ag1_out.ap().opt()]), "cc")

    wd_mm = {}
    dq_done = {}
    fout_store = {}
    fp_ev = {}
    fpart_used = {}
    hC_prev = {}

    def prepD(d):
        par = d % 2
        fkey = f"f{par}"
        dh, dot = d // 32, d % 32
        if d >= 2:
            E.wait("sync", "ve", dq_done[d - 2])
        E.dma("sync", fw[par].rearrange("p (i o) -> p i o", o=128),
              fwT.rearrange("(i p) o -> p i o", p=128)[
                  :, 16 * dh:16 * dh + 16, 128 * dot:128 * dot + 128], fkey)
        sxld = E.dma("sync", sxp[par].rearrange("p (i o) -> p i o", o=128),
                     sexpT.rearrange("(i p) o -> p i o", p=128)[
                         :, 16 * dh:16 * dh + 16, 128 * dot:128 * dot + 128], fkey)
        E.wait("vector", fkey, sxld)
        if d >= 2:
            E.wait("vector", "pe", wd_mm[d - 2])
            E.wait("vector", "ac", hC_prev.get(par, 0))
        E.op("vector", lambda e, par=par: e.tensor_copy(wdf[par][:], fw[par][:]))
        mul_v = E.op("vector", lambda e, par=par: e.tensor_mul(
            wdf[par][:], wdf[par][:], sxp[par][:]), "ve")
        E.wait("scalar", "ve", mul_v)
        E.wait("scalar", "pe", wd_mm[d - 2] if d >= 2 else b_pe)
        hC = E.op("scalar", lambda e, par=par: e.copy(wdh[par][:], wdf[par][:]), "ac")
        hC_prev[par] = hC
        E.wait("vector", "ac", hC)
        dq_done[d] = E.op("vector", lambda e, par=par: e.tensor_sub(
            wdl[par][:], wdf[par][:], wdh[par][:]), "ve")

    di = 0
    prepD(0)
    for half in range(2):
        if half == 1:
            E.wait("gpsimd", "pe", wd_mm[di - 1])
            E.dma("gpsimd", xnhh.rearrange("p (i t) -> p i t", t=1024),
                  lambda: xnhT_s.ap().rearrange("(i p) t -> p i t", p=128)[
                      :, 16:32, DS(pidbox["goff"], 1024)], "gp")
            xn_ld1 = E.dma(
                "gpsimd", xnlh.rearrange("p (i t) -> p i t", t=1024),
                lambda: xnlT_s.ap().rearrange("(i p) t -> p i t", p=128)[
                    :, 16:32, DS(pidbox["goff"], 1024)], "gp")
            E.wait("sync", "fs0", E.val("fs0"))
            E.wait("sync", "fs1", E.val("fs1"))
        for ot in range(32):
            par = di % 2
            if di + 1 < 64:
                prepD(di + 1)
            E.wait("tensor", "ve", dq_done[di])
            if di == 0:
                E.wait("tensor", "gp", xn_ld0)
            if half == 1 and ot == 0:
                E.wait("tensor", "gp", xn_ld1)
            bb = 2 * (di % 2)
            for tc in range(2):
                E.wait("tensor", "ve", fp_ev.get(bb + tc, 0))
            mmv = None
            for i in range(16):
                for tc in range(2):
                    ik = "pe" if (i == 15 and tc == 1) else None
                    mmv = E.op("tensor", lambda e, i=i, tc=tc, par=par, bb=bb: (
                        e.matmul(P[bb + tc][:], wdh[par][:, 128 * i:128 * i + 128],
                                 xnhh[:, 1024 * i + CH * tc:1024 * i + CH * tc + CH],
                                 start=(i == 0), stop=False),
                        e.matmul(P[bb + tc][:], wdh[par][:, 128 * i:128 * i + 128],
                                 xnlh[:, 1024 * i + CH * tc:1024 * i + CH * tc + CH],
                                 start=False, stop=False),
                        e.matmul(P[bb + tc][:], wdl[par][:, 128 * i:128 * i + 128],
                                 xnhh[:, 1024 * i + CH * tc:1024 * i + CH * tc + CH],
                                 start=False, stop=(i == 15)))[-1], ik)
            wd_mm[di] = mmv
            for tc in range(2):
                skey = f"fs{tc}" if half == 0 else f"fs{2 + tc}"
                E.wait("vector", "pe", mmv)
                if half == 0:
                    prev = fout_store.get(tc)
                    if prev is not None:
                        E.wait("vector", prev[0], prev[1])
                    ev = E.op("vector", lambda e, tc=tc, bb=bb: e.tensor_copy(
                        fout[tc][:], P[bb + tc][:]), "ve")
                else:
                    E.wait("sync", "ve", fpart_used.get(tc, 0))
                    pld = E.dma("sync", fpart[tc][:],
                                ffnT_s[128 * ot:128 * ot + 128, CH * tc:CH * tc + CH],
                                f"p{tc}")
                    E.wait("vector", f"p{tc}", pld)
                    prev = fout_store.get(tc)
                    if prev is not None:
                        E.wait("vector", prev[0], prev[1])
                    E.op("vector", lambda e, tc=tc, ot=ot, bb=bb: e.tensor_scalar(
                        fout[tc][:], P[bb + tc][:], fbias_c[:, ot:ot + 1], None, ALU.add))
                    ev = E.op("vector", lambda e, tc=tc: e.tensor_add(
                        fout[tc][:], fout[tc][:], fpart[tc][:]), "ve")
                    fpart_used[tc] = ev
                fp_ev[bb + tc] = ev
                E.wait("scalar", "ve", ev)
                fout_store[tc] = (skey, E.dma(
                    "scalar", ffnT_s[128 * ot:128 * ot + 128, CH * tc:CH * tc + CH],
                    fout[tc][:], skey))
            di += 1
    d1_pe = E.val("pe")

    ohh = AB[:, 0:16384]
    ohl = AB[:, 16384:32768]
    owh = [AB[:, 32768:34816], AB[:, 34816:36864]]
    owl = [AB[:, 36864:38912], AB[:, 38912:40960]]
    ohstrip = FA[:, 0:4096]
    owf = [FA[:, 4096:6144], FA[:, 6144:8192]]
    extr = FA[:, 8192:11264]
    oout = [FA[:, 11264:11776], FA[:, 11776:12288]]

    def extile(kind, par):
        base = 8192 + {"x": 0, "ffn": 1024, "ap": 2048}[kind] + 512 * par
        return FA[:, base:base + 512]

    oslab_done = {}
    osub_done = {}
    oo_store = {}
    oev = {}
    ext_used = {}

    def prepO(o):
        par = o % 2
        okey, gok = f"o{par}", f"go{par}"
        oh_, oot = o // 32, o % 32
        owT_ap = owT.rearrange("(f p) o -> p f o", p=128)[
            :, 16 * oh_:16 * oh_ + 16, 128 * oot:128 * oot + 128]
        if o >= 2:
            E.wait("sync", "ve", osub_done[o - 2])
            E.wait("gpsimd", "pe", oslab_done[o - 2])
        owld = E.dma("sync", owf[par].rearrange("p (f o) -> p f o", o=128),
                     owT_ap, okey)
        gld = E.dma("gpsimd", owh[par].rearrange("p (f o) -> p f o", o=128),
                    owT_ap, gok)
        E.wait("vector", okey, owld)
        E.wait("vector", gok, gld)
        if o >= 2:
            E.wait("vector", "pe", oslab_done[o - 2])
        osub_done[o] = E.op("vector", lambda e, par=par: e.tensor_sub(
            owl[par][:], owf[par][:], owh[par][:]), "ve")

    oi = 0
    E.wait("gpsimd", "cc", ag_done)
    E.wait("sync", "ve", E.val("ve"))
    E.wait("sync", "ac", E.val("ac"))
    E.wait("gpsimd", "ve", E.val("ve"))
    E.wait("gpsimd", "ac", E.val("ac"))
    E.wait("gpsimd", "pe", d1_pe)
    prepO(0)
    for half in range(2):
        E.wait("gpsimd", "pe", d1_pe if half == 0 else oslab_done[oi - 1])
        strip_done = None
        for srci in range(2):
            agsrc = ag0_out if srci == 0 else ag1_out
            for kq in range(4):
                sld = E.dma("gpsimd",
                            ohstrip.rearrange("p (f t) -> p f t", t=256)[:, 0:8, :],
                            lambda half=half, kq=kq, agsrc=agsrc: agsrc.ap()[
                                DS(pidbox[f"rbA{half}"], 1024),
                                DS(pidbox[f"go{half}_{kq}"], 256)].rearrange(
                                "(f p) t -> p f t", p=128), "gp")
                E.wait("scalar", "gp", sld)
                if strip_done is not None:
                    E.wait("scalar", "ve", strip_done)
                shv = E.op("scalar", lambda e, kq=kq, srci=srci: e.copy(
                    ohh.rearrange("p (f t) -> p f t", t=1024)[
                        :, 8 * srci:8 * srci + 8, 256 * kq:256 * kq + 256],
                    ohstrip.rearrange("p (f t) -> p f t", t=256)[:, 0:8, :]), "ac")
                E.wait("vector", "ac", shv)
                strip_done = E.op("vector", lambda e, kq=kq, srci=srci: e.tensor_sub(
                    ohl.rearrange("p (f t) -> p f t", t=1024)[
                        :, 8 * srci:8 * srci + 8, 256 * kq:256 * kq + 256],
                    ohstrip.rearrange("p (f t) -> p f t", t=256)[:, 0:8, :],
                    ohh.rearrange("p (f t) -> p f t", t=1024)[
                        :, 8 * srci:8 * srci + 8, 256 * kq:256 * kq + 256]), "ve")
                E.wait("gpsimd", "ve", strip_done)
        oh_ready = strip_done
        if half == 1:
            E.wait("sync", "ap0", E.val("ap0"))
            E.wait("sync", "ap1", E.val("ap1"))
            E.wait("sync", "fs2", E.val("fs2"))
            E.wait("sync", "fs3", E.val("fs3"))
        for ot in range(32):
            par = oi % 2
            if oi + 1 < 64:
                prepO(oi + 1)
            E.wait("tensor", "ve", osub_done[oi])
            E.wait("tensor", "ve", oh_ready)
            ob_ = 2 * (oi % 2)
            for tc in range(2):
                E.wait("tensor", "ve", oev.get(ob_ + tc, 0))
            ommv = None
            for f in range(16):
                for tc in range(2):
                    ik = "pe" if (f == 15 and tc == 1) else None
                    ommv = E.op("tensor", lambda e, f=f, tc=tc, par=par, ob_=ob_: (
                        e.matmul(P[ob_ + tc][:], owh[par][:, 128 * f:128 * f + 128],
                                 ohh[:, 1024 * f + CH * tc:1024 * f + CH * tc + CH],
                                 start=(f == 0), stop=False),
                        e.matmul(P[ob_ + tc][:], owh[par][:, 128 * f:128 * f + 128],
                                 ohl[:, 1024 * f + CH * tc:1024 * f + CH * tc + CH],
                                 start=False, stop=False),
                        e.matmul(P[ob_ + tc][:], owl[par][:, 128 * f:128 * f + 128],
                                 ohh[:, 1024 * f + CH * tc:1024 * f + CH * tc + CH],
                                 start=False, stop=(f == 15)))[-1], ik)
            oslab_done[oi] = ommv
            for tc in range(2):
                E.wait("vector", "pe", ommv)
                prev = oo_store.get(tc)
                if half == 0:
                    if prev is not None:
                        E.wait("vector", prev[0], prev[1])
                    ev = E.op("vector", lambda e, tc=tc, ob_=ob_: e.tensor_copy(
                        oout[tc][:], P[ob_ + tc][:]), "ve")
                    oev[ob_ + tc] = ev
                    E.wait("scalar", "ve", ev)
                    oo_store[tc] = (f"ap{tc}", E.dma(
                        "scalar", apT_s[128 * ot:128 * ot + 128, CH * tc:CH * tc + CH],
                        oout[tc][:], f"ap{tc}"))
                else:
                    ekey = f"e{tc}"
                    E.wait("sync", "ve", ext_used.get(tc, 0))
                    E.dma("sync", extile("ap", tc)[:],
                          apT_s[128 * ot:128 * ot + 128, CH * tc:CH * tc + CH], ekey)
                    E.dma("sync", extile("ffn", tc)[:],
                          ffnT_s[128 * ot:128 * ot + 128, CH * tc:CH * tc + CH], ekey)
                    xld2 = E.dma("sync", extile("x", tc)[:],
                                 xTo[128 * ot:128 * ot + 128, CH * tc:CH * tc + CH],
                                 ekey)
                    E.wait("vector", ekey, xld2)
                    if prev is not None:
                        E.wait("vector", prev[0], prev[1])
                    E.op("vector", lambda e, tc=tc, ob_=ob_: e.tensor_add(
                        oout[tc][:], P[ob_ + tc][:], extile("ap", tc)[:]))
                    E.op("vector", lambda e, tc=tc: e.tensor_add(
                        oout[tc][:], oout[tc][:], extile("x", tc)[:]))
                    ev = E.op("vector", lambda e, tc=tc: e.tensor_add(
                        oout[tc][:], oout[tc][:], extile("ffn", tc)[:]), "ve")
                    ext_used[tc] = ev
                    oev[ob_ + tc] = ev
                    E.wait("scalar", "ve", ev)
                    oo_store[tc] = (f"ou{tc}", E.dma(
                        "scalar", outT[128 * ot:128 * ot + 128, CH * tc:CH * tc + CH],
                        oout[tc][:], f"ou{tc}"))
            oi += 1

    E.wait("scalar", "ou0", E.val("ou0"))
    E.wait("scalar", "ou1", E.val("ou1"))
    E.wait("gpsimd", "gp")

    with nc.Block() as block:
        block.sync(lambda e: [f(e) for f in E.ops["sync"]])
        block.scalar(lambda e: [f(e) for f in E.ops["scalar"]])
        block.vector(lambda e: [f(e) for f in E.ops["vector"]])
        block.gpsimd(lambda e: [f(e) for f in E.ops["gpsimd"]])
        block.tensor(lambda e: [f(e) for f in E.ops["tensor"]])

    for k in sem_ctx:
        sem_ctx[k].__exit__(None, None, None)
    return nc


_NC_CACHE = {}


def kernel(L0_attn_hidden_states, L0_attn_q_w, L0_attn_k_w, L0_attn_v_w,
           L0_attn_o_w, L0_ffn_qweight, L0_ffn_scales, L0_ffn_bias):
    import ml_dtypes
    x = np.asarray(L0_attn_hidden_states, dtype=np.float32)
    q_w = np.asarray(L0_attn_q_w, dtype=np.float32)
    k_w = np.asarray(L0_attn_k_w, dtype=np.float32)
    v_w = np.asarray(L0_attn_v_w, dtype=np.float32)
    o_w = np.asarray(L0_attn_o_w, dtype=np.float32)
    qw_i = np.asarray(L0_ffn_qweight, dtype=np.int32)
    scales = np.asarray(L0_ffn_scales, dtype=np.float32)
    fb = np.asarray(L0_ffn_bias, dtype=np.float32)

    owT = np.ascontiguousarray(o_w.T)
    fwT = np.ascontiguousarray(qw_i.T)
    s2 = scales.reshape(HID, 64)
    sexpT = np.ascontiguousarray(np.repeat(s2.T, 64, axis=0))
    fbias = np.ascontiguousarray(fb.reshape(HID, 1))

    inv = 1.0 / (10000.0 ** (np.arange(0, DH, 2, dtype=np.float64) / DH))
    freqs = np.arange(S, dtype=np.float64)[:, None] * inv[None, :]
    emb = np.concatenate([freqs, freqs], axis=-1)
    cosT = np.ascontiguousarray(np.cos(emb).T.astype(np.float32))
    sinT = np.ascontiguousarray(np.sin(emb).T.astype(np.float32))

    ptT = np.zeros((128, 128), dtype=np.float32)
    ptT[np.arange(64), np.arange(64) + 64] = 1.0
    ptT[np.arange(64) + 64, np.arange(64)] = -1.0

    ident = np.eye(128, dtype=np.float32)
    onesm = np.ones((128, 128), dtype=np.float32)
    onesb = np.ones((128, 128), dtype=ml_dtypes.bfloat16)
    identb = np.eye(128, dtype=ml_dtypes.bfloat16)

    in_maps = []
    for c in range(NCORES):
        b, g = c // 2, c % 2
        in_maps.append({
            "xT": np.ascontiguousarray(x[b].T),
            "xTo": np.ascontiguousarray(x[b].T[:, 1024 * g:1024 * g + 1024]),
            "qwT": np.ascontiguousarray(q_w.T[:, 2048 * g:2048 * g + 2048]),
            "kwT": np.ascontiguousarray(k_w.T[:, 2048 * g:2048 * g + 2048]),
            "vwT": np.ascontiguousarray(v_w.T[:, 2048 * g:2048 * g + 2048]),
            "owT": owT, "fwT": fwT, "sexpT": sexpT, "fbias": fbias,
            "cosT": cosT, "sinT": sinT, "ptT": ptT, "ident": ident,
            "onesm": onesm, "onesb": onesb, "identb": identb,
            "impsc": np.full((1, 1), 1.0 if b == 0 else 0.0, dtype=np.float32),
        })

    if "prog" not in _NC_CACHE:
        _NC_CACHE["prog"] = build_program()
    nc = _NC_CACHE["prog"]
    res = run_bass_kernel_spmd(nc, in_maps, core_ids=list(range(NCORES)))

    out = np.empty((B, S, HID), dtype=np.float32)
    for c in range(NCORES):
        b, g = c // 2, c % 2
        out[b, 1024 * g:1024 * g + 1024, :] = res.results[c]["outT"].T
    return out
